# version 03
# baseline (speedup 1.0000x reference)
import math
import numpy as np
from contextlib import ExitStack

import concourse.bass as bass
import concourse.mybir as mybir
from concourse.bass_utils import run_bass_kernel_spmd

F32 = mybir.dt.float32
BF16 = mybir.dt.bfloat16
I8 = mybir.dt.int8
ALU = mybir.AluOpType
AF = mybir.ActivationFunctionType
AX = mybir.AxisListType

B, T, D = 4, 2048, 2048
NCORES = 8
TC = T // 2
NCH = D // 128
NEG = -1.0e30
LN_EPS = 1e-5

ENGS = ("sync", "tensor", "scalar", "vector", "gpsimd")


class Sem:
    def __init__(self, nc, name):
        self.h = nc.alloc_semaphore(name=name)
        self.n = 0

    def inc(self, bi, k=1):
        bi.then_inc(self.h, k)
        self.n += k
        return self.n

    def wait(self, eng, val):
        eng.wait_ge(self.h, val)

    def wait_all(self, eng):
        eng.wait_ge(self.h, self.n)


class Ctx:
    def __init__(self):
        self.nc = bass.Bass("TRN2", target_bir_lowering=False, debug=False)
        self.block = None
        self.sems = {}
        self.sbufs = {}
        self._barsem = None
        self._barcnt = 0
        self._uid = 0
        self._consts = {}

    def uid(self):
        self._uid += 1
        return self._uid

    def sem(self, name):
        if name not in self.sems:
            self.sems[name] = Sem(self.nc, name)
        return self.sems[name]

    def inp(self, name, shape, dt=F32):
        return self.nc.dram_tensor(name, list(shape), dt, kind="ExternalInput").ap()

    def outp(self, name, shape, dt=F32):
        return self.nc.dram_tensor(name, list(shape), dt, kind="ExternalOutput").ap()

    def dram(self, name, shape, dt=F32):
        return self.nc.dram_tensor(name, list(shape), dt, kind="Internal")

    def sb(self, name, shape, dt=F32):
        t = self.nc.alloc_sbuf_tensor(name, list(shape), dt)
        self.sbufs[name] = t
        return t

    def on(self, engname, fn):
        getattr(self.block, engname)(fn)

    def barrier(self, engines=ENGS):
        if self._barsem is None:
            self._barsem = Sem(self.nc, "barrier")
        self._barcnt += len(engines)
        target = self._barcnt
        for en in engines:
            def f(eng, target=target):
                eng.drain()
                eng.sem_inc(self._barsem.h, 1)
                eng.wait_ge(self._barsem.h, target)
            self.on(en, f)

    def edge(self, engname, eng, ins):
        s = self.sem(f"edge_{engname}")
        s.inc(ins)
        s.wait(eng, s.n)

    def cconst(self, value):
        key = float(value)
        if key not in self._consts:
            t = self.sb(f"const{len(self._consts)}", [128, 1], F32)
            self._consts[key] = t
        return self._consts[key]

    def setup_common(self):
        for v in (LN_EPS, 1.0, 0.5):
            self.cconst(v)
        if not hasattr(self, "psA"):
            self.psA = self.nc.alloc_psum_tensor("psA", [128, 2048], F32)
        self.pb = [self.psA[:, i * 512:(i + 1) * 512] for i in range(4)] + \
                  [self.nc.alloc_psum_tensor(f"pb{i}", [128, 512], F32)[:, :] for i in range(4, 8)]
        self.ones_col_b = self.sb("ones_col_b", [128, 1], BF16)
        self.ones_row = self.sb("ones_row", [1, 128], F32)
        self.ones_row_b = self.sb("ones_row_b", [1, 128], BF16)
        def f(v):
            v.memset(self.ones_col_b[:], 1.0)
            v.memset(self.ones_row[:], 1.0)
            v.memset(self.ones_row_b[:], 1.0)
            for val, t in self._consts.items():
                v.memset(t[:], val)
        self.on("vector", f)
        self.barrier()


def cdiv(a, b):
    return (a + b - 1) // b


def emit_ln(ctx, x_sb, xn_sb, ntok, p0=0):
    nc = ctx.nc
    NT = ntok // 512
    u = ctx.uid()
    s_cast = ctx.sem(f"lncast{u}")
    s_stat = ctx.sem(f"lnstat{u}")
    s_rowa = ctx.sem(f"lnrowa{u}")
    s_rowb = ctx.sem(f"lnrowb{u}")
    s_row = ctx.sem(f"lnrow{u}")
    s_bc = ctx.sem(f"lnbc{u}")
    s_app = ctx.sem(f"lnapp{u}")
    if "sh_scr1" in ctx.sbufs:
        sq = ctx.sbufs["sh_scr1"]
    elif "lnsq_sh" in ctx.sbufs:
        sq = ctx.sbufs["lnsq_sh"]
    else:
        sq = ctx.sb("lnsq_sh", [128, 512], BF16)
    if "lnrows_sh" in ctx.sbufs:
        rows = ctx.sbufs["lnrows_sh"]
    else:
        rows = ctx.sb("lnrows_sh", [1, 3 * 512], F32)
    pb = ctx.pb

    def row_sl(i, t):
        return rows[0:1, i * 512:(i + 1) * 512]

    def sl(sbuf, c, t):
        return sbuf[:, c * ntok + t * 512: c * ntok + t * 512 + 512]

    def fv(v):
        ins = v.tensor_copy(out=xn_sb[:, 0:NCH * ntok], in_=x_sb[:, 0:NCH * ntok])
        ctx.edge("vector", v, ins)
        for t in range(NT):
            for c in range(NCH):
                k = t * NCH + c
                if k >= 1:
                    s_stat.wait(v, k)
                ins = v.tensor_mul(out=sq[:, 0:512],
                                   in0=sl(xn_sb, c, t), in1=sl(xn_sb, c, t))
                s_cast.inc(ins)
            s_rowa.wait(v, 2 * (t + 1))
            tmp = row_sl(2, t)
            ins = v.tensor_mul(out=tmp, in0=row_sl(0, t), in1=row_sl(0, t))
            ctx.edge("vector", v, ins)
            ins = v.tensor_sub(out=row_sl(1, t), in0=row_sl(1, t), in1=tmp)
            s_rowb.inc(ins)
            s_bc.wait(v, t + 1)
            for c in range(NCH):
                ins = v.tensor_sub(out=sl(xn_sb, c, t), in0=sl(x_sb, c, t), in1=pb[p0 + 2][:, :])
            ctx.edge("vector", v, ins)
            for c in range(NCH):
                ins = v.tensor_mul(out=sl(xn_sb, c, t), in0=sl(xn_sb, c, t), in1=pb[p0 + 3][:, :])
                s_app.inc(ins)
    ctx.on("vector", fv)

    def ft(te):
        for t in range(NT):
            if t > 0:
                s_rowa.wait(te, 2 * t)
            for c in range(NCH):
                k = t * NCH + c
                s_cast.wait(te, k + 1)
                te.matmul(pb[p0][0:1, :], ctx.ones_col_b[:], sl(xn_sb, c, t),
                          start=(c == 0), stop=(c == NCH - 1))
                ins = te.matmul(pb[p0 + 1][0:1, :], ctx.ones_col_b[:],
                                sq[:, 0:512],
                                start=(c == 0), stop=(c == NCH - 1))
                s_stat.inc(ins)
            s_row.wait(te, t + 1)
            if t > 0:
                s_app.wait(te, t * NCH)
            te.matmul(pb[p0 + 2][:, :], ctx.ones_row[:], row_sl(0, t), start=True, stop=True)
            ins = te.matmul(pb[p0 + 3][:, :], ctx.ones_row[:], row_sl(2, t), start=True, stop=True)
            s_bc.inc(ins)
    ctx.on("tensor", ft)

    def fs(se):
        for t in range(NT):
            s_stat.wait(se, (t + 1) * NCH)
            if t > 0:
                s_bc.wait(se, t)
            se.activation(out=row_sl(0, t), in_=pb[p0][0:1, :], func=AF.Copy, scale=1.0 / D)
            ins = se.activation(out=row_sl(1, t), in_=pb[p0 + 1][0:1, :], func=AF.Copy, scale=1.0 / D)
            s_rowa.inc(ins, 2)
            s_rowb.wait(se, t + 1)
            ins = se.activation(out=row_sl(2, t), in_=row_sl(1, t), func=AF.Ln, bias=ctx.cconst(LN_EPS)[0:1, :])
            ctx.edge("scalar", se, ins)
            ins = se.activation(out=row_sl(2, t), in_=row_sl(2, t), func=AF.Exp, scale=-0.5)
            s_row.inc(ins)
    ctx.on("scalar", fs)

    return s_app



def emit_attention(ctx, cfg):
    nc = ctx.nc
    u = ctx.uid()
    T_ = cfg["T"]
    NHc, NKVc, HD = cfg["NHc"], cfg["NKVc"], cfg["HD"]
    group = cfg["group"]
    pad = cfg.get("pad", 0)
    Tp = T_ + pad
    DVC = NKVc * HD
    NQT = T_ // 512
    NTC = T_ // 128
    window = cfg.get("window")
    xn = cfg["xn_sb"]
    scale = 1.0 / math.sqrt(HD)
    convv = cfg.get("convv")
    rope = cfg.get("rope")
    convk = cfg.get("convk_cols")
    woT = cfg.get("woT")

    def xsl(c, lo, n):
        return xn[:, c * Tp + lo: c * Tp + lo + n]

    _pool = ctx.sb(f"at{u}_pool", [128, max(NHc * T_, NCH * DVC * (2 if convv is not None else 1))], BF16)
    wv_sb = _pool[:, 0:NCH * DVC]
    o_sb = _pool[:, 0:NHc * T_]
    v_sb = ctx.sb(f"at{u}_v", [128, NTC * DVC], BF16)
    wk_sb = ctx.sb(f"at{u}_wk", [128, 2 * NCH * HD], BF16)
    wq_sb = ctx.sb(f"at{u}_wq", [128, 2 * NCH * HD], BF16)
    k_sb = ctx.sb(f"at{u}_k", [128, T_], BF16)
    q_sb = ctx.sb(f"at{u}_q", [128, 2 * T_], BF16)
    scr_sb = ctx.sb(f"at{u}_scr", [128, T_ if (cfg.get("rope") or cfg.get("convk_cols")) else 4], BF16)
    probs = ctx.sb(f"at{u}_pr", [128, 2 * 1024], BF16)
    rz_sb = ctx.sb(f"at{u}_rz", [128, 2 * 512], F32)

    onesq = ctx.sb(f"at{u}_1s", [128, 128], BF16)
    if convv is not None:
        wv0_sb = _pool[:, NCH * DVC:2 * NCH * DVC]
    if woT is not None:
        wo_sb = ctx.sb(f"at{u}_wo", [128, 2 * NHc * HD], BF16)
        y_sb = ctx.sb(f"at{u}_y", [128, 2 * 512], BF16)

    pA = ctx.psA[:, :]
    pb4, pb5, pb6, pb7 = ctx.pb[4], ctx.pb[5], ctx.pb[6], ctx.pb[7]
    pp = [pb6, pb7]

    s_ld = ctx.sem(f"at{u}_ld")
    s_wkb = [ctx.sem(f"at{u}_wk0"), ctx.sem(f"at{u}_wk1")]
    s_wqb = [ctx.sem(f"at{u}_wq0"), ctx.sem(f"at{u}_wq1")]
    s_wob = [ctx.sem(f"at{u}_wo0"), ctx.sem(f"at{u}_wo1")]
    s_vp = ctx.sem(f"at{u}_vp")
    s_ve = ctx.sem(f"at{u}_ve")
    s_kq = ctx.sem(f"at{u}_kq")
    s_kqe = ctx.sem(f"at{u}_kqe")
    s_build = ctx.sem(f"at{u}_bld")
    s_sc = ctx.sem(f"at{u}_sc")
    s_pr = ctx.sem(f"at{u}_pr")
    s_pm = ctx.sem(f"at{u}_pm")
    s_pvg = ctx.sem(f"at{u}_pvg")
    s_rz = ctx.sem(f"at{u}_rz")
    s_oe = ctx.sem(f"at{u}_oe")
    s_op = ctx.sem(f"at{u}_op")
    s_ye = ctx.sem(f"at{u}_ye")
    s_ydb = [ctx.sem(f"at{u}_yd0"), ctx.sem(f"at{u}_yd1")]

    NS = cdiv(DVC, 512)
    n_groups = NHc // group

    def kc_list(qt):
        kc_end = 4 * qt + 4
        kc_start = 0
        if window is not None:
            kc_start = max(0, (qt * 512 - (window - 1)) // 128)
            if kc_start % 2:
                kc_start -= 1
        return list(range(kc_start, kc_end))

    def mask_for(qt, kc):
        ms = []
        r = kc - 4 * qt
        if 0 <= r < 4:
            ms.append(cfg["masks"][:, r * 512:(r + 1) * 512])
        if window is not None:
            q0 = qt * 512
            lo_k = q0 - (window - 1)
            if lo_k > 0 and kc * 128 < q0 - 512:
                rl = kc - (lo_k // 128)
                assert 0 <= rl < 4, (qt, kc, rl)
                ms.append(cfg["masks_lo"][:, rl * 512:(rl + 1) * 512])
        return ms

    builds = []
    for g in range(n_groups):
        builds.append(("K", g))
        for j in range(group):
            builds.append(("Q", g * group + j))
    sgroups = []
    for g in range(n_groups):
        for j in range(group):
            h = g * group + j
            for qt in range(NQT):
                kcs = kc_list(qt)
                gs = [kcs[i:i + 2] for i in range(0, len(kcs), 2)]
                for gi, kpair in enumerate(gs):
                    sgroups.append(dict(h=h, g=g, qt=qt, gi=gi, kcs=kpair,
                                        first=(gi == 0), last=(gi == len(gs) - 1)))
    n_sg = len(sgroups)
    mrun = []
    tot = 0
    for sg in sgroups:
        nm = sum(len(mask_for(sg["qt"], kc)) for kc in sg["kcs"])
        tot += nm
        mrun.append(tot)

    tile_of_build_end = {}
    ti = 0
    for b in builds:
        ti += NQT
        tile_of_build_end[b] = ti
    n_build_tiles = ti

    def fload(sy):
        ins = sy.dma_start(out=wv_sb[:].rearrange("p (c v) -> p c v", c=NCH),
                           in_=cfg["wvT"].rearrange("(c p) v -> p c v", p=128))
        s_ld.inc(ins, 16)
    ctx.on("sync", fload)

    def fmemset(v):
        v.memset(onesq[:], 1.0)
    ctx.on("vector", fmemset)
    ctx.barrier()

    if convv is not None:
        w0bc, w1bc = convv
        def fconvv(v):
            s_ld.wait(v, 16)
            ins = None
            for c in range(NCH):
                ins = v.tensor_mul(out=wv0_sb[:, c * DVC:(c + 1) * DVC],
                                   in0=wv_sb[:, c * DVC:(c + 1) * DVC], in1=w0bc[:])
            ctx.edge("vector", v, ins)
            for c in range(NCH):
                ins = v.tensor_mul(out=wv_sb[:, c * DVC:(c + 1) * DVC],
                                   in0=wv_sb[:, c * DVC:(c + 1) * DVC], in1=w1bc[:])
            ctx.edge("vector", v, ins)
        ctx.on("vector", fconvv)
        ctx.barrier()

    def fload2(sy):
        for bi, b in enumerate(builds):
            kind, idx = b
            same = [bb for bb in builds[:bi] if bb[0] == kind]
            if len(same) >= 2:
                s_kq.wait(sy, tile_of_build_end[same[-2]])
            if kind == "K":
                o0 = (idx % 2) * NCH * HD
                dst = wk_sb[:, o0:o0 + NCH * HD].rearrange("p (c f) -> p c f", c=NCH)
                src = cfg["wkT"].rearrange("(c p) f -> p c f", p=128)[:, :, idx * HD:(idx + 1) * HD]
                ins = sy.dma_start(out=dst, in_=src)
                s_wkb[idx % 2].inc(ins, 16)
            else:
                o0 = (idx % 2) * NCH * HD
                dst = wq_sb[:, o0:o0 + NCH * HD].rearrange("p (c f) -> p c f", c=NCH)
                src = cfg["wqT"].rearrange("(c p) f -> p c f", p=128)[:, :, idx * HD:(idx + 1) * HD]
                ins = sy.dma_start(out=dst, in_=src)
                s_wqb[idx % 2].inc(ins, 16)
        if woT is not None:
            for fo in range(NCH):
                if fo >= 2:
                    s_op.wait(sy, (fo - 1) * NQT)
                o0 = (fo % 2) * NHc * HD
                dst = wo_sb[:, o0:o0 + NHc * HD].rearrange("p (c f) -> p c f", c=NHc)
                src = woT.rearrange("(c p) f -> p c f", p=128)[:, :, fo * HD:(fo + 1) * HD]
                ins = sy.dma_start(out=dst, in_=src)
                s_wob[fo % 2].inc(ins, 16)
    ctx.on("sync", fload2)

    def ft(te):
        s_ld.wait(te, 16)
        for tc in range(NTC):
            for s in range(NS):
                iv = tc * NS + s
                bank = pA[:, ((iv % 4)) * 512: ((iv % 4)) * 512 + 512]
                ncols = min(512, DVC - s * 512)
                if iv >= 4:
                    s_ve.wait(te, iv - 3)
                for ic in range(NCH):
                    last = (ic == NCH - 1) and convv is None
                    ins = te.matmul(bank[:, 0:ncols], xsl(ic, pad + tc * 128, 128),
                                    wv_sb[:, ic * DVC + s * 512: ic * DVC + s * 512 + ncols],
                                    start=(ic == 0), stop=last)
                    if convv is not None:
                        last = (ic == NCH - 1)
                        ins = te.matmul(bank[:, 0:ncols], xsl(ic, tc * 128, 128),
                                        wv0_sb[:, ic * DVC + s * 512: ic * DVC + s * 512 + ncols],
                                        start=False, stop=last)
                s_vp.inc(ins)
        bi_ti = 0
        sg_i = 0
        hq = 0
        builds_done = 0
        for g in range(n_groups):
            s_wkb[g % 2].wait(te, 16 * (g // 2 + 1))
            for nt in range(NQT):
                if bi_ti >= 2:
                    s_kqe.wait(te, bi_ti - 1)
                bank = pp[bi_ti % 2]
                for ic in range(NCH):
                    ins = te.matmul(bank[:, :], wk_sb[:, (g % 2) * NCH * HD + ic * HD: (g % 2) * NCH * HD + ic * HD + HD],
                                    xsl(ic, pad + nt * 512, 512),
                                    start=(ic == 0), stop=(ic == NCH - 1))
                s_kq.inc(ins)
                bi_ti += 1
            builds_done += 1
            for j in range(group):
                h = g * group + j
                s_wqb[h % 2].wait(te, 16 * (h // 2 + 1))
                for nt in range(NQT):
                    if bi_ti >= 2:
                        s_kqe.wait(te, bi_ti - 1)
                    bank = pp[bi_ti % 2]
                    for ic in range(NCH):
                        ins = te.matmul(bank[:, :], wq_sb[:, (h % 2) * NCH * HD + ic * HD: (h % 2) * NCH * HD + ic * HD + HD],
                                        xsl(ic, pad + nt * 512, 512),
                                        start=(ic == 0), stop=(ic == NCH - 1))
                    s_kq.inc(ins)
                    bi_ti += 1
                builds_done += 1
                s_build.wait(te, builds_done)
                for qt in range(NQT):
                    first_sg_of_hq = True
                    while sg_i < n_sg and sgroups[sg_i]["h"] == h and sgroups[sg_i]["qt"] == qt:
                        sg = sgroups[sg_i]
                        pair = sg_i % 2
                        if sg_i >= 2:
                            s_pr.wait(te, sg_i - 1)
                        else:
                            s_ve.wait(te, NTC * NS)
                        for ci, kc in enumerate(sg["kcs"]):
                            bank = pA[:, (pair * 2 + ci) * 512:(pair * 2 + ci) * 512 + 512]
                            ins = te.matmul(bank, k_sb[:, kc * 128:kc * 128 + 128],
                                            q_sb[:, (h % 2) * T_ + qt * 512:(h % 2) * T_ + qt * 512 + 512],
                                            start=True, stop=True)
                        s_sc.inc(ins)
                        if mrun[sg_i] > (mrun[sg_i - 1] if sg_i else 0):
                            s_pm.wait(te, mrun[sg_i])
                        else:
                            s_pr.wait(te, sg_i + 1)
                        if first_sg_of_hq and hq > 0:
                            s_oe.wait(te, hq)
                        for ci, kc in enumerate(sg["kcs"]):
                            prb = probs[:, pair * 1024 + ci * 512: pair * 1024 + ci * 512 + 512]
                            lastmm = sg["last"] and ci == len(sg["kcs"]) - 1
                            te.matmul(pb4[:, :], v_sb[:, kc * DVC + g * HD: kc * DVC + g * HD + HD],
                                      prb, start=(first_sg_of_hq and ci == 0), stop=lastmm)
                            ins = te.matmul(pb5[:, :], onesq[:], prb,
                                            start=(first_sg_of_hq and ci == 0), stop=lastmm)
                        s_pvg.inc(ins)
                        first_sg_of_hq = False
                        sg_i += 1
                    hq += 1
        if woT is not None:
            s_oe.wait(te, NHc * NQT)
            opi = 0
            n_build_tiles_local = bi_ti
            for fo in range(NCH):
                s_wob[fo % 2].wait(te, 16 * (fo // 2 + 1))
                for qt in range(NQT):
                    if opi >= 2:
                        s_ye.wait(te, opi - 1)
                    elif bi_ti >= 2:
                        s_kqe.wait(te, bi_ti - 1)
                    bank = pp[bi_ti % 2]
                    for hc in range(NHc):
                        ins = te.matmul(bank[:, :],
                                        wo_sb[:, (fo % 2) * NHc * HD + hc * HD:(fo % 2) * NHc * HD + hc * HD + HD],
                                        o_sb[:, hc * T_ + qt * 512: hc * T_ + qt * 512 + 512],
                                        start=(hc == 0), stop=(hc == NHc - 1))
                    s_op.inc(ins)
                    bi_ti += 1
                    opi += 1
    ctx.on("tensor", ft)

    def fv(v):
        for tc in range(NTC):
            for s in range(NS):
                iv = tc * NS + s
                bank = pA[:, (iv % 4) * 512: (iv % 4) * 512 + 512]
                ncols = min(512, DVC - s * 512)
                s_vp.wait(v, iv + 1)
                dst = v_sb[:, tc * DVC + s * 512: tc * DVC + s * 512 + ncols]
                if cfg.get("bv_bc") is not None:
                    ins = v.tensor_add(out=dst, in0=bank[:, 0:ncols], in1=cfg["bv_bc"][:, s * 512:s * 512 + ncols])
                else:
                    ins = v.tensor_copy(out=dst, in_=bank[:, 0:ncols])
                s_ve.inc(ins)
        bi_ti = 0
        sg_i = 0
        hq = 0
        for g in range(n_groups):
            for nt in range(NQT):
                s_kq.wait(v, bi_ti + 1)
                bank = pp[bi_ti % 2]
                dst = k_sb[:, nt * 512:(nt + 1) * 512]
                if cfg.get("bk_cols") is not None:
                    ins = v.tensor_scalar_add(out=dst, in0=bank[:, :], scalar1=cfg["bk_cols"][:, g:g + 1])
                else:
                    ins = v.tensor_copy(out=dst, in_=bank[:, :])
                s_kqe.inc(ins)
                bi_ti += 1
            ins = _postproc(ctx, v, k_sb, scr_sb, T_, rope, convk, g, ins)
            s_build.inc(ins)
            for j in range(group):
                h = g * group + j
                qb = q_sb[:, (h % 2) * T_:(h % 2) * T_ + T_]
                for nt in range(NQT):
                    s_kq.wait(v, bi_ti + 1)
                    bank = pp[bi_ti % 2]
                    dst = qb[:, nt * 512:(nt + 1) * 512]
                    if cfg.get("bq_cols") is not None:
                        ins = v.tensor_scalar_add(out=dst, in0=bank[:, :], scalar1=cfg["bq_cols"][:, h:h + 1])
                    else:
                        ins = v.tensor_copy(out=dst, in_=bank[:, :])
                    s_kqe.inc(ins)
                    bi_ti += 1
                ins = _postproc(ctx, v, qb, scr_sb, T_, rope, None, None, ins)
                s_build.inc(ins)
                for qt in range(NQT):
                    while sg_i < n_sg and sgroups[sg_i]["h"] == h and sgroups[sg_i]["qt"] == qt:
                        sg = sgroups[sg_i]
                        pair = sg_i % 2
                        nm = mrun[sg_i] - (mrun[sg_i - 1] if sg_i else 0)
                        if nm:
                            s_pr.wait(v, sg_i + 1)
                            for ci, kc in enumerate(sg["kcs"]):
                                prb = probs[:, pair * 1024 + ci * 512: pair * 1024 + ci * 512 + 512]
                                for m in mask_for(sg["qt"], kc):
                                    ins = v.tensor_mul(out=prb, in0=prb, in1=m)
                                    s_pm.inc(ins)
                        sg_i += 1
                    s_rz.wait(v, hq + 1)
                    ins = v.tensor_mul(out=o_sb[:, h * T_ + qt * 512: h * T_ + qt * 512 + 512],
                                       in0=pb4[:, :], in1=rz_sb[:, 512:1024])
                    s_oe.inc(ins)
                    hq += 1
        if woT is not None:
            opi = 0
            for fo in range(NCH):
                for qt in range(NQT):
                    s_op.wait(v, opi + 1)
                    if opi >= 2:
                        s_ydb[opi % 2].wait(v, (opi // 2) * 16)
                    bank = pp[(bi_ti + opi) % 2]
                    ins = v.tensor_copy(out=y_sb[:, (opi % 2) * 512:(opi % 2) * 512 + 512], in_=bank[:, :])
                    s_ye.inc(ins)
                    opi += 1
    ctx.on("vector", fv)

    def fs(se):
        sg_i = 0
        hq = 0
        for sgx in range(n_sg):
            sg = sgroups[sgx]
            pair = sgx % 2
            s_sc.wait(se, sgx + 1)
            if sgx >= 2:
                s_pvg.wait(se, sgx - 1)
            ins = se.activation(out=probs[:, pair * 1024:pair * 1024 + 1024],
                                in_=pA[:, pair * 1024:pair * 1024 + 1024],
                                func=AF.Exp, scale=scale)
            s_pr.inc(ins)
            if sg["last"]:
                s_pvg.wait(se, sgx + 1)
                if hq > 0:
                    s_oe.wait(se, hq)
                ins = se.activation(out=rz_sb[:, 0:512], in_=pb5[:, :], func=AF.Ln)
                ctx.edge("scalar", se, ins)
                ins = se.activation(out=rz_sb[:, 512:1024], in_=rz_sb[:, 0:512],
                                    func=AF.Exp, scale=-1.0)
                s_rz.inc(ins)
                hq += 1
    ctx.on("scalar", fs)

    if woT is not None:
        def fg(gp):
            opi = 0
            for fo in range(NCH):
                for qt in range(NQT):
                    s_ye.wait(gp, opi + 1)
                    ins = gp.dma_start(out=cfg["y_out"][fo * 128:(fo + 1) * 128, qt * 512:(qt + 1) * 512],
                                       in_=y_sb[:, (opi % 2) * 512:(opi % 2) * 512 + 512])
                    s_ydb[opi % 2].inc(ins, 16)
                    opi += 1
            s_ydb[0].wait(gp, s_ydb[0].n)
            s_ydb[1].wait(gp, s_ydb[1].n)
        ctx.on("gpsimd", fg)
    return o_sb


def _postproc(ctx, v, buf, scr, T_, rope, convk_cols, g, last_ins):
    ins = last_ins
    if rope is None and convk_cols is None:
        return v.tensor_copy(out=scr[:, 0:1], in_=buf[:, 0:1])
    tiny = v.tensor_copy(out=scr[:, 0:1], in_=buf[:, 0:1])
    ctx.edge("vector", v, tiny)
    if rope is not None:
        C2, S2 = rope
        v.tensor_copy(out=scr[0:64, :], in_=buf[64:128, 0:T_])
        ins = v.tensor_copy(out=scr[64:128, :], in_=buf[0:64, 0:T_])
        ctx.edge("vector", v, ins)
        ins = v.tensor_mul(out=scr[:, :], in0=scr[:, :], in1=S2[:])
        ins2 = v.tensor_mul(out=buf[:, 0:T_], in0=buf[:, 0:T_], in1=C2[:])
        ctx.edge("vector", v, ins2)
        ins = v.tensor_add(out=buf[:, 0:T_], in0=buf[:, 0:T_], in1=scr[:, :])
    if convk_cols is not None:
        NKVc = convk_cols.shape[1] // 2
        w0 = convk_cols[:, g:g + 1]
        w1 = convk_cols[:, NKVc + g:NKVc + g + 1]
        if rope is not None:
            ctx.edge("vector", v, ins)
        ins = v.tensor_scalar_mul(out=scr[:, 0:T_ - 1], in0=buf[:, 0:T_ - 1], scalar1=w0)
        ctx.edge("vector", v, ins)
        v.scalar_tensor_tensor(out=buf[:, 1:T_], in0=buf[:, 1:T_], scalar=w1,
                               in1=scr[:, 0:T_ - 1], op0=ALU.mult, op1=ALU.add)
        ins = v.tensor_scalar_mul(out=buf[:, 0:1], in0=buf[:, 0:1], scalar1=w1)
    return ins


def emit_l1ffn(ctx, xn_sb, x_sb, ntok, cfg):
    nc = ctx.nc
    u = ctx.uid()
    INTER = 8192
    NOC = INTER // 128
    TT = 256
    NTILE = ntok // TT
    wg, wu, wd = cfg["wg"], cfg["wu"], cfg["wd"]
    sg_sb, su_sb, sd_sb = cfg["sg_sb"], cfg["su_sb"], cfg["sd_sb"]

    wgu_sb = ctx.sb(f"lf{u}_wgu", [128, 2 * 2 * NCH * 128], BF16)
    dn_sb = ctx.sb(f"lf{u}_dn", [128, 2 * NCH * 128], BF16)
    fuse_sb = ctx.sb(f"lf{u}_fuse", [128, NOC * TT], BF16)
    gs_sb = ctx.sb(f"lf{u}_gs", [128, 2 * TT], BF16)

    s_wg = [ctx.sem(f"lf{u}_wg0"), ctx.sem(f"lf{u}_wg1")]
    s_wd = [ctx.sem(f"lf{u}_wd0"), ctx.sem(f"lf{u}_wd1")]
    s_gu = ctx.sem(f"lf{u}_gu")
    s_si = ctx.sem(f"lf{u}_si")
    s_fe = ctx.sem(f"lf{u}_fe")
    s_dn = ctx.sem(f"lf{u}_dnm")
    s_xu = ctx.sem(f"lf{u}_xu")

    pb = ctx.pb

    def fload(sy):
        for t in range(NTILE):
            for oc in range(NOC):
                gidx = t * NOC + oc
                par = gidx % 2
                if gidx >= 2:
                    s_gu.wait(sy, gidx - 1)
                o0 = par * 2 * NCH * 128
                dst = wgu_sb[:, o0:o0 + NCH * 128].rearrange("p (c f) -> p c f", c=NCH)
                ins = sy.dma_start(out=dst, in_=wg.rearrange("(c p) f -> p c f", p=128)[:, :, oc * 128:(oc + 1) * 128])
                s_wg[par].inc(ins, 16)
                dst = wgu_sb[:, o0 + NCH * 128:o0 + 2 * NCH * 128].rearrange("p (c f) -> p c f", c=NCH)
                ins = sy.dma_start(out=dst, in_=wu.rearrange("(c p) f -> p c f", p=128)[:, :, oc * 128:(oc + 1) * 128])
                s_wg[par].inc(ins, 16)
            for sp in range(2):
                for oc in range(NOC):
                    didx = (t * 2 + sp) * NOC + oc
                    par = didx % 2
                    if didx >= 2:
                        s_dn.wait(sy, didx - 1)
                    o0 = par * NCH * 128
                    dst = dn_sb[:, o0:o0 + NCH * 128]
                    ins = sy.dma_start(out=dst, in_=wd[oc * 128:(oc + 1) * 128, :])
                    s_wd[par].inc(ins, 16)
    ctx.on("sync", fload)

    def ft(te):
        for t in range(NTILE):
            if t > 0:
                s_xu.wait(te, t * NCH)
            for oc in range(NOC):
                gidx = t * NOC + oc
                par = gidx % 2
                s_wg[par].wait(te, 32 * (gidx // 2 + 1))
                if gidx >= 2:
                    s_fe.wait(te, gidx - 1)
                gbank = pb[par][:, 0:TT]
                ubank = pb[2 + par][:, 0:TT]
                o0 = par * 2 * NCH * 128
                for ic in range(NCH):
                    te.matmul(gbank, wgu_sb[:, o0 + ic * 128:o0 + ic * 128 + 128],
                              xn_sb[:, ic * ntok + t * TT: ic * ntok + t * TT + TT],
                              start=(ic == 0), stop=(ic == NCH - 1))
                for ic in range(NCH):
                    ins = te.matmul(ubank, wgu_sb[:, o0 + NCH * 128 + ic * 128:o0 + NCH * 128 + ic * 128 + 128],
                                    xn_sb[:, ic * ntok + t * TT: ic * ntok + t * TT + TT],
                                    start=(ic == 0), stop=(ic == NCH - 1))
                s_gu.inc(ins)
            s_fe.wait(te, (t + 1) * NOC)
            for sp in range(2):
                s_xu.wait(te, (t * 2 + sp) * 8)
                for oc in range(NOC):
                    didx = (t * 2 + sp) * NOC + oc
                    par = didx % 2
                    s_wd[par].wait(te, 16 * (didx // 2 + 1))
                    o0 = par * NCH * 128
                    for i8 in range(8):
                        dc = sp * 8 + i8
                        ins = te.matmul(pb[i8][:, 0:TT],
                                        dn_sb[:, o0 + dc * 128:o0 + dc * 128 + 128],
                                        fuse_sb[:, oc * TT:(oc + 1) * TT],
                                        start=(oc == 0), stop=(oc == NOC - 1))
                    s_dn.inc(ins)
    ctx.on("tensor", ft)

    def fs(se):
        for t in range(NTILE):
            for oc in range(NOC):
                gidx = t * NOC + oc
                par = gidx % 2
                s_gu.wait(se, gidx + 1)
                gbank = pb[par][:, 0:TT]
                ins = se.activation(out=gs_sb[:, par * TT:par * TT + TT], in_=gbank,
                                    func=AF.Silu, scale=sg_sb[:, oc:oc + 1])
                s_si.inc(ins)
    ctx.on("scalar", fs)

    def fv(v):
        for t in range(NTILE):
            for oc in range(NOC):
                gidx = t * NOC + oc
                par = gidx % 2
                s_si.wait(v, gidx + 1)
                ubank = pb[2 + par][:, 0:TT]
                ins = v.scalar_tensor_tensor(out=fuse_sb[:, oc * TT:(oc + 1) * TT],
                                             in0=ubank, scalar=su_sb[:, oc:oc + 1],
                                             in1=gs_sb[:, par * TT:par * TT + TT],
                                             op0=ALU.mult, op1=ALU.mult)
                s_fe.inc(ins)
            for sp in range(2):
                s_dn.wait(v, (t * 2 + sp + 1) * NOC)
                for i8 in range(8):
                    dc = sp * 8 + i8
                    xs = x_sb[:, dc * ntok + t * TT: dc * ntok + t * TT + TT]
                    ins = v.scalar_tensor_tensor(out=xs, in0=pb[i8][:, 0:TT], scalar=sd_sb[:, dc:dc + 1],
                                                 in1=xs, op0=ALU.mult, op1=ALU.add)
                    s_xu.inc(ins)
    ctx.on("vector", fv)
    return s_xu


def emit_l0ffn(ctx, xn_sb, x_sb, ntok, cfg):
    nc = ctx.nc
    u = ctx.uid()
    DFF = 5632
    NJ = DFF // 128
    TT = 256
    NTILE = ntok // TT
    E = 8
    wg, wu, wd, wr = cfg["wg"], cfg["wu"], cfg["wd"], cfg["wr"]
    agu, bgu, adt, bdt = cfg["agu"], cfg["bgu"], cfg["adt"], cfg["bdt"]
    ident_b = cfg["ident_b"]
    oh8 = cfg["oh8"]

    G_sb = ctx.sb(f"mf{u}_G", [128, NJ * TT], BF16)
    U_sb = ctx.sb(f"mf{u}_U", [128, NJ * TT], BF16)
    wgu_sb = ctx.sb(f"mf{u}_wgu", [128, 2 * 2 * NCH * 128], BF16)
    wd_sb = ctx.sb(f"mf{u}_wd", [128, 2 * 1024], BF16)
    agu_sb = ctx.sb(f"mf{u}_agu", [128, NCH * 128], BF16)
    bgu_sb = ctx.sb(f"mf{u}_bgu", [16, 2048], BF16)
    adt_sb = ctx.sb(f"mf{u}_adt", [128, 2 * 128], BF16)
    bdt_sb = ctx.sb(f"mf{u}_bdt", [16, E * 1024], BF16)
    wr_sb = ctx.sb(f"mf{u}_wr", [128, NCH * E], BF16)
    r_sb = ctx.sb(f"mf{u}_r", [16, 2 * E * TT], BF16)
    glb = ctx.sbufs["sh_scr1"] if "sh_scr1" in ctx.sbufs else ctx.sb(f"mf{u}_gl", [128, 5 * TT], BF16)
    ulb = ctx.sb(f"mf{u}_ul", [128, 5 * TT], BF16)
    hbuf = ctx.sb(f"mf{u}_h", [128, 5 * TT], BF16)
    t_sb = ctx.sb(f"mf{u}_t", [16, E * TT], BF16)
    rt_sb = ctx.sb(f"mf{u}_rt", [128, 16], F32)
    we_tok = ctx.sb(f"mf{u}_wet", [128, 2 * 8], BF16)
    we_all = ctx.sb(f"mf{u}_wea", [8, TT], BF16)
    webc_sb = ctx.sb(f"mf{u}_web", [16, TT], F32)
    scr1 = ctx.sb(f"mf{u}_s1", [128, 1], BF16)

    s_wgu = [ctx.sem(f"mf{u}_wg0"), ctx.sem(f"mf{u}_wg1")]
    s_wd = [ctx.sem(f"mf{u}_wd0"), ctx.sem(f"mf{u}_wd1")]
    s_bgu = [ctx.sem(f"mf{u}_bg0"), ctx.sem(f"mf{u}_bg1")]
    s_adt = [ctx.sem(f"mf{u}_ad0"), ctx.sem(f"mf{u}_ad1")]
    s_ld1 = ctx.sem(f"mf{u}_ld1")
    s_bd = ctx.sem(f"mf{u}_bd")
    s_gu = ctx.sem(f"mf{u}_gu")
    s_gue = ctx.sem(f"mf{u}_gue")
    s_rt = ctx.sem(f"mf{u}_rtp")
    s_nm = ctx.sem(f"mf{u}_nm")
    s_rte = ctx.sem(f"mf{u}_rte")
    s_wet = ctx.sem(f"mf{u}_wet")
    s_wtr = ctx.sem(f"mf{u}_wtr")
    s_wea = ctx.sem(f"mf{u}_weaS")
    s_r = ctx.sem(f"mf{u}_rr")
    s_re = ctx.sem(f"mf{u}_re")
    s_x1 = ctx.sem(f"mf{u}_x1")
    s_x1e = ctx.sem(f"mf{u}_x1e")
    s_sil = ctx.sem(f"mf{u}_sil")
    s_h = ctx.sem(f"mf{u}_hh")
    s_hdone = ctx.sem(f"mf{u}_hd")
    s_adm = ctx.sem(f"mf{u}_adm")
    s_web = ctx.sem(f"mf{u}_webc")
    s_te = ctx.sem(f"mf{u}_tev")
    s_dn = ctx.sem(f"mf{u}_dns")
    s_xu = ctx.sem(f"mf{u}_xus")

    pb = ctx.pb
    NU = NJ + 1

    def xnsl(ic, t, n=TT, o=0):
        return xn_sb[:, ic * ntok + t * TT + o: ic * ntok + t * TT + o + n]

    def fload(sy):
        ins = sy.dma_start(out=wr_sb[:].rearrange("p (c f) -> p c f", c=NCH),
                           in_=wr.rearrange("(c p) f -> p c f", p=128))
        s_ld1.inc(ins, 16)
        for t in range(NTILE):
            for j in range(NJ):
                gidx = t * NJ + j
                par = gidx % 2
                if gidx >= 2:
                    s_gu.wait(sy, gidx - 1)
                o0 = par * 2 * NCH * 128
                dst = wgu_sb[:, o0:o0 + NCH * 128].rearrange("p (c f) -> p c f", c=NCH)
                ins = sy.dma_start(out=dst, in_=wg.rearrange("(c p) f -> p c f", p=128)[:, :, j * 128:(j + 1) * 128])
                s_wgu[par].inc(ins, 16)
                dst = wgu_sb[:, o0 + NCH * 128:o0 + 2 * NCH * 128].rearrange("p (c f) -> p c f", c=NCH)
                ins = sy.dma_start(out=dst, in_=wu.rearrange("(c p) f -> p c f", p=128)[:, :, j * 128:(j + 1) * 128])
                s_wgu[par].inc(ins, 16)
            for side in range(2):
                si = t * 2 + side
                if si >= 1:
                    s_r.wait(sy, si * E)
                ins = sy.dma_start(out=agu_sb[:].rearrange("p (c f) -> p c f", c=NCH),
                                   in_=agu.rearrange("(c p) f -> p c f", p=128)[:, :, side * 128:(side + 1) * 128])
                s_agu = ctx.sem(f"mf{u}_agu_s")
                s_agu.inc(ins, 16)
            for b2 in range(2):
                for j in range(NJ):
                    gidx = (t * 2 + b2) * NJ + j
                    par = gidx % 2
                    if gidx >= 1:
                        s_x1.wait(sy, 2 * gidx)
                    ins = sy.dma_start(out=bgu_sb[:, 0:2048],
                                       in_=bgu[:, (b2 * NJ + j) * 2048:(b2 * NJ + j + 1) * 2048])
                    s_bgu[0].inc(ins, 16)
                    if gidx >= 2:
                        s_adm.wait(sy, gidx - 1)
                    ins = sy.dma_start(out=adt_sb[:, par * 128:(par + 1) * 128],
                                       in_=adt[:, j * 128:(j + 1) * 128])
                    s_adt[par].inc(ins, 16)
            for sp in range(2):
                spi = t * 2 + sp
                if spi >= 1:
                    s_dn.wait(sy, spi * NU)
                ins = sy.dma_start(out=bdt_sb[:].rearrange("p (e f) -> p e f", e=E),
                                   in_=bdt.rearrange("p (e f) -> p e f", e=E)[:, :, sp * 1024:(sp + 1) * 1024])
                s_bd.inc(ins, 16)
                for j in range(NJ):
                    didx = spi * NJ + j
                    par = didx % 2
                    Lg = spi * NU + j
                    if didx >= 2:
                        s_dn.wait(sy, Lg - 1)
                    ins = sy.dma_start(out=wd_sb[:, par * 1024:(par + 1) * 1024],
                                       in_=wd[j * 128:(j + 1) * 128, sp * 1024:(sp + 1) * 1024])
                    s_wd[par].inc(ins, 16)
    ctx.on("sync", fload)

    def ft(te):
        s_ld1.wait(te, 16)
        for t in range(NTILE):
            for tk in range(2):
                if t > 0 and tk == 0:
                    s_xu.wait(te, t * 16)
                bank = pb[4 + tk]
                for ic in range(NCH):
                    ins = te.matmul(bank[:, 0:E], xnsl(ic, t, 128, tk * 128),
                                    wr_sb[:, ic * E:(ic + 1) * E], start=(ic == 0), stop=(ic == NCH - 1))
                s_rt.inc(ins)
            for tk in range(2):
                s_wet.wait(te, 2 * t + tk + 1)
                pv = pb[6].bitcast(BF16)
                ins = te.transpose(pv[0:8, 0:128], we_tok[:, tk * 8:(tk + 1) * 8], ident_b[:])
                s_wtr.inc(ins)
            for j in range(NJ):
                gidx = t * NJ + j
                par = gidx % 2
                s_wgu[par].wait(te, 32 * (gidx // 2 + 1))
                if gidx >= 2:
                    s_gue.wait(te, gidx - 1)
                elif t == 0 and j == 0:
                    pass
                if t > 0 and j < 2:
                    s_xu.wait(te, t * 16)
                gbank = pb[par][:, 0:TT]
                ubank = pb[2 + par][:, 0:TT]
                o0 = par * 2 * NCH * 128
                for ic in range(NCH):
                    te.matmul(gbank, wgu_sb[:, o0 + ic * 128:o0 + ic * 128 + 128], xnsl(ic, t),
                              start=(ic == 0), stop=(ic == NCH - 1))
                for ic in range(NCH):
                    ins = te.matmul(ubank, wgu_sb[:, o0 + NCH * 128 + ic * 128:o0 + NCH * 128 + ic * 128 + 128],
                                    xnsl(ic, t), start=(ic == 0), stop=(ic == NCH - 1))
                s_gu.inc(ins)
            s_rte.wait(te, 2 * t + 2)
            s_wea.wait(te, 2 * t + 2)
            for side in range(2):
                ctx.sem(f"mf{u}_agu_s").wait(te, 16 * (t * 2 + side + 1))
                for e in range(E):
                    ri = (t * 2 + side) * E + e
                    bank = pb[4 + ri % 4][0:16, 0:TT]
                    if ri >= 4:
                        s_re.wait(te, ri - 3)
                    for ic in range(NCH):
                        ins = te.matmul(bank, agu_sb[:, ic * 128 + e * 16: ic * 128 + e * 16 + 16],
                                        xnsl(ic, t), start=(ic == 0), stop=(ic == NCH - 1))
                    s_r.inc(ins)
            for b2 in range(2):
                for j in range(NJ):
                    gidx = (t * 2 + b2) * NJ + j
                    par = gidx % 2
                    s_bgu[0].wait(te, 16 * (gidx + 1))
                    s_adt[par].wait(te, 16 * (gidx // 2 + 1))
                    if b2 == 0 and j == 0:
                        s_gue.wait(te, (t + 1) * NJ)
                        s_re.wait(te, 2 * E * (t + 1))
                    for side in range(2):
                        if b2 == 1 and j == 0 and side == 0:
                            s_te.wait(te, 2 * t + 1)
                        if gidx * 2 + side >= 1 and not (b2 == 0 and j == 0 and side == 0):
                            s_x1e.wait(te, 2 * gidx + side)
                        for e4 in range(4):
                            e = b2 * 4 + e4
                            ins = te.matmul(pb[e4][:, 0:TT],
                                            bgu_sb[0:16, side * 1024 + e4 * 256: side * 1024 + e4 * 256 + 128],
                                            r_sb[0:16, (side * E + e) * TT:(side * E + e) * TT + TT],
                                            start=True, stop=True)
                        s_x1.inc(ins)
                    s_h.wait(te, gidx + 1)
                    if b2 == 1 and j == 0:
                        s_te.wait(te, 2 * t + 1)
                    for e4 in range(4):
                        e = b2 * 4 + e4
                        ins = te.matmul(pb[4 + e4][0:16, 0:TT],
                                        adt_sb[:, par * 128 + e * 16: par * 128 + e * 16 + 16],
                                        hbuf[:, e4 * TT:(e4 + 1) * TT],
                                        start=(j == 0), stop=(j == NJ - 1))
                    s_adm.inc(ins)
                s_x1e.wait(te, 2 * ((t * 2 + b2) * NJ + NJ))
                for e4 in range(4):
                    e = b2 * 4 + e4
                    ins = te.matmul(pb[e4][0:16, 0:TT], oh8[0:8, e * 16:(e + 1) * 16],
                                    we_all[0:8, 0:TT], start=True, stop=True)
                s_web.inc(ins)
            for sp in range(2):
                if sp == 0:
                    s_te.wait(te, 2 * (t + 1))
                s_xu.wait(te, (t * 2 + sp) * 8)
                for j in range(NJ):
                    didx = (t * 2 + sp) * NJ + j
                    par = didx % 2
                    s_wd[par].wait(te, 16 * (didx // 2 + 1))
                    for i8 in range(8):
                        ins = te.matmul(pb[i8][:, 0:TT],
                                        wd_sb[:, par * 1024 + i8 * 128: par * 1024 + i8 * 128 + 128],
                                        G_sb[:, j * TT:(j + 1) * TT],
                                        start=(j == 0), stop=False)
                    s_dn.inc(ins)
                s_bd.wait(te, 16 * (t * 2 + sp + 1))
                for e in range(E):
                    for i8 in range(8):
                        ins = te.matmul(pb[i8][:, 0:TT],
                                        bdt_sb[0:16, e * 1024 + i8 * 128: e * 1024 + i8 * 128 + 128],
                                        t_sb[0:16, e * TT:(e + 1) * TT],
                                        start=False, stop=(e == E - 1))
                s_dn.inc(ins)
    ctx.on("tensor", ft)

    def fs(se):
        for t in range(NTILE):
            for tk in range(2):
                s_rt.wait(se, 2 * t + tk + 1)
                s_nm.wait(se, 2 * t + tk + 1)
                ins = se.activation(out=rt_sb[:, 8:16], in_=pb[4 + tk][:, 0:E], func=AF.Exp,
                                    bias=rt_sb[:, 0:1])
                s_rte.inc(ins)
            for b2 in range(2):
                for j in range(NJ):
                    gidx = (t * 2 + b2) * NJ + j
                    s_x1e.wait(se, 2 * gidx + 1)
                    if gidx > 0:
                        s_hdone.wait(se, gidx)
                        s_adm.wait(se, gidx)
                    ins = se.activation(out=hbuf[:, 0:5 * TT], in_=glb[:, 0:5 * TT], func=AF.Silu)
                    s_sil.inc(ins)
    ctx.on("scalar", fs)

    def fv(v):
        for t in range(NTILE):
            for tk in range(2):
                s_rt.wait(v, 2 * t + tk + 1)
                ins = v.reduce_max(out=rt_sb[:, 0:1], in_=pb[4 + tk][:, 0:E], axis=AX.X, negate=True)
                s_nm.inc(ins)
                s_rte.wait(v, 2 * t + tk + 1)
                ins = v.reduce_sum(out=rt_sb[:, 1:2], in_=rt_sb[:, 8:16], axis=AX.X)
                ctx.edge("vector", v, ins)
                ins = v.reciprocal(out=rt_sb[:, 2:3], in_=rt_sb[:, 1:2])
                ctx.edge("vector", v, ins)
                ins = v.tensor_scalar_mul(out=rt_sb[:, 8:16], in0=rt_sb[:, 8:16], scalar1=rt_sb[:, 2:3])
                ctx.edge("vector", v, ins)
                ins = v.max(out=rt_sb[:, 0:8], in_=rt_sb[:, 8:16])
                ctx.edge("vector", v, ins)
                ins = v.tensor_scalar(out=we_tok[:, tk * 8:(tk + 1) * 8], in0=rt_sb[:, 8:16],
                                      scalar1=rt_sb[:, 1:2], scalar2=2.0,
                                      op0=ALU.is_ge, op1=ALU.mult)
                ctx.edge("vector", v, ins)
                ins = v.tensor_mul(out=we_tok[:, tk * 8:(tk + 1) * 8], in0=we_tok[:, tk * 8:(tk + 1) * 8],
                                   in1=rt_sb[:, 8:16])
                s_wet.inc(ins)
                s_wtr.wait(v, 2 * t + tk + 1)
                ins = v.tensor_copy(out=we_all[0:8, tk * 128:(tk + 1) * 128],
                                    in_=pb[6].bitcast(BF16)[0:8, 0:128])
                s_wea.inc(ins)
            for j in range(NJ):
                gidx = t * NJ + j
                par = gidx % 2
                s_gu.wait(v, gidx + 1)
                v.tensor_copy(out=G_sb[:, j * TT:(j + 1) * TT], in_=pb[par][:, 0:TT])
                ins = v.tensor_copy(out=U_sb[:, j * TT:(j + 1) * TT], in_=pb[2 + par][:, 0:TT])
                s_gue.inc(ins)
            for side in range(2):
                for e in range(E):
                    ri = (t * 2 + side) * E + e
                    bank = pb[4 + ri % 4][0:16, 0:TT]
                    s_r.wait(v, ri + 1)
                    ins = v.tensor_scalar_mul(out=r_sb[0:16, (side * E + e) * TT:(side * E + e) * TT + TT],
                                              in0=bank, scalar1=2.0)
                    s_re.inc(ins)
            for b2 in range(2):
                for j in range(NJ):
                    gidx = (t * 2 + b2) * NJ + j
                    s_x1.wait(v, 2 * gidx + 1)
                    if gidx > 0:
                        s_sil.wait(v, gidx)
                    for e4 in range(4):
                        ins = v.tensor_copy(out=glb[:, e4 * TT:(e4 + 1) * TT], in_=pb[e4][:, 0:TT])
                    ctx.edge("vector", v, ins)
                    gsl = G_sb[:, j * TT:(j + 1) * TT]
                    gbc = gsl.unsqueeze(1).broadcast_to([128, 4, TT])
                    ins = v.tensor_add(out=glb[:, 0:4 * TT].rearrange("p (s f) -> p s f", s=4),
                                       in0=glb[:, 0:4 * TT].rearrange("p (s f) -> p s f", s=4), in1=gbc)
                    ins = v.tensor_copy(out=glb[:, 4 * TT:5 * TT], in_=gsl)
                    s_x1e.inc(ins)
                    s_x1.wait(v, 2 * gidx + 2)
                    if gidx > 0:
                        s_h.wait(v, gidx)
                    for e4 in range(4):
                        ins = v.tensor_copy(out=ulb[:, e4 * TT:(e4 + 1) * TT], in_=pb[e4][:, 0:TT])
                    ctx.edge("vector", v, ins)
                    usl = U_sb[:, j * TT:(j + 1) * TT]
                    ubc = usl.unsqueeze(1).broadcast_to([128, 4, TT])
                    ins = v.tensor_add(out=ulb[:, 0:4 * TT].rearrange("p (s f) -> p s f", s=4),
                                       in0=ulb[:, 0:4 * TT].rearrange("p (s f) -> p s f", s=4), in1=ubc)
                    s_x1e.inc(ins)
                    ins2 = v.tensor_copy(out=ulb[:, 4 * TT:5 * TT], in_=usl)
                    s_sil.wait(v, gidx + 1)
                    s_x1e.wait(v, s_x1e.n)
                    ctx.edge("vector", v, ins2)
                    ins = v.tensor_mul(out=hbuf[:, 0:5 * TT], in0=hbuf[:, 0:5 * TT], in1=ulb[:, 0:5 * TT])
                    s_h.inc(ins)
                    s_h.wait(v, s_h.n)
                    if b2 == 1:
                        ins = v.tensor_copy(out=G_sb[:, j * TT:(j + 1) * TT], in_=hbuf[:, 4 * TT:5 * TT])
                    else:
                        ins = v.tensor_copy(out=scr1[:], in_=hbuf[:, 0:1])
                    s_hdone.inc(ins)
                s_web.wait(v, 2 * t + b2 + 1)
                s_adm.wait(v, (t * 2 + b2 + 1) * NJ)
                for e4 in range(4):
                    e = b2 * 4 + e4
                    ins = v.tensor_copy(out=webc_sb[0:16, 0:TT], in_=pb[e4][0:16, 0:TT])
                    ctx.edge("vector", v, ins)
                    ins = v.tensor_mul(out=t_sb[0:16, e * TT:(e + 1) * TT],
                                       in0=pb[4 + e4][0:16, 0:TT], in1=webc_sb[0:16, 0:TT])
                    if e4 == 3:
                        s_te.inc(ins)
                    else:
                        ctx.edge("vector", v, ins)
            for sp in range(2):
                s_dn.wait(v, (t * 2 + sp + 1) * NU)
                for i8 in range(8):
                    dc = sp * 8 + i8
                    xs = x_sb[:, dc * ntok + t * TT: dc * ntok + t * TT + TT]
                    ins = v.tensor_add(out=xs, in0=pb[i8][:, 0:TT], in1=xs)
                    s_xu.inc(ins)
    ctx.on("vector", fv)
    return s_xu


def emit_l1attnb(ctx, xn_sb, ntok, cfg):
    nc = ctx.nc
    u = ctx.uid()
    wzT, whT = cfg["wzT"], cfg["whT"]
    cols = cfg["cols_sb"]
    tril_f = cfg["tril_f"]
    onesc_f = cfg["onesc_f"]
    pair_groups = cfg["pair_groups"]
    all_groups = cfg["all_groups"]
    x_load = cfg["x_load"]
    x_store = cfg["x_store"]
    NSTAT = float(cfg.get("nstat", 4.0 * 2049.0 * 2048.0))
    DUM = -1.0e6

    f0 = cols[:, 80:81]
    f1 = cols[:, 81:82]
    bzc = lambda c: cols[:, c:c + 1]
    bhc = lambda c: cols[:, 16 + c:17 + c]

    UB = ctx.sb(f"gr{u}_U", [128, NCH * ntok], F32)
    AB = ctx.sb(f"gr{u}_A", [128, NCH * ntok], F32)
    wbuf = ctx.sb(f"gr{u}_w", [128, 2 * 2048], BF16)
    tB = ctx.sb(f"gr{u}_tB", [128, 1024], F32)
    tC = ctx.sb(f"gr{u}_tC", [128, 1024], F32)
    tD = ctx.sb(f"gr{u}_tD", [128, 1024], F32)
    tE = ctx.sb(f"gr{u}_tE", [128, 1024], F32)
    tF = ctx.sb(f"gr{u}_tF", [128, 1024], F32)
    tG = ctx.sb(f"gr{u}_tG", [128, 1024], F32)
    prow = ctx.sb(f"gr{u}_pr", [1, 1024], F32)
    pub = ctx.sb(f"gr{u}_pub", [128, 32], F32)
    ppre = ctx.sb(f"gr{u}_ppre", [128, 32], F32)
    scol = ctx.sb(f"gr{u}_scol", [128, 32], F32)
    rows = ctx.sb(f"gr{u}_rows", [1, 1024], F32)
    st2 = ctx.sb(f"gr{u}_st2", [1, 8], F32)
    xs_sb = ctx.sb(f"gr{u}_xs", [128, 1024], F32)
    u0 = ctx.sb(f"gr{u}_u0", [128, 16], F32)
    fneg = ctx.sb(f"gr{u}_fn", [128, 2], F32)
    msk0_sb = ctx.sb(f"gr{u}_msk0", [128, 16], mybir.dt.uint8)

    d_pub = ctx.dram(f"gr{u}_dpub", [128, 32], F32)
    d_pre = ctx.dram(f"gr{u}_dpre", [128, 32], F32)
    d_st = ctx.dram(f"gr{u}_dst", [1, 2], F32)
    d_st2 = ctx.dram(f"gr{u}_dst2", [1, 2], F32)

    s_w = [ctx.sem(f"gr{u}_w0"), ctx.sem(f"gr{u}_w1")]
    s_kp = ctx.sem(f"gr{u}_kp")
    s_hp = ctx.sem(f"gr{u}_hp")
    s_t3k = ctx.sem(f"gr{u}_t3k")
    s_lcD = ctx.sem(f"gr{u}_lcD")
    s_csum = ctx.sem(f"gr{u}_cs")
    s_prow = ctx.sem(f"gr{u}_prw")
    s_asp = ctx.sem(f"gr{u}_asp")
    s_t3h = ctx.sem(f"gr{u}_t3h")
    s_p2 = ctx.sem(f"gr{u}_p2")
    s_p1 = ctx.sem(f"gr{u}_p1")
    s_u = ctx.sem(f"gr{u}_su")
    s_am = ctx.sem(f"gr{u}_am")
    s_ax = ctx.sem(f"gr{u}_ax")
    s_scan = ctx.sem(f"gr{u}_sc")
    s_u0a = ctx.sem(f"gr{u}_u0a")
    s_u0b = ctx.sem(f"gr{u}_u0b")
    s_u0c = ctx.sem(f"gr{u}_u0c")
    s_cc = ctx.sem(f"gr{u}_cc")
    s_ccop = ctx.sem(f"gr{u}_ccop")
    s_am2 = ctx.sem(f"gr{u}_am2")
    s_ax2 = ctx.sem(f"gr{u}_ax2")
    s_v1 = ctx.sem(f"gr{u}_v1")
    s_t2 = ctx.sem(f"gr{u}_t2")
    s_e1 = ctx.sem(f"gr{u}_e1")
    s_e2 = ctx.sem(f"gr{u}_e2")
    s_vp2 = ctx.sem(f"gr{u}_vp2")
    s_lnv = ctx.sem(f"gr{u}_lnv")
    s_lh = ctx.sem(f"gr{u}_lh")
    s_sq = ctx.sem(f"gr{u}_sq")
    s_st = ctx.sem(f"gr{u}_st")
    s_sr = ctx.sem(f"gr{u}_sr")
    s_sred = ctx.sem(f"gr{u}_srd")
    s_gs = ctx.sem(f"gr{u}_gs")
    s_gr = ctx.sem(f"gr{u}_grr")
    s_gb = ctx.sem(f"gr{u}_gb")
    s_mr = ctx.sem(f"gr{u}_mr")
    s_hx = ctx.sem(f"gr{u}_hx")
    s_v = ctx.sem(f"gr{u}_v")
    s_vsq = ctx.sem(f"gr{u}_vsq")
    s_vs1 = ctx.sem(f"gr{u}_vs1")
    s_vs2 = ctx.sem(f"gr{u}_vs2")
    s_rowA = ctx.sem(f"gr{u}_rwA")
    s_rowB = ctx.sem(f"gr{u}_rwB")
    s_rowC = ctx.sem(f"gr{u}_rwC")
    s_bc2 = ctx.sem(f"gr{u}_bc2")
    s_xin = [ctx.sem(f"gr{u}_xi0"), ctx.sem(f"gr{u}_xi1")]
    s_xap = ctx.sem(f"gr{u}_xa")
    s_xo = [ctx.sem(f"gr{u}_xo0"), ctx.sem(f"gr{u}_xo1")]

    pb = ctx.pb

    NCHL = cfg.get("nch1", NCH)

    def usl(buf, c, lo=0, n=None):
        n = ntok - lo if n is None else n
        return buf[:, c * ntok + lo: c * ntok + lo + n]

    def xnsl(ic, lo, n):
        return xn_sb[:, ic * ntok + lo: ic * ntok + lo + n]

    def u0c(c):
        return u0[:, c:c + 1]

    smalldma = "smalldma" in cfg.get("skip", ())
    def fsy(sy):
        for c in range(NCHL):
            for w_i, wsrc in ((0, wzT), (1, whT)):
                gi = 2 * c + w_i
                par = gi % 2
                if gi >= 2:
                    (s_kp if w_i == 0 else s_hp).wait(sy, c)
                dst = wbuf[:, par * 2048:(par + 1) * 2048].rearrange("p (i f) -> p i f", i=NCH)
                src = wsrc.rearrange("(i p) f -> p i f", p=128)[:, :, c * 128:(c + 1) * 128]
                if smalldma:
                    ins = sy.dma_start(out=dst[0:1, 0:1, 0:16], in_=src[0:1, 0:1, 0:16])
                else:
                    ins = sy.dma_start(out=dst, in_=src)
                s_w[par].inc(ins, 16)
    ctx.on("sync", fsy)

    def ft(te):
        skip = cfg.get("skip", ())
        for c in range(NCHL):
            gi = 2 * c
            s_w[gi % 2].wait(te, 16 * (gi // 2 + 1))
            if c > 0:
                s_u.wait(te, c)
            if "nok" in skip:
                ins = te.nop()
            else:
                for half in range(2):
                    for ic in range(NCH):
                        ins = te.matmul(pb[half][:, :], wbuf[:, (gi % 2) * 2048 + ic * 128:(gi % 2) * 2048 + ic * 128 + 128],
                                        xnsl(ic, half * 512, 512), start=(ic == 0), stop=(ic == NCH - 1))
            s_kp.inc(ins)
            gi = 2 * c + 1
            s_w[gi % 2].wait(te, 16 * (gi // 2 + 1))
            if "nok" in skip:
                ins = te.nop()
            else:
                for half in range(2):
                    for ic in range(NCH):
                        ins = te.matmul(pb[2 + half][:, :], wbuf[:, (gi % 2) * 2048 + ic * 128:(gi % 2) * 2048 + ic * 128 + 128],
                                        xnsl(ic, half * 512, 512), start=(ic == 0), stop=(ic == NCH - 1))
            s_hp.inc(ins)
            s_lcD.wait(te, c + 1)
            if c > 0:
                s_prow.wait(te, c + 1)
            if "noas" in skip:
                ins = te.nop()
                s_csum.inc(ins)
                s_prow.wait(te, c + 1)
                ins = te.nop()
                s_asp.inc(ins)
                continue
            for half in range(2):
                ins = te.matmul(pb[6 + half][0:1, 0:512], onesc_f[:],
                                tD[:, half * 512:half * 512 + 512], start=True, stop=True)
            s_csum.inc(ins)
            s_prow.wait(te, c + 1)
            for half in range(2):
                te.matmul(pb[4 + half][:, :], tril_f[:], tD[:, half * 512:half * 512 + 512],
                          start=True, stop=False)
                ins = te.matmul(pb[4 + half][:, :], ctx.ones_row[:], prow[0:1, half * 512:half * 512 + 512],
                                start=False, stop=True)
            s_asp.inc(ins)
    ctx.on("tensor", ft)

    def fs(se):
        if cfg.get("skip_u0"):
            s_u0a.wait(se, 1)
            return _fs_chunks(se)
        s_u0a.wait(se, 1)
        ins = se.activation(out=tF[:, 32:48], in_=tF[:, 0:16], func=AF.Ln, bias=ctx.cconst(0.5)[0:128, 0:1])
        ctx.edge("scalar", se, ins)
        ins = se.activation(out=tF[:, 48:64], in_=tF[:, 16:32], func=AF.Exp, scale=-1.0)
        ctx.edge("scalar", se, ins)
        ins = se.activation(out=tF[:, 48:64], in_=tF[:, 48:64], func=AF.Ln, bias=ctx.cconst(1.0)[0:128, 0:1])
        s_u0b.inc(ins)
        return _fs_chunks(se)

    def _fs_chunks(se):
        sk = cfg.get("skip", ())
        SWk = 16 if ("smallscalar" in sk or "smt3k" in sk) else 512
        SWh = 16 if ("smallscalar" in sk or "smt3h" in sk) else 512
        SWp = 16 if ("smallscalar" in sk or "smp1" in sk) else 512
        for c in range(NCHL):
            s_kp.wait(se, c + 1)
            if c > 0:
                s_u.wait(se, c)
            for half in range(2):
                SW = SWk
                hs = slice(half * 512, half * 512 + SW)
                ins = se.activation(out=tB[:, hs], in_=pb[half][:, 0:SW], func=AF.Abs, bias=bzc(c))
                ctx.edge("scalar", se, ins)
                ins = se.activation(out=tB[:, hs], in_=tB[:, hs], func=AF.Exp, scale=-1.0)
                ctx.edge("scalar", se, ins)
                ins = se.activation(out=tB[:, hs], in_=tB[:, hs], func=AF.Ln, bias=ctx.cconst(1.0)[0:128, 0:1])
            s_t3k.inc(ins)
            s_hp.wait(se, c + 1)
            s_lcD.wait(se, c + 1)
            for half in range(2):
                SW = SWh
                hs = slice(half * 512, half * 512 + SW)
                ins = se.activation(out=tB[:, hs], in_=pb[2 + half][:, 0:SW], func=AF.Abs, bias=bhc(c))
                ctx.edge("scalar", se, ins)
                ins = se.activation(out=tB[:, hs], in_=tB[:, hs], func=AF.Exp, scale=-1.0)
                ctx.edge("scalar", se, ins)
                ins = se.activation(out=tB[:, hs], in_=tB[:, hs], func=AF.Ln, bias=ctx.cconst(1.0)[0:128, 0:1])
            s_t3h.inc(ins)
            s_p2.wait(se, c + 1)
            for half in range(2):
                SW = SWp
                hs = slice(half * 512, half * 512 + SW)
                ins = se.activation(out=tB[:, hs], in_=pb[2 + half][:, 0:SW], func=AF.Relu, bias=bhc(c))
                ctx.edge("scalar", se, ins)
                ins = se.activation(out=tB[:, hs], in_=tB[:, hs], func=AF.Ln, bias=ctx.cconst(0.5)[0:128, 0:1])
            s_p1.inc(ins)
            s_am.wait(se, 2 * c + 1)
            if "scanexp" in cfg.get("skip", ()):
                ins = se.copy(out=tG[:, 0:1], in_=tG[:, 0:1])
                s_ax.inc(ins)
                s_am.wait(se, 2 * c + 2)
                ins = se.copy(out=tE[:, 0:1], in_=tE[:, 0:1])
                s_ax.inc(ins)
            else:
                ins = se.activation(out=tG[:, 0:1024], in_=tG[:, 0:1024], func=AF.Exp)
                s_ax.inc(ins)
                s_am.wait(se, 2 * c + 2)
                ins = se.activation(out=tE[:, 0:1024], in_=tE[:, 0:1024], func=AF.Exp)
                s_ax.inc(ins)
    ctx.on("scalar", fs)

    def fv(v):
        ins = v.memset(prow[0:1, :], 0.0)
        s_prow.inc(ins)
        if cfg.get("skip_u0"):
            ins = v.memset(u0[:], 0.0)
            ctx.edge("vector", v, ins)
            ins = v.memset(fneg[:], 0.0)
            s_u0a.inc(ins)
            s_u0b.wait(v, 0)
            ins = v.memset(tF[:, 0:96], 0.0)
            s_u0c.inc(ins)
            return _fv_chunks(v)
        h0s = cols[:, 64:80]
        ins = v.tensor_scalar_max(out=tF[:, 0:16], in0=h0s, scalar1=0.0)
        ctx.edge("vector", v, ins)
        ins = v.tensor_scalar_mul(out=tF[:, 16:32], in0=h0s, scalar1=-1.0)
        ctx.edge("vector", v, ins)
        ins = v.tensor_max(out=tF[:, 16:32], in0=tF[:, 16:32], in1=h0s)
        s_u0a.inc(ins)
        s_u0b.wait(v, 1)
        ins = v.tensor_scalar_min(out=tF[:, 64:80], in0=h0s, scalar1=0.0)
        ctx.edge("vector", v, ins)
        ins = v.tensor_sub(out=tF[:, 64:80], in0=tF[:, 64:80], in1=tF[:, 48:64])
        ctx.edge("vector", v, ins)
        ins = v.tensor_scalar(out=tF[:, 80:96], in0=h0s, scalar1=0.0, scalar2=None, op0=ALU.is_ge)
        ctx.edge("vector", v, ins)
        ins = v.tensor_sub(out=tF[:, 32:48], in0=tF[:, 32:48], in1=tF[:, 64:80])
        ctx.edge("vector", v, ins)
        ins = v.tensor_mul(out=tF[:, 32:48], in0=tF[:, 32:48], in1=tF[:, 80:96])
        ctx.edge("vector", v, ins)
        ins = v.tensor_add(out=tF[:, 0:16], in0=tF[:, 64:80], in1=tF[:, 32:48])
        ctx.edge("vector", v, ins)
        ins = v.tensor_scalar(out=u0[:], in0=tF[:, 0:16], scalar1=f0, scalar2=None, op0=ALU.mult)
        ctx.edge("vector", v, ins)
        ins = v.tensor_scalar_mul(out=fneg[:, 1:2], in0=f1, scalar1=DUM)
        ctx.edge("vector", v, ins)
        ins = v.tensor_scalar(out=u0[:], in0=u0[:], scalar1=fneg[:, 1:2], scalar2=None, op0=ALU.add)
        ctx.edge("vector", v, ins)
        ins = v.tensor_scalar_mul(out=fneg[:, 0:1], in0=f0, scalar1=DUM)
        s_u0c.inc(ins)
        return _fv_chunks(v)

    def _fv_chunks(v):
        sk = cfg.get("skip", ())
        VWa = 16 if ("smallvector" in sk or "smv_pre" in sk) else 512
        VWb = 16 if ("smallvector" in sk or "smv_te" in sk) else 512
        VWc = 16 if ("smallvector" in sk or "smv_uab" in sk) else 512
        for c in range(NCHL):
            VW = VWa; FW = 2 * VW
            s_kp.wait(v, c + 1)
            s_t3k.wait(v, c + 1)
            for half in range(2):
                hs = slice(half * 512, half * 512 + VW)
                ins = v.tensor_scalar(out=tC[:, hs], in0=pb[half][:, 0:VW], scalar1=bzc(c),
                                      scalar2=0.0, op0=ALU.add, op1=ALU.min)
            ctx.edge("vector", v, ins)
            ins = v.tensor_sub(out=tC[:, 0:FW], in0=tC[:, 0:FW], in1=tB[:, 0:FW])
            ctx.edge("vector", v, ins)
            for half in range(2):
                hs = slice(half * 512, half * 512 + VW)
                ins = v.scalar_tensor_tensor(out=tD[:, hs], in0=tC[:, hs], scalar=bzc(c),
                                             in1=pb[half][:, 0:VW], op0=ALU.subtract, op1=ALU.subtract)
            s_lcD.inc(ins)
            VW = VWb; FW = 2 * VW
            s_hp.wait(v, c + 1)
            s_t3h.wait(v, c + 1)
            for half in range(2):
                hs = slice(half * 512, half * 512 + VW)
                ins = v.tensor_scalar(out=tE[:, hs], in0=pb[2 + half][:, 0:VW], scalar1=bhc(c),
                                      scalar2=0.0, op0=ALU.add, op1=ALU.min)
            ctx.edge("vector", v, ins)
            ins = v.tensor_sub(out=tE[:, 0:FW], in0=tE[:, 0:FW], in1=tB[:, 0:FW])
            s_p2.inc(ins)
            s_p1.wait(v, c + 1)
            for half in range(2):
                hs = slice(half * 512, half * 512 + VW)
                ins = v.tensor_scalar(out=tG[:, hs], in0=pb[2 + half][:, 0:VW], scalar1=bhc(c),
                                      scalar2=0.0, op0=ALU.add, op1=ALU.is_lt)
            ctx.edge("vector", v, ins)
            ins = v.tensor_sub(out=tE[:, 0:FW], in0=tE[:, 0:FW], in1=tB[:, 0:FW])
            ctx.edge("vector", v, ins)
            ins = v.tensor_mul(out=tE[:, 0:FW], in0=tE[:, 0:FW], in1=tG[:, 0:FW])
            ctx.edge("vector", v, ins)
            ins = v.tensor_add(out=tC[:, 0:FW], in0=tC[:, 0:FW], in1=tE[:, 0:FW])
            ctx.edge("vector", v, ins)
            ins = v.tensor_add(out=tC[:, 0:FW], in0=tC[:, 0:FW], in1=tB[:, 0:FW])
            ctx.edge("vector", v, ins)
            VW = VWc
            s_asp.wait(v, c + 1)
            for half in range(2):
                ins = v.tensor_sub(out=usl(UB, c, half * 512, VW), in0=tC[:, half * 512:half * 512 + VW],
                                   in1=pb[4 + half][:, 0:VW])
            ctx.edge("vector", v, ins)
            for half in range(2):
                ins = v.tensor_copy(out=usl(AB, c, half * 512, VW), in_=pb[4 + half][:, 0:VW])
            s_u.inc(ins)
            s_csum.wait(v, c + 1)
            s_asp.wait(v, c + 1)
            v.tensor_add(out=prow[0:1, 0:VW], in0=prow[0:1, 0:VW], in1=pb[6][0:1, 0:VW])
            ins = v.tensor_add(out=prow[0:1, 512:512 + VW], in0=prow[0:1, 512:512 + VW], in1=pb[7][0:1, 0:VW])
            s_prow.inc(ins)
            if cfg.get("phases", 9) <= 0:
                s_scan.inc(ins) if False else None
                continue
            s_u.wait(v, s_u.n)
            s_u0c.wait(v, 1)
            s_ax.wait(v, 2 * c)
            if "scan" in cfg.get("skip", ()):
                ins = v.tensor_copy(out=tF[:, 0:1024], in_=usl(UB, c))
            else:
                ins = v.tensor_tensor_scan(out=tF[:, 0:1024], data0=usl(UB, c), data1=usl(UB, c),
                                           initial=u0c(c), op0=ALU.max, op1=ALU.max)
            ctx.edge("vector", v, ins)
            ins = v.tensor_sub(out=tG[:, 1:1024], in0=tF[:, 0:1023], in1=tF[:, 1:1024])
            ctx.edge("vector", v, ins)
            ins = v.tensor_scalar(out=tG[:, 0:1], in0=tF[:, 0:1], scalar1=u0c(c), scalar2=-1.0,
                                  op0=ALU.subtract, op1=ALU.mult)
            s_am.inc(ins)
            ins = v.tensor_sub(out=tE[:, 0:1024], in0=usl(UB, c), in1=tF[:, 0:1024])
            s_am.inc(ins)
            ins = v.tensor_scalar(out=pub[:, c:c + 1], in0=tF[:, 1023:1024], scalar1=f0,
                                  scalar2=None, op0=ALU.mult)
            ctx.edge("vector", v, ins)
            s_ax.wait(v, 2 * (c + 1))
            if "scan" in cfg.get("skip", ()):
                ins = v.tensor_copy(out=tF[:, 0:1024], in_=tG[:, 0:1024])
            else:
                ins = v.tensor_tensor_scan(out=tF[:, 0:1024], data0=tG[:, 0:1024], data1=tE[:, 0:1024],
                                           initial=1.0, op0=ALU.mult, op1=ALU.add)
            ctx.edge("vector", v, ins)
            ins = v.tensor_scalar(out=pub[:, 16 + c:17 + c], in0=tF[:, 1023:1024], scalar1=f0,
                                  scalar2=None, op0=ALU.mult)
            s_scan.inc(ins)
    ctx.on("vector", fv)
    ctx.barrier()
    if cfg.get("phases", 9) <= 1:
        return

    if cfg.get("dbg_u") is not None:
        sdbg = ctx.sem(f"gr{u}_dbg")
        def fdbgu(gp):
            ins = gp.dma_start(out=cfg["dbg_u"].rearrange("(c p) t -> p c t", p=128),
                               in_=UB[:].rearrange("p (c t) -> p c t", c=NCH))
            sdbg.inc(ins, 16)
            ins = gp.dma_start(out=cfg["dbg_as"].rearrange("(c p) t -> p c t", p=128),
                               in_=AB[:].rearrange("p (c t) -> p c t", c=NCH))
            sdbg.inc(ins, 16)
            gp.wait_ge(sdbg.h, sdbg.n)
        ctx.on("gpsimd", fdbgu)
        ctx.barrier()

    def fg1(gp):
        ins = gp.dma_start(out=d_pub.ap(), in_=pub[:])
        s_cc.inc(ins, 16)
        gp.wait_ge(s_cc.h, s_cc.n)
        ins = gp.collective_compute("AllReduce", ALU.add, replica_groups=pair_groups,
                                    ins=[d_pub.ap().opt()], outs=[d_pre.ap().opt()])
        s_ccop.inc(ins, 1)
        gp.wait_ge(s_ccop.h, s_ccop.n)
        ins = gp.dma_start(out=ppre[:], in_=d_pre.ap())
        s_cc.inc(ins, 16)
        gp.wait_ge(s_cc.h, s_cc.n)
    ctx.on("gpsimd", fg1)
    ctx.barrier()
    if cfg.get("phases", 9) <= 2:
        return

    def fv2(v):
        for c in range(NCH):
            ins = v.tensor_scalar(out=fneg[:, 1:2], in0=ppre[:, c:c + 1], scalar1=f1, scalar2=None, op0=ALU.mult)
            ctx.edge("vector", v, ins)
            ins = v.tensor_scalar(out=fneg[:, 1:2], in0=fneg[:, 1:2], scalar1=fneg[:, 0:1], scalar2=None, op0=ALU.add)
            ctx.edge("vector", v, ins)
            if c > 0:
                s_sq.wait(v, c)
                s_ax2.wait(v, 2 * c)
            ins = v.tensor_tensor_scan(out=tF[:, 0:1024], data0=usl(UB, c), data1=usl(UB, c),
                                       initial=u0c(c), op0=ALU.max, op1=ALU.max)
            ctx.edge("vector", v, ins)
            ins = v.tensor_sub(out=tG[:, 1:1024], in0=tF[:, 0:1023], in1=tF[:, 1:1024])
            ctx.edge("vector", v, ins)
            ins = v.tensor_scalar(out=tG[:, 0:1], in0=tF[:, 0:1], scalar1=u0c(c), scalar2=-1.0,
                                  op0=ALU.subtract, op1=ALU.mult)
            s_am2.inc(ins)
            ins = v.tensor_sub(out=tE[:, 0:1024], in0=usl(UB, c), in1=tF[:, 0:1024])
            s_am2.inc(ins)
            s_ax2.wait(v, 2 * (c + 1))
            ins = v.tensor_tensor_scan(out=tD[:, 0:1024], data0=tG[:, 0:1024], data1=tE[:, 0:1024],
                                       initial=1.0, op0=ALU.mult, op1=ALU.add)
            ctx.edge("vector", v, ins)
            ins = v.tensor_scalar(out=tG[:, 0:1024], in0=tF[:, 0:1024], scalar1=fneg[:, 1:2],
                                  scalar2=0.0, op0=ALU.subtract, op1=ALU.min)
            ctx.edge("vector", v, ins)
            ins = v.tensor_sub(out=tE[:, 0:1024], in0=tF[:, 0:1024], in1=tG[:, 0:1024])
            s_t2.inc(ins)
            s_e1.wait(v, c + 1)
            ins = v.tensor_mul(out=tD[:, 0:1024], in0=tD[:, 0:1024], in1=tG[:, 0:1024])
            s_v1.inc(ins)
            s_e2.wait(v, c + 1)
            ins = v.tensor_scalar(out=fneg[:, 1:2], in0=ppre[:, 16 + c:17 + c], scalar1=f1, scalar2=None, op0=ALU.mult)
            ctx.edge("vector", v, ins)
            ins = v.scalar_tensor_tensor(out=tD[:, 0:1024], in0=tB[:, 0:1024], scalar=fneg[:, 1:2],
                                         in1=tD[:, 0:1024], op0=ALU.mult, op1=ALU.add)
            s_vp2.inc(ins)
            s_lnv.wait(v, c + 1)
            ins = v.tensor_add(out=tE[:, 0:1024], in0=tE[:, 0:1024], in1=tD[:, 0:1024])
            ctx.edge("vector", v, ins)
            ins = v.tensor_add(out=usl(UB, c), in0=usl(AB, c), in1=tE[:, 0:1024])
            s_lh.inc(ins)
            s_lh.wait(v, s_lh.n)
            ins = v.reduce_sum(out=scol[:, c:c + 1], in_=usl(UB, c), axis=AX.X)
            ctx.edge("vector", v, ins)
            s_sq.wait(v, c + 1)
            ins = v.reduce_sum(out=scol[:, 16 + c:17 + c], in_=tE[:, 0:1024], axis=AX.X)
            s_st.inc(ins)
        s_st.wait(v, s_st.n)
        ins = v.scalar_tensor_tensor(out=scol[:, 0:16], in0=u0[:], scalar=f0, in1=scol[:, 0:16],
                                     op0=ALU.mult, op1=ALU.add)
        ctx.edge("vector", v, ins)
        ins = v.tensor_mul(out=tB[:, 0:16], in0=u0[:], in1=u0[:])
        ctx.edge("vector", v, ins)
        ins = v.scalar_tensor_tensor(out=scol[:, 16:32], in0=tB[:, 0:16], scalar=f0, in1=scol[:, 16:32],
                                     op0=ALU.mult, op1=ALU.add)
        s_st.inc(ins)
    ctx.on("vector", fv2)

    def fs3(se):
        for c in range(NCH):
            s_am2.wait(se, 2 * c + 1)
            ins = se.activation(out=tG[:, 0:1024], in_=tG[:, 0:1024], func=AF.Exp)
            s_ax2.inc(ins)
            s_am2.wait(se, 2 * c + 2)
            ins = se.activation(out=tE[:, 0:1024], in_=tE[:, 0:1024], func=AF.Exp)
            s_ax2.inc(ins)
            s_t2.wait(se, c + 1)
            ins = se.activation(out=tG[:, 0:1024], in_=tG[:, 0:1024], func=AF.Exp)
            s_e1.inc(ins)
            s_v1.wait(se, c + 1)
            ins = se.activation(out=tB[:, 0:1024], in_=tE[:, 0:1024], func=AF.Exp, scale=-1.0,
                                bias=fneg[:, 1:2])
            s_e2.inc(ins)
            s_vp2.wait(se, c + 1)
            ins = se.activation(out=tD[:, 0:1024], in_=tD[:, 0:1024], func=AF.Ln)
            s_lnv.inc(ins)
            s_lh.wait(se, c + 1)
            ins = se.activation(out=tE[:, 0:1024], in_=usl(UB, c), func=AF.Square)
            s_sq.inc(ins)
    ctx.on("scalar", fs3)
    ctx.barrier()
    if cfg.get("phases", 9) <= 3:
        return

    def ft2(te):
        ins = te.matmul(pb[0][0:1, 0:32], onesc_f[:], scol[:], start=True, stop=True)
        s_sr.inc(ins)
    ctx.on("tensor", ft2)

    def fv3(v):
        s_sr.wait(v, 1)
        ins = v.reduce_sum(out=st2[0:1, 0:1], in_=pb[0][0:1, 0:16], axis=AX.X)
        ctx.edge("vector", v, ins)
        ins = v.reduce_sum(out=st2[0:1, 1:2], in_=pb[0][0:1, 16:32], axis=AX.X)
        s_sred.inc(ins)
    ctx.on("vector", fv3)

    def fg2(gp):
        gp.wait_ge(s_sred.h, 1)
        ins = gp.dma_start(out=d_st.ap(), in_=st2[0:1, 0:2])
        s_cc.inc(ins, 16)
        gp.wait_ge(s_cc.h, s_cc.n)
        ins = gp.collective_compute("AllReduce", ALU.add, replica_groups=all_groups,
                                    ins=[d_st.ap().opt()], outs=[d_st2.ap().opt()])
        s_ccop.inc(ins, 1)
        gp.wait_ge(s_ccop.h, s_ccop.n)
        ins = gp.dma_start(out=st2[0:1, 2:4], in_=d_st2.ap())
        s_cc.inc(ins, 16)
        gp.wait_ge(s_cc.h, s_cc.n)
    ctx.on("gpsimd", fg2)
    ctx.barrier()

    def fv4(v):
        ins = v.tensor_scalar_mul(out=st2[0:1, 4:5], in0=st2[0:1, 2:3], scalar1=1.0 / NSTAT)
        ctx.edge("vector", v, ins)
        ins = v.tensor_mul(out=st2[0:1, 5:6], in0=st2[0:1, 4:5], in1=st2[0:1, 2:3])
        ctx.edge("vector", v, ins)
        ins = v.tensor_sub(out=st2[0:1, 5:6], in0=st2[0:1, 3:4], in1=st2[0:1, 5:6])
        ctx.edge("vector", v, ins)
        ins = v.tensor_scalar_mul(out=st2[0:1, 5:6], in0=st2[0:1, 5:6], scalar1=1.0 / (NSTAT - 1.0))
        s_gs.inc(ins)
    ctx.on("vector", fv4)

    def fs4(se):
        s_gs.wait(se, 1)
        ins = se.activation(out=st2[0:1, 6:7], in_=st2[0:1, 5:6], func=AF.Ln)
        ctx.edge("scalar", se, ins)
        ins = se.activation(out=st2[0:1, 6:7], in_=st2[0:1, 6:7], func=AF.Exp, scale=-0.5)
        s_gr.inc(ins)
    ctx.on("scalar", fs4)

    def ft3(te):
        s_gr.wait(te, 1)
        te.matmul(pb[1][:, 0:1], ctx.ones_row[:], st2[0:1, 4:5], start=True, stop=True,
                  skip_group_check=True)
        ins = te.matmul(pb[1][:, 1:2], ctx.ones_row[:], st2[0:1, 6:7], start=True, stop=True,
                        skip_group_check=True)
        s_gb.inc(ins)
    ctx.on("tensor", ft3)

    def fv5(v):
        s_gb.wait(v, 1)
        ins = v.tensor_copy(out=tF[:, 0:2], in_=pb[1][:, 0:2])
        ctx.edge("vector", v, ins)
        ins = v.tensor_mul(out=tF[:, 2:3], in0=tF[:, 0:1], in1=tF[:, 1:2])
        ctx.edge("vector", v, ins)
        ins = v.tensor_scalar_mul(out=tF[:, 2:3], in0=tF[:, 2:3], scalar1=-1.0)
        s_mr.inc(ins)
    ctx.on("vector", fv5)
    ctx.barrier()
    if cfg.get("phases", 9) <= 4:
        return

    def fs5(se):
        for c in range(NCH):
            if c > 0:
                s_v.wait(se, c)
            ins = se.activation(out=tB[:, 0:1024], in_=usl(UB, c), func=AF.Exp,
                                scale=tF[:, 1:2], bias=tF[:, 2:3])
            s_hx.inc(ins)
            s_v.wait(se, c + 1)
            if c > 0:
                s_vs2.wait(se, c)
            ins = se.activation(out=tC[:, 0:1024], in_=usl(AB, c), func=AF.Square)
            s_vsq.inc(ins)
    ctx.on("scalar", fs5)

    def fv6(v):
        for c in range(NCH):
            s_hx.wait(v, c + 1)
            ins = v.tensor_add(out=usl(AB, c), in0=tB[:, 0:1024],
                               in1=xn_sb[:, c * ntok:(c + 1) * ntok])
            s_v.inc(ins)
    ctx.on("vector", fv6)

    def ft4(te):
        for c in range(NCH):
            s_v.wait(te, c + 1)
            for half in range(2):
                ins = te.matmul(pb[2 + half][0:1, 0:512], onesc_f[:],
                                usl(AB, c, half * 512, 512), start=(c == 0), stop=(c == NCH - 1))
            s_vs1.inc(ins)
            s_vsq.wait(te, c + 1)
            for half in range(2):
                ins = te.matmul(pb[6 + half][0:1, 0:512], onesc_f[:],
                                tC[:, half * 512:half * 512 + 512], start=(c == 0), stop=(c == NCH - 1))
            s_vs2.inc(ins)
        s_rowC.wait(te, 1)
        te.matmul(pb[4][:, :], ctx.ones_row[:], rows[0:1, 0:512], start=True, stop=True,
                  skip_group_check=True)
        ins = te.matmul(pb[5][:, :], ctx.ones_row[:], rows[0:1, 512:1024], start=True, stop=True,
                        skip_group_check=True)
        te.matmul(pb[0][:, :], ctx.ones_row[:], prow[0:1, 0:512], start=True, stop=True,
                  skip_group_check=True)
        ins = te.matmul(pb[1][:, :], ctx.ones_row[:], prow[0:1, 512:1024], start=True, stop=True,
                        skip_group_check=True)
        s_bc2.inc(ins)
    ctx.on("tensor", ft4)

    def fs6(se):
        s_vs1.wait(se, NCH)
        se.activation(out=rows[0:1, 0:512], in_=pb[2][0:1, 0:512], func=AF.Copy, scale=1.0 / D)
        ins = se.activation(out=rows[0:1, 512:1024], in_=pb[3][0:1, 0:512], func=AF.Copy, scale=1.0 / D)
        s_vs2.wait(se, NCH)
        se.activation(out=prow[0:1, 0:512], in_=pb[6][0:1, 0:512], func=AF.Copy, scale=1.0 / D)
        ins = se.activation(out=prow[0:1, 512:1024], in_=pb[7][0:1, 0:512], func=AF.Copy, scale=1.0 / D)
        s_rowA.inc(ins)
    ctx.on("scalar", fs6)

    def fv8(v):
        s_rowA.wait(v, 1)
        ins = v.tensor_mul(out=tB[0:1, 0:1024], in0=rows[0:1, 0:1024], in1=rows[0:1, 0:1024])
        ctx.edge("vector", v, ins)
        ins = v.tensor_sub(out=prow[0:1, 0:1024], in0=prow[0:1, 0:1024], in1=tB[0:1, 0:1024])
        s_rowB.inc(ins)
    ctx.on("vector", fv8)

    def fs7(se):
        s_rowB.wait(se, 1)
        ins = se.activation(out=prow[0:1, 0:1024], in_=prow[0:1, 0:1024], func=AF.Ln,
                            bias=ctx.cconst(LN_EPS)[0:1, 0:1])
        ctx.edge("scalar", se, ins)
        ins = se.activation(out=prow[0:1, 0:1024], in_=prow[0:1, 0:1024], func=AF.Exp, scale=-0.5)
        s_rowC.inc(ins)
    ctx.on("scalar", fs7)

    def fg3(gp):
        for c in range(NCH):
            if c >= 1:
                gp.wait_ge(s_xo[0].h, 16 * c)
            ins = x_load(gp, c, xs_sb[:, 0:1024])
            s_xin[0].inc(ins, 16)
            gp.wait_ge(s_xap.h, c + 1)
            ins = x_store(gp, c, xs_sb[:, 0:1024])
            s_xo[0].inc(ins, 16)
        gp.wait_ge(s_xo[0].h, s_xo[0].n)
    ctx.on("gpsimd", fg3)

    def fv9(v):
        s_bc2.wait(v, 1)
        for c in range(NCH):
            s_xin[0].wait(v, 16 * (c + 1))
            lnw_c = cols[:, 32 + c:33 + c]
            lnb_c = cols[:, 48 + c:49 + c]
            for half in range(2):
                hs = slice(half * 512, half * 512 + 512)
                ins = v.tensor_sub(out=tB[:, hs], in0=usl(AB, c, half * 512, 512),
                                   in1=pb[4 + half][:, :])
            ctx.edge("vector", v, ins)
            for half in range(2):
                hs = slice(half * 512, half * 512 + 512)
                ins = v.tensor_mul(out=tB[:, hs], in0=tB[:, hs], in1=pb[0 + half][:, :])
            ctx.edge("vector", v, ins)
            ins = v.tensor_scalar(out=tB[:, 0:1024], in0=tB[:, 0:1024], scalar1=lnw_c, scalar2=lnb_c,
                                  op0=ALU.mult, op1=ALU.add)
            ctx.edge("vector", v, ins)
            ins = v.tensor_add(out=xs_sb[:, 0:1024], in0=xs_sb[:, 0:1024], in1=tB[:, 0:1024])
            s_xap.inc(ins)
    ctx.on("vector", fv9)
import ml_dtypes

BF16NP = ml_dtypes.bfloat16

_PROGS = {}

_CASTS = [
    ("l0a_wqT", 2048, 2048, "f4"), ("l0a_wkT", 2048, 2048, "f4"),
    ("l0a_wvT", 2048, 2048, "f4"), ("l0a_woT", 2048, 2048, "f4"),
    ("l0b_pack", 2048, 4096, "f4"), ("l0b_woT", 2048, 2048, "f4"),
    ("f0_wg", 2048, 5632, "f4"), ("f0_wu", 2048, 5632, "f4"),
    ("f0_wd", 5632, 2048, "f4"), ("f0_wr", 2048, 8, "f4"),
    ("f0_agu", 2048, 256, "f4"), ("f0_bgu", 16, 2 * 44 * 2048, "f4"),
    ("f0_adt", 128, 44 * 128, "f4"), ("f0_bdt", 16, 8 * 2048, "f4"),
    ("l1a_wqT", 2048, 2048, "f4"), ("l1a_wkT", 2048, 512, "f4"),
    ("l1a_wvT", 2048, 512, "f4"),
    ("f1_wg", 2048, 8192, "i1"), ("f1_wu", 2048, 8192, "i1"),
    ("f1_wd", 8192, 2048, "i1"),
    ("b_wzT", 2048, 2048, "f4"), ("b_whT", 2048, 2048, "f4"),
]
CTILE = 4096


def build_prep():
    ctx = Ctx()
    d_in = {}
    d_out = {}
    tiles = []
    for nm, R, C, dt in _CASTS:
        m = R * C // NCORES // 128
        d_in[nm] = ctx.inp(nm + "_in", [128, m], I8 if dt == "i1" else F32)
        d_out[nm] = ctx.outp(nm + "_bf", [128, m], BF16)
        off = 0
        while off < m:
            w = min(CTILE, m - off)
            tiles.append((nm, dt, off, w))
            off += w
    d_x0 = ctx.inp("x0", [D, TC], F32)
    d_xn0 = ctx.outp("xn0", [D, TC], BF16)
    x_sb = ctx.sb("x_sb", [128, NCH * TC], F32)
    xn_sb = ctx.sb("xn_sb", [128, NCH * TC], BF16)
    ibufs = [ctx.sb(f"cif{p}", [128, CTILE], F32) for p in range(2)]
    i8bufs = [ctx.sb(f"ci8{p}", [128, CTILE], I8) for p in range(2)]
    obufs = [ctx.sb(f"cob{p}", [128, CTILE], BF16) for p in range(2)]
    ld = ctx.sem("ld")
    st = ctx.sem("st")
    s_ci = ctx.sem("cast_i")
    s_cc = ctx.sem("cast_c")
    s_co = ctx.sem("cast_o")
    with ctx.nc.Block() as block:
        ctx.block = block

        def fload(sy):
            ins = sy.dma_start(out=x_sb[:].rearrange("p (c n) -> p c n", c=NCH),
                               in_=d_x0.rearrange("(c p) n -> p c n", p=128))
            ld.inc(ins, 16)
        ctx.on("sync", fload)
        ctx.cconst(LN_EPS)
        ctx.setup_common()

        def fw(v):
            ld.wait_all(v)
        ctx.on("vector", fw)
        ctx.barrier()
        emit_ln(ctx, x_sb, xn_sb, TC)
        ctx.barrier()

        def fxn(g):
            ins = g.dma_start(out=d_xn0.rearrange("(c p) n -> p c n", p=128),
                              in_=xn_sb[:].rearrange("p (c n) -> p c n", c=NCH))
            st.inc(ins, 16)
        ctx.on("gpsimd", fxn)

        def fci(sy):
            for k, (nm, dt, off, w) in enumerate(tiles):
                par = k % 2
                if k >= 2:
                    sy.wait_ge(s_cc.h, k - 1)
                buf = (i8bufs if dt == "i1" else ibufs)[par]
                ins = sy.dma_start(out=buf[:, 0:w], in_=d_in[nm][:, off:off + w])
                s_ci.inc(ins, 16)
        ctx.on("sync", fci)

        def fcc(v):
            for k, (nm, dt, off, w) in enumerate(tiles):
                par = k % 2
                s_ci.wait(v, 16 * (k + 1))
                if k >= 2:
                    s_co.wait(v, 16 * (k - 1))
                buf = (i8bufs if dt == "i1" else ibufs)[par]
                ins = v.tensor_copy(out=obufs[par][:, 0:w], in_=buf[:, 0:w])
                s_cc.inc(ins)
        ctx.on("vector", fcc)

        def fco(g):
            for k, (nm, dt, off, w) in enumerate(tiles):
                g.wait_ge(s_cc.h, k + 1)
                ins = g.dma_start(out=d_out[nm][:, off:off + w], in_=obufs[k % 2][:, 0:w])
                s_co.inc(ins, 16)
            g.wait_ge(s_co.h, s_co.n)
            g.wait_ge(st.h, 16)
        ctx.on("gpsimd", fco)
    return ctx.nc


def build_attn(kind):
    if kind == "l0a":
        NHc, NKVc, grp, pad, window, oproj, bias, rope = 8, 8, 1, 0, None, True, True, False
    elif kind == "l0b":
        NHc, NKVc, grp, pad, window, oproj, bias, rope = 8, 4, 2, 1, 1024, True, False, True
    else:
        NHc, NKVc, grp, pad, window, oproj, bias, rope = 8, 2, 4, 0, None, False, True, False
    DVC = NKVc * 128
    ctx = Ctx()
    Tp = T + pad
    d_xn0 = ctx.inp("xn_h0", [D, TC], BF16)
    d_xn1 = ctx.inp("xn_h1", [D, TC], BF16)
    d_wqT = ctx.inp("wqT", [D, NHc * 128], BF16)
    d_wkT = ctx.inp("wkT", [D, DVC], BF16)
    d_wvT = ctx.inp("wvT", [D, DVC], BF16)
    d_woT = ctx.inp("woT", [NHc * 128, D], BF16) if oproj else None
    if bias:
        d_bq = ctx.inp("bqc", [128, NHc])
        d_bk = ctx.inp("bkc", [128, NKVc])
        d_bv = ctx.inp("bvbc", [128, DVC])
    d_msk = ctx.inp("masks", [128, 4 * 512], BF16)
    if window:
        d_mskl = ctx.inp("masks_lo", [128, 4 * 512], BF16)
        d_c2 = ctx.inp("C2", [128, T], BF16)
        d_s2 = ctx.inp("S2", [128, T], BF16)
        d_ck = ctx.inp("convk", [128, 2 * NKVc])
        d_w0v = ctx.inp("w0v", [128, DVC])
        d_w1v = ctx.inp("w1v", [128, DVC])
    if oproj:
        d_y = ctx.outp("y_part", [D, T], BF16)
    else:
        d_o = ctx.outp("o_part", [NHc * 128, T], BF16)

    xn_sb = ctx.sb("xn_sb", [128, NCH * Tp], BF16)
    mk_sb = ctx.sb("mk_sb", [128, 4 * 512], BF16)
    if bias:
        bq_sb = ctx.sb("bq_sb", [128, NHc], F32)
        bk_sb = ctx.sb("bk_sb", [128, NKVc], F32)
        bv_sb = ctx.sb("bv_sb", [128, DVC], F32)
    if window:
        mkl_sb = ctx.sb("mkl_sb", [128, 4 * 512], BF16)
        c2_sb = ctx.sb("c2_sb", [128, T], BF16)
        s2_sb = ctx.sb("s2_sb", [128, T], BF16)
        ck_sb = ctx.sb("ck_sb", [128, 2 * NKVc], F32)
        w0v_sb = ctx.sb("w0v_sb", [128, DVC], F32)
        w1v_sb = ctx.sb("w1v_sb", [128, DVC], F32)
    ld = ctx.sem("ld0")
    ms = ctx.sem("ms0")
    st = ctx.sem("st0")
    with ctx.nc.Block() as block:
        ctx.block = block
        if pad:
            def fz(v):
                ins = v.memset(xn_sb[:], 0.0)
                ms.inc(ins)
            ctx.on("vector", fz)

        def fload(sy):
            if pad:
                ms.wait(sy, 1)
            xv = xn_sb[:].rearrange("p (c t) -> p c t", c=NCH)
            pairs = [
                (xv[:, :, pad:TC + pad], d_xn0.rearrange("(c p) t -> p c t", p=128)),
                (xv[:, :, TC + pad:Tp], d_xn1.rearrange("(c p) t -> p c t", p=128)),
                (mk_sb[:], d_msk)]
            if bias:
                pairs += [(bq_sb[:], d_bq), (bk_sb[:], d_bk), (bv_sb[:], d_bv)]
            if window:
                pairs += [(mkl_sb[:], d_mskl), (c2_sb[:], d_c2), (s2_sb[:], d_s2),
                          (ck_sb[:], d_ck), (w0v_sb[:], d_w0v), (w1v_sb[:], d_w1v)]
            for dst, src in pairs:
                ins = sy.dma_start(out=dst, in_=src)
                ld.inc(ins, 16)
        ctx.on("sync", fload)
        ctx.setup_common()

        def fw(v):
            ld.wait_all(v)
        ctx.on("vector", fw)
        ctx.barrier()
        cfg = dict(T=T, NHc=NHc, NKVc=NKVc, HD=128, group=grp, pad=pad,
                   xn_sb=xn_sb, wqT=d_wqT, wkT=d_wkT, wvT=d_wvT,
                   woT=d_woT,
                   bq_cols=bq_sb if bias else None,
                   bk_cols=bk_sb if bias else None,
                   bv_bc=bv_sb if bias else None,
                   masks=mk_sb)
        if window:
            cfg.update(window=window, masks_lo=mkl_sb, rope=(c2_sb, s2_sb),
                       convk_cols=ck_sb, convv=(w0v_sb, w1v_sb))
        if oproj:
            cfg["y_out"] = d_y
        o_sb = emit_attention(ctx, cfg)
        ctx.barrier()
        if not oproj:
            def fstore(g):
                ins = g.dma_start(out=d_o.rearrange("(h p) t -> p h t", p=128),
                                  in_=o_sb.rearrange("p (h t) -> p h t", h=NHc))
                st.inc(ins, 16)
                st.wait_all(g)
            ctx.on("gpsimd", fstore)
    return ctx.nc


def build_ffn(which):
    ctx = Ctx()
    d_x = ctx.inp("x", [D, TC])
    d_pa = ctx.inp("pLO", [1024, TC], BF16)
    d_pb = ctx.inp("pHI", [1024, TC], BF16)
    d_wg = ctx.inp("wg", [D, 8192], BF16)
    d_wu = ctx.inp("wu", [D, 8192], BF16)
    d_wd = ctx.inp("wd", [8192, D], BF16)
    d_sg = ctx.inp("sg", [128, 64])
    d_su = ctx.inp("su", [128, 64])
    d_sd = ctx.inp("sd", [128, NCH])
    d_xo = ctx.outp("xout", [D, TC])
    d_xn = ctx.outp("xnout", [D, TC], BF16)

    x_sb = ctx.sb("x_sb", [128, NCH * TC], F32)
    xn_sb = ctx.sb("xn_sb", [128, NCH * TC], BF16)
    sg_sb = ctx.sb("sg_sb", [128, 64], F32)
    su_sb = ctx.sb("su_sb", [128, 64], F32)
    sd_sb = ctx.sb("sd_sb", [128, NCH], F32)
    ld = ctx.sem("ld0")
    sa = ctx.sem("sa0")
    st = ctx.sem("st0")
    with ctx.nc.Block() as block:
        ctx.block = block

        def fload(sy):
            pairs = [(x_sb[:].rearrange("p (c t) -> p c t", c=NCH),
                      d_x.rearrange("(c p) t -> p c t", p=128)),
                     (xn_sb[:, 0:8 * TC].rearrange("p (c t) -> p c t", c=8),
                      d_pa.rearrange("(c p) t -> p c t", p=128)),
                     (xn_sb[:, 8 * TC:16 * TC].rearrange("p (c t) -> p c t", c=8),
                      d_pb.rearrange("(c p) t -> p c t", p=128)),
                     (sg_sb[:], d_sg), (su_sb[:], d_su), (sd_sb[:], d_sd)]
            for dst, src in pairs:
                ins = sy.dma_start(out=dst, in_=src)
                ld.inc(ins, 16)
        ctx.on("sync", fload)
        ctx.cconst(LN_EPS)
        ctx.setup_common()

        def fadd1(v):
            ld.wait_all(v)
            ins = v.tensor_add(out=x_sb[:], in0=x_sb[:], in1=xn_sb[:])
            sa.inc(ins)
        ctx.on("vector", fadd1)
        ctx.barrier()
        emit_ln(ctx, x_sb, xn_sb, TC)
        ctx.barrier()
        emit_l1ffn(ctx, xn_sb[:], x_sb[:], TC, dict(
            wg=d_wg, wu=d_wu, wd=d_wd, sg_sb=sg_sb, su_sb=su_sb, sd_sb=sd_sb))
        ctx.barrier()
        emit_ln(ctx, x_sb, xn_sb, TC)
        ctx.barrier()

        def fstore(g):
            for dst, src in [
                    (d_xo.rearrange("(c p) t -> p c t", p=128),
                     x_sb[:].rearrange("p (c t) -> p c t", c=NCH)),
                    (d_xn.rearrange("(c p) t -> p c t", p=128),
                     xn_sb[:].rearrange("p (c t) -> p c t", c=NCH))]:
                ins = g.dma_start(out=dst, in_=src)
                st.inc(ins, 16)
            st.wait_all(g)
        ctx.on("gpsimd", fstore)
    return ctx.nc


def build_lnp():
    ctx = Ctx()
    d_x = ctx.inp("x", [D, TC])
    d_pa = ctx.inp("yA", [D, TC], BF16)
    d_pb = ctx.inp("yB", [D, TC], BF16)
    d_bo = ctx.inp("bo_cols", [128, NCH])
    d_xo = ctx.outp("xout", [D, TC])
    d_xn = ctx.outp("xnout", [D, TC], BF16)
    x_sb = ctx.sb("x_sb", [128, NCH * TC], F32)
    xn_sb = ctx.sb("xn_sb", [128, NCH * TC], BF16)
    bo_sb = ctx.sb("bo_sb", [128, NCH], F32)
    ld = ctx.sem("ld0")
    ld2 = ctx.sem("ld2")
    sa = ctx.sem("sa0")
    st = ctx.sem("st0")
    with ctx.nc.Block() as block:
        ctx.block = block

        def fload(sy):
            for dst, src in [(x_sb[:].rearrange("p (c t) -> p c t", c=NCH),
                              d_x.rearrange("(c p) t -> p c t", p=128)),
                             (xn_sb[:].rearrange("p (c t) -> p c t", c=NCH),
                              d_pa.rearrange("(c p) t -> p c t", p=128)),
                             (bo_sb[:], d_bo)]:
                ins = sy.dma_start(out=dst, in_=src)
                ld.inc(ins, 16)
        ctx.on("sync", fload)
        ctx.cconst(LN_EPS)
        ctx.setup_common()

        def fadd1(v):
            ld.wait_all(v)
            ins = v.tensor_add(out=x_sb[:], in0=x_sb[:], in1=xn_sb[:])
            sa.inc(ins)
        ctx.on("vector", fadd1)

        def fload2(sy):
            sa.wait(sy, 1)
            ins = sy.dma_start(out=xn_sb[:].rearrange("p (c t) -> p c t", c=NCH),
                               in_=d_pb.rearrange("(c p) t -> p c t", p=128))
            ld2.inc(ins, 16)
        ctx.on("sync", fload2)

        def fadd2(v):
            ld2.wait_all(v)
            ins = v.tensor_add(out=x_sb[:], in0=x_sb[:], in1=xn_sb[:])
            ctx.edge("vector", v, ins)
            for c in range(NCH):
                ins = v.tensor_scalar(out=x_sb[:, c * TC:(c + 1) * TC],
                                      in0=x_sb[:, c * TC:(c + 1) * TC],
                                      scalar1=bo_sb[:, c:c + 1], scalar2=None, op0=ALU.add)
                ctx.edge("vector", v, ins)
        ctx.on("vector", fadd2)
        ctx.barrier()
        emit_ln(ctx, x_sb, xn_sb, TC)
        ctx.barrier()

        def fstore(g):
            for dst, src in [
                    (d_xo.rearrange("(c p) t -> p c t", p=128),
                     x_sb[:].rearrange("p (c t) -> p c t", c=NCH)),
                    (d_xn.rearrange("(c p) t -> p c t", p=128),
                     xn_sb[:].rearrange("p (c t) -> p c t", c=NCH))]:
                ins = g.dma_start(out=dst, in_=src)
                st.inc(ins, 16)
            st.wait_all(g)
        ctx.on("gpsimd", fstore)
    return ctx.nc


def build_moe():
    ctx = Ctx()
    d_x = ctx.inp("x", [D, TC])
    d_xnin = ctx.inp("xn", [D, TC], BF16)
    d_wg = ctx.inp("wg", [D, 5632], BF16)
    d_wu = ctx.inp("wu", [D, 5632], BF16)
    d_wd = ctx.inp("wd", [5632, D], BF16)
    d_wr = ctx.inp("wr", [D, 8], BF16)
    d_agu = ctx.inp("agu", [D, 256], BF16)
    d_bgu = ctx.inp("bgu", [16, 2 * 44 * 2048], BF16)
    d_adt = ctx.inp("adt", [128, 44 * 128], BF16)
    d_bdt = ctx.inp("bdt", [16, 8 * 2048], BF16)
    d_id = ctx.inp("identb", [128, 128], BF16)
    d_oh8 = ctx.inp("oh8", [8, 128], BF16)
    d_xo = ctx.outp("xout", [D, TC])
    x_sb = ctx.sb("x_sb", [128, NCH * TC], F32)
    xn_sb = ctx.sb("xn_sb", [128, NCH * TC], BF16)
    id_sb = ctx.sb("id_sb", [128, 128], BF16)
    oh8_sb = ctx.sb("oh8_sb", [8, 128], BF16)
    ld = ctx.sem("ld0")
    st = ctx.sem("st0")
    with ctx.nc.Block() as block:
        ctx.block = block

        def fload(sy):
            for dst, src in [
                    (x_sb[:].rearrange("p (c t) -> p c t", c=NCH),
                     d_x.rearrange("(c p) t -> p c t", p=128)),
                    (xn_sb[:].rearrange("p (c t) -> p c t", c=NCH),
                     d_xnin.rearrange("(c p) t -> p c t", p=128)),
                    (id_sb[:], d_id), (oh8_sb[:], d_oh8)]:
                ins = sy.dma_start(out=dst, in_=src)
                ld.inc(ins, 16)
        ctx.on("sync", fload)
        ctx.setup_common()

        def fw(v):
            ld.wait_all(v)
        ctx.on("vector", fw)
        ctx.barrier()
        emit_l0ffn(ctx, xn_sb[:], x_sb[:], TC, dict(
            wg=d_wg, wu=d_wu, wd=d_wd, wr=d_wr, agu=d_agu, bgu=d_bgu,
            adt=d_adt, bdt=d_bdt, ident_b=id_sb[:], oh8=oh8_sb[:]))
        ctx.barrier()

        def fstore(g):
            ins = g.dma_start(out=d_xo.rearrange("(c p) t -> p c t", p=128),
                              in_=x_sb[:].rearrange("p (c t) -> p c t", c=NCH))
            st.inc(ins, 16)
            st.wait_all(g)
        ctx.on("gpsimd", fstore)
    return ctx.nc


def build_l1b():
    ctx = Ctx()
    d_x = ctx.inp("x", [D, TC])
    d_xn = ctx.inp("xn", [D, TC], BF16)
    d_wz = ctx.inp("wzT", [D, D], BF16)
    d_wh = ctx.inp("whT", [D, D], BF16)
    d_cols = ctx.inp("colsin", [128, 96])
    d_tril = ctx.inp("tril", [128, 128])
    d_out = ctx.outp("xout", [D, TC])
    xn_sb = ctx.sb("xn_sb", [128, NCH * TC], BF16)
    cols_sb = ctx.sb("cols_sb", [128, 96], F32)
    tril_sb = ctx.sb("tril_sb", [128, 128], F32)
    onescf = ctx.sb("onescf", [128, 1], F32)
    ld = ctx.sem("ld0")
    with ctx.nc.Block() as block:
        ctx.block = block

        def fload(sy):
            for dst, src in [
                    (xn_sb[:].rearrange("p (c t) -> p c t", c=NCH),
                     d_xn.rearrange("(c p) t -> p c t", p=128)),
                    (cols_sb[:], d_cols), (tril_sb[:], d_tril)]:
                ins = sy.dma_start(out=dst, in_=src)
                ld.inc(ins, 16)
        ctx.on("sync", fload)

        def fones(v):
            v.memset(onescf[:], 1.0)
        ctx.on("vector", fones)
        ctx.setup_common()

        def fw(v):
            ld.wait_all(v)
        ctx.on("vector", fw)
        ctx.barrier()

        def x_load(gp, c, buf):
            return gp.dma_start(out=buf, in_=d_x[c * 128:(c + 1) * 128, :])

        def x_store(gp, c, buf):
            return gp.dma_start(out=d_out[c * 128:(c + 1) * 128, :], in_=buf)
        emit_l1attnb(ctx, xn_sb[:], TC, dict(
            wzT=d_wz, whT=d_wh, cols_sb=cols_sb[:], tril_f=tril_sb[:], onesc_f=onescf[:],
            pair_groups=[[2 * i, 2 * i + 1] for i in range(4)],
            all_groups=[list(range(NCORES))],
            x_load=x_load, x_store=x_store,
            nstat=4.0 * 2049.0 * 2048.0))
        ctx.barrier()
    return ctx.nc


def _A(a):
    return np.ascontiguousarray(a)


def _consts():
    kp = np.arange(128)[:, None]
    qf = np.arange(512)[None, :]
    masks = np.stack([((128 * r + kp) <= qf) for r in range(4)], 1).astype(np.float32)
    masks_lo = np.stack([((128 * r + kp) >= qf + 1) for r in range(4)], 1).astype(np.float32)
    inv_freq = 1.0 / (100000.0 ** (np.arange(0, 128, 2, dtype=np.float32) / 128.0))
    fr = np.outer(np.arange(T, dtype=np.float32), inv_freq)
    cos, sin = np.cos(fr), np.sin(fr)
    return dict(
        masks=masks.reshape(128, -1).astype(BF16NP),
        masks_lo=masks_lo.reshape(128, -1).astype(BF16NP),
        C2=np.concatenate([cos.T, cos.T], 0).astype(BF16NP),
        S2=np.concatenate([-sin.T, sin.T], 0).astype(BF16NP),
        identb=np.eye(128, dtype=np.float32).astype(BF16NP),
        oh8=np.kron(np.eye(8, dtype=np.float32), np.ones((1, 16), np.float32)).astype(BF16NP),
        tril=np.triu(np.ones((128, 128), np.float32)),
        zero_bo=np.zeros((128, NCH), np.float32),
        zero_p=np.zeros((1024, TC), BF16NP),
    )


def kernel(**inputs):
    inp = {k: np.asarray(v) for k, v in inputs.items()}
    X = inp["L0_attn_a_x"].astype(np.float32, copy=False)

    E, R_ = 8, 16
    Ag, Bg = inp["L0_ffn_Ag"], inp["L0_ffn_Bg"]
    Au, Bu = inp["L0_ffn_Au"], inp["L0_ffn_Bu"]
    Ad_, Bd = inp["L0_ffn_Ad"], inp["L0_ffn_Bd"]
    agu_h = np.zeros((D, 256), np.float32)
    for e in range(E):
        agu_h[:, e * 16:(e + 1) * 16] = Ag[e].T
        agu_h[:, 128 + e * 16:128 + (e + 1) * 16] = Au[e].T
    bgu_h = np.zeros((16, 2 * 44 * 2048), np.float32)
    for b2 in range(2):
        for j in range(44):
            blk = np.zeros((16, 2048), np.float32)
            for e4 in range(4):
                e = b2 * 4 + e4
                blk[:, e4 * 256:e4 * 256 + 128] = Bg[e][j * 128:(j + 1) * 128, :].T
                blk[:, 1024 + e4 * 256:1024 + e4 * 256 + 128] = Bu[e][j * 128:(j + 1) * 128, :].T
            bgu_h[:, (b2 * 44 + j) * 2048:(b2 * 44 + j + 1) * 2048] = blk
    adt_h = np.zeros((128, 44 * 128), np.float32)
    for j in range(44):
        for e in range(E):
            adt_h[:, j * 128 + e * 16: j * 128 + (e + 1) * 16] = Ad_[e][:, j * 128:(j + 1) * 128].T
    bdt_h = np.zeros((16, E * 2048), np.float32)
    for e in range(E):
        bdt_h[:, e * 2048:(e + 1) * 2048] = Bd[e].T

    arranged = {
        "l0a_wqT": _A(inp["L0_attn_a_wq"].T), "l0a_wkT": _A(inp["L0_attn_a_wk"].T),
        "l0a_wvT": _A(inp["L0_attn_a_wv"].T), "l0a_woT": _A(inp["L0_attn_a_wo"].T),
        "l0b_pack": _A(inp["L0_attn_b_W_pack"]), "l0b_woT": _A(inp["L0_attn_b_W_o"]),
        "f0_wg": _A(inp["L0_ffn_Wg"].T), "f0_wu": _A(inp["L0_ffn_Wu"].T),
        "f0_wd": _A(inp["L0_ffn_Wd"].T), "f0_wr": _A(inp["L0_ffn_Wr"].T),
        "f0_agu": agu_h, "f0_bgu": bgu_h, "f0_adt": adt_h, "f0_bdt": bdt_h,
        "l1a_wqT": _A(np.concatenate([inp["L1_attn_a_Wq"][h] for h in range(16)], 1)),
        "l1a_wkT": _A(np.concatenate([inp["L1_attn_a_Wk"][g] for g in range(4)], 1)),
        "l1a_wvT": _A(np.concatenate([inp["L1_attn_a_Wv"][g] for g in range(4)], 1)),
        "f1_wg": _A(inp["L1_ffn_w_gate"]), "f1_wu": _A(inp["L1_ffn_w_up"]),
        "f1_wd": _A(inp["L1_ffn_w_down"]),
        "b_wzT": _A(inp["L1_attn_b_W_z"].T), "b_whT": _A(inp["L1_attn_b_W_h"].T),
    }
    C = _consts()

    for key, builder in [("prep", build_prep),
                         ("a0a", lambda: build_attn("l0a")),
                         ("a0b", lambda: build_attn("l0b")),
                         ("a1a", lambda: build_attn("l1a")),
                         ("lnp", build_lnp),
                         ("moe", build_moe),
                         ("f1", lambda: build_ffn("f1")),
                         ("l1b", build_l1b)]:
        if key not in _PROGS:
            _PROGS[key] = builder()
    cores = list(range(NCORES))
    zero_y = np.zeros((D, TC), BF16NP)

    x0c = []
    im = []
    for c in cores:
        q, s = c // 2, c % 2
        x0c.append(_A(X[q, s * TC:(s + 1) * TC, :].T))
        d = {"x0": x0c[c]}
        for nm, R, Cc, dt in _CASTS:
            n = R * Cc // NCORES
            d[nm + "_in"] = _A(arranged[nm].reshape(-1)[c * n:(c + 1) * n].reshape(128, n // 128))
        im.append(d)
    r0 = run_bass_kernel_spmd(_PROGS["prep"], im, core_ids=cores)
    xn_cur = [r0.results[c]["xn0"] for c in cores]
    W = {}
    for nm, R, Cc, dt in _CASTS:
        W[nm] = np.concatenate([np.asarray(r0.results[c][nm + "_bf"]).reshape(-1)
                                for c in cores]).reshape(R, Cc)
    x_cur = x0c

    def lnp_launch(x_cur, yp, bo):
        im = []
        for c in cores:
            q, s = c // 2, c % 2
            ssl = slice(s * TC, (s + 1) * TC)
            im.append({"x": x_cur[c],
                       "yA": zero_y if yp is None else _A(yp[2 * q][:, ssl]),
                       "yB": zero_y if yp is None else _A(yp[2 * q + 1][:, ssl]),
                       "bo_cols": bo})
        r = run_bass_kernel_spmd(_PROGS["lnp"], im, core_ids=cores)
        return ([np.asarray(r.results[c]["xout"]) for c in cores],
                [np.asarray(r.results[c]["xnout"]) for c in cores])

    def moe_launch(x_cur, xn_cur):
        im = []
        for c in cores:
            d = {"x": x_cur[c], "xn": xn_cur[c], "identb": C["identb"], "oh8": C["oh8"],
                 "wg": W["f0_wg"], "wu": W["f0_wu"], "wd": W["f0_wd"], "wr": W["f0_wr"],
                 "agu": W["f0_agu"], "bgu": W["f0_bgu"], "adt": W["f0_adt"], "bdt": W["f0_bdt"]}
            im.append(d)
        r = run_bass_kernel_spmd(_PROGS["moe"], im, core_ids=cores)
        return [np.asarray(r.results[c]["xout"]) for c in cores]

    bq, bk, bv = inp["L0_attn_a_bq"], inp["L0_attn_a_bk"], inp["L0_attn_a_bv"]
    im = []
    for c in cores:
        q, r = c // 2, c % 2
        rsl = slice(r * 1024, (r + 1) * 1024)
        im.append({
            "xn_h0": xn_cur[2 * q], "xn_h1": xn_cur[2 * q + 1],
            "wqT": _A(W["l0a_wqT"][:, rsl]), "wkT": _A(W["l0a_wkT"][:, rsl]),
            "wvT": _A(W["l0a_wvT"][:, rsl]), "woT": _A(W["l0a_woT"][rsl, :]),
            "bqc": _A(bq[rsl].reshape(8, 128).T), "bkc": _A(bk[rsl].reshape(8, 128).T),
            "bvbc": _A(np.broadcast_to(bv[rsl], (128, 1024)).astype(np.float32)),
            "masks": C["masks"]})
    r1 = run_bass_kernel_spmd(_PROGS["a0a"], im, core_ids=cores)
    yp = [np.asarray(r1.results[c]["y_part"]) for c in cores]

    bo_cols = _A(inp["L0_attn_a_bo"].reshape(16, 128).T)
    x_cur, xn_cur = lnp_launch(x_cur, yp, bo_cols)
    x_cur = moe_launch(x_cur, xn_cur)
    x_cur, xn_cur = lnp_launch(x_cur, None, C["zero_bo"])

    conv_k, conv_v = inp["L0_attn_b_conv_k"], inp["L0_attn_b_conv_v"]
    im = []
    for c in cores:
        q, r = c // 2, c % 2
        qsl = slice(r * 1024, (r + 1) * 1024)
        ksl = slice(2048 + r * 512, 2048 + (r + 1) * 512)
        vsl = slice(3072 + r * 512, 3072 + (r + 1) * 512)
        ckc = np.zeros((128, 8), np.float32)
        for g in range(4):
            ckc[:, g] = conv_k[r * 4 + g, 0]
            ckc[:, 4 + g] = conv_k[r * 4 + g, 1]
        w0v = np.repeat(conv_v[r * 4:(r + 1) * 4, 0], 128)
        w1v = np.repeat(conv_v[r * 4:(r + 1) * 4, 1], 128)
        im.append({
            "xn_h0": xn_cur[2 * q], "xn_h1": xn_cur[2 * q + 1],
            "wqT": _A(W["l0b_pack"][:, qsl]), "wkT": _A(W["l0b_pack"][:, ksl]),
            "wvT": _A(W["l0b_pack"][:, vsl]), "woT": _A(W["l0b_woT"][qsl, :]),
            "masks": C["masks"], "masks_lo": C["masks_lo"], "C2": C["C2"], "S2": C["S2"],
            "convk": ckc,
            "w0v": _A(np.broadcast_to(w0v, (128, 512)).astype(np.float32)),
            "w1v": _A(np.broadcast_to(w1v, (128, 512)).astype(np.float32))})
    r3 = run_bass_kernel_spmd(_PROGS["a0b"], im, core_ids=cores)
    yp = [np.asarray(r3.results[c]["y_part"]) for c in cores]

    x_cur, xn_cur = lnp_launch(x_cur, yp, C["zero_bo"])
    x_cur = moe_launch(x_cur, xn_cur)
    x_cur, xn_cur = lnp_launch(x_cur, None, C["zero_bo"])

    bq1, bk1, bv1 = inp["L1_attn_a_bq"], inp["L1_attn_a_bk"], inp["L1_attn_a_bv"]
    im = []
    for c in cores:
        q, r = c // 2, c % 2
        im.append({
            "xn_h0": xn_cur[2 * q], "xn_h1": xn_cur[2 * q + 1],
            "wqT": _A(W["l1a_wqT"][:, r * 1024:(r + 1) * 1024]),
            "wkT": _A(W["l1a_wkT"][:, r * 256:(r + 1) * 256]),
            "wvT": _A(W["l1a_wvT"][:, r * 256:(r + 1) * 256]),
            "bqc": _A(bq1[r * 8:(r + 1) * 8].T), "bkc": _A(bk1[r * 2:(r + 1) * 2].T),
            "bvbc": _A(np.broadcast_to(bv1[r * 2:(r + 1) * 2].reshape(-1), (128, 256)).astype(np.float32)),
            "masks": C["masks"]})
    r5 = run_bass_kernel_spmd(_PROGS["a1a"], im, core_ids=cores)
    op = [np.asarray(r5.results[c]["o_part"]) for c in cores]

    f1_shared = {"wg": W["f1_wg"], "wu": W["f1_wu"], "wd": W["f1_wd"],
                 "sg": _A(inp["L1_ffn_s_gate"].reshape(64, 128).T),
                 "su": _A(inp["L1_ffn_s_up"].reshape(64, 128).T),
                 "sd": _A(inp["L1_ffn_s_down"].reshape(16, 128).T)}
    im = []
    for c in cores:
        q, s = c // 2, c % 2
        ssl = slice(s * TC, (s + 1) * TC)
        d = {"x": x_cur[c], "pLO": _A(op[2 * q][:, ssl]), "pHI": _A(op[2 * q + 1][:, ssl])}
        d.update(f1_shared)
        im.append(d)
    r6 = run_bass_kernel_spmd(_PROGS["f1"], im, core_ids=cores)
    x_cur = [np.asarray(r6.results[c]["xout"]) for c in cores]
    xn_cur = [np.asarray(r6.results[c]["xnout"]) for c in cores]

    cols = np.zeros((128, 96), np.float32)
    cols[:, 0:16] = inp["L1_attn_b_b_z"].reshape(16, 128).T
    cols[:, 16:32] = inp["L1_attn_b_b_h"].reshape(16, 128).T
    cols[:, 32:48] = inp["L1_attn_b_ln_w"].reshape(16, 128).T
    cols[:, 48:64] = inp["L1_attn_b_ln_b"].reshape(16, 128).T
    cols[:, 64:80] = inp["L1_attn_b_h0"].reshape(16, 128).T
    im = []
    for c in cores:
        s = c % 2
        colsin = cols.copy()
        colsin[:, 80] = 1.0 - s
        colsin[:, 81] = s
        im.append({"x": x_cur[c], "xn": xn_cur[c], "wzT": W["b_wzT"], "whT": W["b_whT"],
                   "colsin": colsin, "tril": C["tril"]})
    r7 = run_bass_kernel_spmd(_PROGS["l1b"], im, core_ids=cores)
    x_cur = [np.asarray(r7.results[c]["xout"]) for c in cores]

    im = []
    for c in cores:
        d = {"x": x_cur[c], "pLO": C["zero_p"], "pHI": C["zero_p"]}
        d.update(f1_shared)
        im.append(d)
    r8 = run_bass_kernel_spmd(_PROGS["f1"], im, core_ids=cores)

    OUT = np.empty((B, T, D), np.float32)
    for c in cores:
        q, s = c // 2, c % 2
        OUT[q, s * TC:(s + 1) * TC, :] = np.asarray(r8.results[c]["xout"]).T
    return OUT


# version 04
# speedup vs baseline: 1.0145x; 1.0145x over previous
import math
import numpy as np
from contextlib import ExitStack

import concourse.bass as bass
import concourse.mybir as mybir
from concourse.bass_utils import run_bass_kernel_spmd

F32 = mybir.dt.float32
BF16 = mybir.dt.bfloat16
I8 = mybir.dt.int8
ALU = mybir.AluOpType
AF = mybir.ActivationFunctionType
AX = mybir.AxisListType

B, T, D = 4, 2048, 2048
NCORES = 8
TC = T // 2
NCH = D // 128
NEG = -1.0e30
LN_EPS = 1e-5

ENGS = ("sync", "tensor", "scalar", "vector", "gpsimd")


class Sem:
    def __init__(self, nc, name):
        self.h = nc.alloc_semaphore(name=name)
        self.n = 0

    def inc(self, bi, k=1):
        bi.then_inc(self.h, k)
        self.n += k
        return self.n

    def wait(self, eng, val):
        eng.wait_ge(self.h, val)

    def wait_all(self, eng):
        eng.wait_ge(self.h, self.n)


class Ctx:
    def __init__(self):
        self.nc = bass.Bass("TRN2", target_bir_lowering=False, debug=False)
        self.block = None
        self.sems = {}
        self.sbufs = {}
        self._barsem = None
        self._barcnt = 0
        self._uid = 0
        self._consts = {}
        self._scopes = []

    def push_scope(self):
        self._scopes.append((ExitStack(), []))

    def pop_scope(self):
        st, names = self._scopes.pop()
        st.close()
        for n in names:
            self.sbufs.pop(n, None)

    def uid(self):
        self._uid += 1
        return self._uid

    def sem(self, name):
        if name not in self.sems:
            self.sems[name] = Sem(self.nc, name)
        return self.sems[name]

    def inp(self, name, shape, dt=F32):
        return self.nc.dram_tensor(name, list(shape), dt, kind="ExternalInput").ap()

    def outp(self, name, shape, dt=F32):
        return self.nc.dram_tensor(name, list(shape), dt, kind="ExternalOutput").ap()

    def dram(self, name, shape, dt=F32):
        return self.nc.dram_tensor(name, list(shape), dt, kind="Internal")

    def sb(self, name, shape, dt=F32):
        if self._scopes:
            st, names = self._scopes[-1]
            t = st.enter_context(self.nc.sbuf_tensor(name, list(shape), dt))
            names.append(name)
        else:
            t = self.nc.alloc_sbuf_tensor(name, list(shape), dt)
        self.sbufs[name] = t
        return t

    def on(self, engname, fn):
        getattr(self.block, engname)(fn)

    def barrier(self, engines=ENGS):
        if self._barsem is None:
            self._barsem = Sem(self.nc, "barrier")
        self._barcnt += len(engines)
        target = self._barcnt
        for en in engines:
            def f(eng, target=target):
                eng.drain()
                eng.sem_inc(self._barsem.h, 1)
                eng.wait_ge(self._barsem.h, target)
            self.on(en, f)

    def edge(self, engname, eng, ins):
        s = self.sem(f"edge_{engname}")
        s.inc(ins)
        s.wait(eng, s.n)

    def cconst(self, value):
        key = float(value)
        if key not in self._consts:
            t = self.sb(f"const{len(self._consts)}", [128, 1], F32)
            self._consts[key] = t
        return self._consts[key]

    def setup_common(self):
        for v in (LN_EPS, 1.0, 0.5):
            self.cconst(v)
        if not hasattr(self, "psA"):
            self.psA = self.nc.alloc_psum_tensor("psA", [128, 2048], F32)
        self.pb = [self.psA[:, i * 512:(i + 1) * 512] for i in range(4)] + \
                  [self.nc.alloc_psum_tensor(f"pb{i}", [128, 512], F32)[:, :] for i in range(4, 8)]
        self.ones_col_b = self.sb("ones_col_b", [128, 1], BF16)
        self.ones_row = self.sb("ones_row", [1, 128], F32)
        self.ones_row_b = self.sb("ones_row_b", [1, 128], BF16)
        def f(v):
            v.memset(self.ones_col_b[:], 1.0)
            v.memset(self.ones_row[:], 1.0)
            v.memset(self.ones_row_b[:], 1.0)
            for val, t in self._consts.items():
                v.memset(t[:], val)
        self.on("vector", f)
        self.barrier()


def cdiv(a, b):
    return (a + b - 1) // b


def emit_ln(ctx, x_sb, xn_sb, ntok, p0=0):
    nc = ctx.nc
    NT = ntok // 512
    u = ctx.uid()
    s_cast = ctx.sem(f"lncast{u}")
    s_stat = ctx.sem(f"lnstat{u}")
    s_rowa = ctx.sem(f"lnrowa{u}")
    s_rowb = ctx.sem(f"lnrowb{u}")
    s_row = ctx.sem(f"lnrow{u}")
    s_bc = ctx.sem(f"lnbc{u}")
    s_app = ctx.sem(f"lnapp{u}")
    if "sh_scr1" in ctx.sbufs:
        sq = ctx.sbufs["sh_scr1"]
    else:
        sq = ctx.sb(f"lnsq{u}", [128, 512], BF16)
    rows = ctx.sb(f"lnrows{u}", [1, 3 * 512], F32)
    pb = ctx.pb

    def row_sl(i, t):
        return rows[0:1, i * 512:(i + 1) * 512]

    def sl(sbuf, c, t):
        return sbuf[:, c * ntok + t * 512: c * ntok + t * 512 + 512]

    def fv(v):
        ins = v.tensor_copy(out=xn_sb[:, 0:NCH * ntok], in_=x_sb[:, 0:NCH * ntok])
        ctx.edge("vector", v, ins)
        for t in range(NT):
            for c in range(NCH):
                k = t * NCH + c
                if k >= 1:
                    s_stat.wait(v, k)
                ins = v.tensor_mul(out=sq[:, 0:512],
                                   in0=sl(xn_sb, c, t), in1=sl(xn_sb, c, t))
                s_cast.inc(ins)
            s_rowa.wait(v, 2 * (t + 1))
            tmp = row_sl(2, t)
            ins = v.tensor_mul(out=tmp, in0=row_sl(0, t), in1=row_sl(0, t))
            ctx.edge("vector", v, ins)
            ins = v.tensor_sub(out=row_sl(1, t), in0=row_sl(1, t), in1=tmp)
            s_rowb.inc(ins)
            s_bc.wait(v, t + 1)
            for c in range(NCH):
                ins = v.tensor_sub(out=sl(xn_sb, c, t), in0=sl(x_sb, c, t), in1=pb[p0 + 2][:, :])
            ctx.edge("vector", v, ins)
            for c in range(NCH):
                ins = v.tensor_mul(out=sl(xn_sb, c, t), in0=sl(xn_sb, c, t), in1=pb[p0 + 3][:, :])
                s_app.inc(ins)
    ctx.on("vector", fv)

    def ft(te):
        for t in range(NT):
            if t > 0:
                s_rowa.wait(te, 2 * t)
            for c in range(NCH):
                k = t * NCH + c
                s_cast.wait(te, k + 1)
                te.matmul(pb[p0][0:1, :], ctx.ones_col_b[:], sl(xn_sb, c, t),
                          start=(c == 0), stop=(c == NCH - 1))
                ins = te.matmul(pb[p0 + 1][0:1, :], ctx.ones_col_b[:],
                                sq[:, 0:512],
                                start=(c == 0), stop=(c == NCH - 1))
                s_stat.inc(ins)
            s_row.wait(te, t + 1)
            if t > 0:
                s_app.wait(te, t * NCH)
            te.matmul(pb[p0 + 2][:, :], ctx.ones_row[:], row_sl(0, t), start=True, stop=True)
            ins = te.matmul(pb[p0 + 3][:, :], ctx.ones_row[:], row_sl(2, t), start=True, stop=True)
            s_bc.inc(ins)
    ctx.on("tensor", ft)

    def fs(se):
        for t in range(NT):
            s_stat.wait(se, (t + 1) * NCH)
            if t > 0:
                s_bc.wait(se, t)
            se.activation(out=row_sl(0, t), in_=pb[p0][0:1, :], func=AF.Copy, scale=1.0 / D)
            ins = se.activation(out=row_sl(1, t), in_=pb[p0 + 1][0:1, :], func=AF.Copy, scale=1.0 / D)
            s_rowa.inc(ins, 2)
            s_rowb.wait(se, t + 1)
            ins = se.activation(out=row_sl(2, t), in_=row_sl(1, t), func=AF.Ln, bias=ctx.cconst(LN_EPS)[0:1, :])
            ctx.edge("scalar", se, ins)
            ins = se.activation(out=row_sl(2, t), in_=row_sl(2, t), func=AF.Exp, scale=-0.5)
            s_row.inc(ins)
    ctx.on("scalar", fs)

    return s_app



def emit_attention(ctx, cfg):
    nc = ctx.nc
    u = ctx.uid()
    T_ = cfg["T"]
    NHc, NKVc, HD = cfg["NHc"], cfg["NKVc"], cfg["HD"]
    group = cfg["group"]
    pad = cfg.get("pad", 0)
    Tp = T_ + pad
    DVC = NKVc * HD
    NQT = T_ // 512
    NTC = T_ // 128
    window = cfg.get("window")
    xn = cfg["xn_sb"]
    scale = 1.0 / math.sqrt(HD)
    convv = cfg.get("convv")
    rope = cfg.get("rope")
    convk = cfg.get("convk_cols")
    woT = cfg.get("woT")

    def xsl(c, lo, n):
        return xn[:, c * Tp + lo: c * Tp + lo + n]

    _pool = ctx.sb(f"at{u}_pool", [128, max(NHc * T_, NCH * DVC * (2 if convv is not None else 1))], BF16)
    wv_sb = _pool[:, 0:NCH * DVC]
    o_sb = _pool[:, 0:NHc * T_]
    v_sb = ctx.sb(f"at{u}_v", [128, NTC * DVC], BF16)
    wk_sb = ctx.sb(f"at{u}_wk", [128, 2 * NCH * HD], BF16)
    wq_sb = ctx.sb(f"at{u}_wq", [128, 2 * NCH * HD], BF16)
    k_sb = ctx.sb(f"at{u}_k", [128, T_], BF16)
    q_sb = ctx.sb(f"at{u}_q", [128, 2 * T_], BF16)
    scr_sb = ctx.sb(f"at{u}_scr", [128, T_ if (cfg.get("rope") or cfg.get("convk_cols")) else 4], BF16)
    probs = ctx.sb(f"at{u}_pr", [128, 2 * 1024], BF16)
    rz_sb = ctx.sb(f"at{u}_rz", [128, 2 * 512], F32)

    onesq = ctx.sb(f"at{u}_1s", [128, 128], BF16)
    if convv is not None:
        wv0_sb = _pool[:, NCH * DVC:2 * NCH * DVC]
    if woT is not None:
        wo_sb = ctx.sb(f"at{u}_wo", [128, 2 * NHc * HD], BF16)
        y_sb = ctx.sb(f"at{u}_y", [128, 2 * 512], BF16)

    pA = ctx.psA[:, :]
    pb4, pb5, pb6, pb7 = ctx.pb[4], ctx.pb[5], ctx.pb[6], ctx.pb[7]
    pp = [pb6, pb7]

    s_ld = ctx.sem(f"at{u}_ld")
    s_wkb = [ctx.sem(f"at{u}_wk0"), ctx.sem(f"at{u}_wk1")]
    s_wqb = [ctx.sem(f"at{u}_wq0"), ctx.sem(f"at{u}_wq1")]
    s_wob = [ctx.sem(f"at{u}_wo0"), ctx.sem(f"at{u}_wo1")]
    s_vp = ctx.sem(f"at{u}_vp")
    s_ve = ctx.sem(f"at{u}_ve")
    s_kq = ctx.sem(f"at{u}_kq")
    s_kqe = ctx.sem(f"at{u}_kqe")
    s_build = ctx.sem(f"at{u}_bld")
    s_sc = ctx.sem(f"at{u}_sc")
    s_pr = ctx.sem(f"at{u}_pr")
    s_pm = ctx.sem(f"at{u}_pm")
    s_pvg = ctx.sem(f"at{u}_pvg")
    s_rz = ctx.sem(f"at{u}_rz")
    s_oe = ctx.sem(f"at{u}_oe")
    s_op = ctx.sem(f"at{u}_op")
    s_ye = ctx.sem(f"at{u}_ye")
    s_ydb = [ctx.sem(f"at{u}_yd0"), ctx.sem(f"at{u}_yd1")]

    NS = cdiv(DVC, 512)
    n_groups = NHc // group

    def kc_list(qt):
        kc_end = 4 * qt + 4
        kc_start = 0
        if window is not None:
            kc_start = max(0, (qt * 512 - (window - 1)) // 128)
            if kc_start % 2:
                kc_start -= 1
        return list(range(kc_start, kc_end))

    def mask_for(qt, kc):
        ms = []
        r = kc - 4 * qt
        if 0 <= r < 4:
            ms.append(cfg["masks"][:, r * 512:(r + 1) * 512])
        if window is not None:
            q0 = qt * 512
            lo_k = q0 - (window - 1)
            if lo_k > 0 and kc * 128 < q0 - 512:
                rl = kc - (lo_k // 128)
                assert 0 <= rl < 4, (qt, kc, rl)
                ms.append(cfg["masks_lo"][:, rl * 512:(rl + 1) * 512])
        return ms

    builds = []
    for g in range(n_groups):
        builds.append(("K", g))
        for j in range(group):
            builds.append(("Q", g * group + j))
    sgroups = []
    for g in range(n_groups):
        for j in range(group):
            h = g * group + j
            for qt in range(NQT):
                kcs = kc_list(qt)
                gs = [kcs[i:i + 2] for i in range(0, len(kcs), 2)]
                for gi, kpair in enumerate(gs):
                    sgroups.append(dict(h=h, g=g, qt=qt, gi=gi, kcs=kpair,
                                        first=(gi == 0), last=(gi == len(gs) - 1)))
    n_sg = len(sgroups)
    mrun = []
    tot = 0
    for sg in sgroups:
        nm = sum(len(mask_for(sg["qt"], kc)) for kc in sg["kcs"])
        tot += nm
        mrun.append(tot)

    tile_of_build_end = {}
    ti = 0
    for b in builds:
        ti += NQT
        tile_of_build_end[b] = ti
    n_build_tiles = ti

    def fload(sy):
        ins = sy.dma_start(out=wv_sb[:].rearrange("p (c v) -> p c v", c=NCH),
                           in_=cfg["wvT"].rearrange("(c p) v -> p c v", p=128))
        s_ld.inc(ins, 16)
    ctx.on("sync", fload)

    def fmemset(v):
        v.memset(onesq[:], 1.0)
    ctx.on("vector", fmemset)
    ctx.barrier()

    if convv is not None:
        w0bc, w1bc = convv
        def fconvv(v):
            s_ld.wait(v, 16)
            ins = None
            for c in range(NCH):
                ins = v.tensor_mul(out=wv0_sb[:, c * DVC:(c + 1) * DVC],
                                   in0=wv_sb[:, c * DVC:(c + 1) * DVC], in1=w0bc[:])
            ctx.edge("vector", v, ins)
            for c in range(NCH):
                ins = v.tensor_mul(out=wv_sb[:, c * DVC:(c + 1) * DVC],
                                   in0=wv_sb[:, c * DVC:(c + 1) * DVC], in1=w1bc[:])
            ctx.edge("vector", v, ins)
        ctx.on("vector", fconvv)
        ctx.barrier()

    def fload2(sy):
        for bi, b in enumerate(builds):
            kind, idx = b
            same = [bb for bb in builds[:bi] if bb[0] == kind]
            if len(same) >= 2:
                s_kq.wait(sy, tile_of_build_end[same[-2]])
            if kind == "K":
                o0 = (idx % 2) * NCH * HD
                dst = wk_sb[:, o0:o0 + NCH * HD].rearrange("p (c f) -> p c f", c=NCH)
                src = cfg["wkT"].rearrange("(c p) f -> p c f", p=128)[:, :, idx * HD:(idx + 1) * HD]
                ins = sy.dma_start(out=dst, in_=src)
                s_wkb[idx % 2].inc(ins, 16)
            else:
                o0 = (idx % 2) * NCH * HD
                dst = wq_sb[:, o0:o0 + NCH * HD].rearrange("p (c f) -> p c f", c=NCH)
                src = cfg["wqT"].rearrange("(c p) f -> p c f", p=128)[:, :, idx * HD:(idx + 1) * HD]
                ins = sy.dma_start(out=dst, in_=src)
                s_wqb[idx % 2].inc(ins, 16)
        if woT is not None:
            for fo in range(NCH):
                if fo >= 2:
                    s_op.wait(sy, (fo - 1) * NQT)
                o0 = (fo % 2) * NHc * HD
                dst = wo_sb[:, o0:o0 + NHc * HD].rearrange("p (c f) -> p c f", c=NHc)
                src = woT.rearrange("(c p) f -> p c f", p=128)[:, :, fo * HD:(fo + 1) * HD]
                ins = sy.dma_start(out=dst, in_=src)
                s_wob[fo % 2].inc(ins, 16)
    ctx.on("sync", fload2)

    def ft(te):
        s_ld.wait(te, 16)
        for tc in range(NTC):
            for s in range(NS):
                iv = tc * NS + s
                bank = pA[:, ((iv % 4)) * 512: ((iv % 4)) * 512 + 512]
                ncols = min(512, DVC - s * 512)
                if iv >= 4:
                    s_ve.wait(te, iv - 3)
                for ic in range(NCH):
                    last = (ic == NCH - 1) and convv is None
                    ins = te.matmul(bank[:, 0:ncols], xsl(ic, pad + tc * 128, 128),
                                    wv_sb[:, ic * DVC + s * 512: ic * DVC + s * 512 + ncols],
                                    start=(ic == 0), stop=last)
                    if convv is not None:
                        last = (ic == NCH - 1)
                        ins = te.matmul(bank[:, 0:ncols], xsl(ic, tc * 128, 128),
                                        wv0_sb[:, ic * DVC + s * 512: ic * DVC + s * 512 + ncols],
                                        start=False, stop=last)
                s_vp.inc(ins)
        bi_ti = 0
        sg_i = 0
        hq = 0
        builds_done = 0
        for g in range(n_groups):
            s_wkb[g % 2].wait(te, 16 * (g // 2 + 1))
            for nt in range(NQT):
                if bi_ti >= 2:
                    s_kqe.wait(te, bi_ti - 1)
                bank = pp[bi_ti % 2]
                for ic in range(NCH):
                    ins = te.matmul(bank[:, :], wk_sb[:, (g % 2) * NCH * HD + ic * HD: (g % 2) * NCH * HD + ic * HD + HD],
                                    xsl(ic, pad + nt * 512, 512),
                                    start=(ic == 0), stop=(ic == NCH - 1))
                s_kq.inc(ins)
                bi_ti += 1
            builds_done += 1
            for j in range(group):
                h = g * group + j
                s_wqb[h % 2].wait(te, 16 * (h // 2 + 1))
                for nt in range(NQT):
                    if bi_ti >= 2:
                        s_kqe.wait(te, bi_ti - 1)
                    bank = pp[bi_ti % 2]
                    for ic in range(NCH):
                        ins = te.matmul(bank[:, :], wq_sb[:, (h % 2) * NCH * HD + ic * HD: (h % 2) * NCH * HD + ic * HD + HD],
                                        xsl(ic, pad + nt * 512, 512),
                                        start=(ic == 0), stop=(ic == NCH - 1))
                    s_kq.inc(ins)
                    bi_ti += 1
                builds_done += 1
                s_build.wait(te, builds_done)
                for qt in range(NQT):
                    first_sg_of_hq = True
                    while sg_i < n_sg and sgroups[sg_i]["h"] == h and sgroups[sg_i]["qt"] == qt:
                        sg = sgroups[sg_i]
                        pair = sg_i % 2
                        if sg_i >= 2:
                            s_pr.wait(te, sg_i - 1)
                        else:
                            s_ve.wait(te, NTC * NS)
                        for ci, kc in enumerate(sg["kcs"]):
                            bank = pA[:, (pair * 2 + ci) * 512:(pair * 2 + ci) * 512 + 512]
                            ins = te.matmul(bank, k_sb[:, kc * 128:kc * 128 + 128],
                                            q_sb[:, (h % 2) * T_ + qt * 512:(h % 2) * T_ + qt * 512 + 512],
                                            start=True, stop=True)
                        s_sc.inc(ins)
                        if mrun[sg_i] > (mrun[sg_i - 1] if sg_i else 0):
                            s_pm.wait(te, mrun[sg_i])
                        else:
                            s_pr.wait(te, sg_i + 1)
                        if first_sg_of_hq and hq > 0:
                            s_oe.wait(te, hq)
                        for ci, kc in enumerate(sg["kcs"]):
                            prb = probs[:, pair * 1024 + ci * 512: pair * 1024 + ci * 512 + 512]
                            lastmm = sg["last"] and ci == len(sg["kcs"]) - 1
                            te.matmul(pb4[:, :], v_sb[:, kc * DVC + g * HD: kc * DVC + g * HD + HD],
                                      prb, start=(first_sg_of_hq and ci == 0), stop=lastmm)
                            ins = te.matmul(pb5[:, :], onesq[:], prb,
                                            start=(first_sg_of_hq and ci == 0), stop=lastmm)
                        s_pvg.inc(ins)
                        first_sg_of_hq = False
                        sg_i += 1
                    hq += 1
        if woT is not None:
            s_oe.wait(te, NHc * NQT)
            opi = 0
            n_build_tiles_local = bi_ti
            for fo in range(NCH):
                s_wob[fo % 2].wait(te, 16 * (fo // 2 + 1))
                for qt in range(NQT):
                    if opi >= 2:
                        s_ye.wait(te, opi - 1)
                    elif bi_ti >= 2:
                        s_kqe.wait(te, bi_ti - 1)
                    bank = pp[bi_ti % 2]
                    for hc in range(NHc):
                        ins = te.matmul(bank[:, :],
                                        wo_sb[:, (fo % 2) * NHc * HD + hc * HD:(fo % 2) * NHc * HD + hc * HD + HD],
                                        o_sb[:, hc * T_ + qt * 512: hc * T_ + qt * 512 + 512],
                                        start=(hc == 0), stop=(hc == NHc - 1))
                    s_op.inc(ins)
                    bi_ti += 1
                    opi += 1
    ctx.on("tensor", ft)

    def fv(v):
        for tc in range(NTC):
            for s in range(NS):
                iv = tc * NS + s
                bank = pA[:, (iv % 4) * 512: (iv % 4) * 512 + 512]
                ncols = min(512, DVC - s * 512)
                s_vp.wait(v, iv + 1)
                dst = v_sb[:, tc * DVC + s * 512: tc * DVC + s * 512 + ncols]
                if cfg.get("bv_bc") is not None:
                    ins = v.tensor_add(out=dst, in0=bank[:, 0:ncols], in1=cfg["bv_bc"][:, s * 512:s * 512 + ncols])
                else:
                    ins = v.tensor_copy(out=dst, in_=bank[:, 0:ncols])
                s_ve.inc(ins)
        bi_ti = 0
        sg_i = 0
        hq = 0
        for g in range(n_groups):
            for nt in range(NQT):
                s_kq.wait(v, bi_ti + 1)
                bank = pp[bi_ti % 2]
                dst = k_sb[:, nt * 512:(nt + 1) * 512]
                if cfg.get("bk_cols") is not None:
                    ins = v.tensor_scalar_add(out=dst, in0=bank[:, :], scalar1=cfg["bk_cols"][:, g:g + 1])
                else:
                    ins = v.tensor_copy(out=dst, in_=bank[:, :])
                s_kqe.inc(ins)
                bi_ti += 1
            ins = _postproc(ctx, v, k_sb, scr_sb, T_, rope, convk, g, ins)
            s_build.inc(ins)
            for j in range(group):
                h = g * group + j
                qb = q_sb[:, (h % 2) * T_:(h % 2) * T_ + T_]
                for nt in range(NQT):
                    s_kq.wait(v, bi_ti + 1)
                    bank = pp[bi_ti % 2]
                    dst = qb[:, nt * 512:(nt + 1) * 512]
                    if cfg.get("bq_cols") is not None:
                        ins = v.tensor_scalar_add(out=dst, in0=bank[:, :], scalar1=cfg["bq_cols"][:, h:h + 1])
                    else:
                        ins = v.tensor_copy(out=dst, in_=bank[:, :])
                    s_kqe.inc(ins)
                    bi_ti += 1
                ins = _postproc(ctx, v, qb, scr_sb, T_, rope, None, None, ins)
                s_build.inc(ins)
                for qt in range(NQT):
                    while sg_i < n_sg and sgroups[sg_i]["h"] == h and sgroups[sg_i]["qt"] == qt:
                        sg = sgroups[sg_i]
                        pair = sg_i % 2
                        nm = mrun[sg_i] - (mrun[sg_i - 1] if sg_i else 0)
                        if nm:
                            s_pr.wait(v, sg_i + 1)
                            for ci, kc in enumerate(sg["kcs"]):
                                prb = probs[:, pair * 1024 + ci * 512: pair * 1024 + ci * 512 + 512]
                                for m in mask_for(sg["qt"], kc):
                                    ins = v.tensor_mul(out=prb, in0=prb, in1=m)
                                    s_pm.inc(ins)
                        sg_i += 1
                    s_rz.wait(v, hq + 1)
                    ins = v.tensor_mul(out=o_sb[:, h * T_ + qt * 512: h * T_ + qt * 512 + 512],
                                       in0=pb4[:, :], in1=rz_sb[:, 512:1024])
                    s_oe.inc(ins)
                    hq += 1
        if woT is not None:
            opi = 0
            for fo in range(NCH):
                for qt in range(NQT):
                    s_op.wait(v, opi + 1)
                    if opi >= 2:
                        s_ydb[opi % 2].wait(v, (opi // 2) * 16)
                    bank = pp[(bi_ti + opi) % 2]
                    ins = v.tensor_copy(out=y_sb[:, (opi % 2) * 512:(opi % 2) * 512 + 512], in_=bank[:, :])
                    s_ye.inc(ins)
                    opi += 1
    ctx.on("vector", fv)

    def fs(se):
        sg_i = 0
        hq = 0
        for sgx in range(n_sg):
            sg = sgroups[sgx]
            pair = sgx % 2
            s_sc.wait(se, sgx + 1)
            if sgx >= 2:
                s_pvg.wait(se, sgx - 1)
            ins = se.activation(out=probs[:, pair * 1024:pair * 1024 + 1024],
                                in_=pA[:, pair * 1024:pair * 1024 + 1024],
                                func=AF.Exp, scale=scale)
            s_pr.inc(ins)
            if sg["last"]:
                s_pvg.wait(se, sgx + 1)
                if hq > 0:
                    s_oe.wait(se, hq)
                ins = se.activation(out=rz_sb[:, 0:512], in_=pb5[:, :], func=AF.Ln)
                ctx.edge("scalar", se, ins)
                ins = se.activation(out=rz_sb[:, 512:1024], in_=rz_sb[:, 0:512],
                                    func=AF.Exp, scale=-1.0)
                s_rz.inc(ins)
                hq += 1
    ctx.on("scalar", fs)

    if woT is not None:
        def fg(gp):
            opi = 0
            for fo in range(NCH):
                for qt in range(NQT):
                    s_ye.wait(gp, opi + 1)
                    ins = gp.dma_start(out=cfg["y_out"][fo * 128:(fo + 1) * 128, qt * 512:(qt + 1) * 512],
                                       in_=y_sb[:, (opi % 2) * 512:(opi % 2) * 512 + 512])
                    s_ydb[opi % 2].inc(ins, 16)
                    opi += 1
            s_ydb[0].wait(gp, s_ydb[0].n)
            s_ydb[1].wait(gp, s_ydb[1].n)
        ctx.on("gpsimd", fg)
    return o_sb


def _postproc(ctx, v, buf, scr, T_, rope, convk_cols, g, last_ins):
    ins = last_ins
    if rope is None and convk_cols is None:
        return v.tensor_copy(out=scr[:, 0:1], in_=buf[:, 0:1])
    tiny = v.tensor_copy(out=scr[:, 0:1], in_=buf[:, 0:1])
    ctx.edge("vector", v, tiny)
    if rope is not None:
        C2, S2 = rope
        v.tensor_copy(out=scr[0:64, :], in_=buf[64:128, 0:T_])
        ins = v.tensor_copy(out=scr[64:128, :], in_=buf[0:64, 0:T_])
        ctx.edge("vector", v, ins)
        ins = v.tensor_mul(out=scr[:, :], in0=scr[:, :], in1=S2[:])
        ins2 = v.tensor_mul(out=buf[:, 0:T_], in0=buf[:, 0:T_], in1=C2[:])
        ctx.edge("vector", v, ins2)
        ins = v.tensor_add(out=buf[:, 0:T_], in0=buf[:, 0:T_], in1=scr[:, :])
    if convk_cols is not None:
        NKVc = convk_cols.shape[1] // 2
        w0 = convk_cols[:, g:g + 1]
        w1 = convk_cols[:, NKVc + g:NKVc + g + 1]
        if rope is not None:
            ctx.edge("vector", v, ins)
        ins = v.tensor_scalar_mul(out=scr[:, 0:T_ - 1], in0=buf[:, 0:T_ - 1], scalar1=w0)
        ctx.edge("vector", v, ins)
        v.scalar_tensor_tensor(out=buf[:, 1:T_], in0=buf[:, 1:T_], scalar=w1,
                               in1=scr[:, 0:T_ - 1], op0=ALU.mult, op1=ALU.add)
        ins = v.tensor_scalar_mul(out=buf[:, 0:1], in0=buf[:, 0:1], scalar1=w1)
    return ins


def emit_l1ffn(ctx, xn_sb, x_sb, ntok, cfg):
    nc = ctx.nc
    u = ctx.uid()
    INTER = 8192
    NOC = INTER // 128
    TT = 256
    NTILE = ntok // TT
    wg, wu, wd = cfg["wg"], cfg["wu"], cfg["wd"]
    sg_sb, su_sb, sd_sb = cfg["sg_sb"], cfg["su_sb"], cfg["sd_sb"]

    wgu_sb = ctx.sb(f"lf{u}_wgu", [128, 2 * 2 * NCH * 128], BF16)
    dn_sb = ctx.sb(f"lf{u}_dn", [128, 2 * NCH * 128], BF16)
    fuse_sb = ctx.sb(f"lf{u}_fuse", [128, NOC * TT], BF16)
    gs_sb = ctx.sb(f"lf{u}_gs", [128, 2 * TT], BF16)

    s_wg = [ctx.sem(f"lf{u}_wg0"), ctx.sem(f"lf{u}_wg1")]
    s_wd = [ctx.sem(f"lf{u}_wd0"), ctx.sem(f"lf{u}_wd1")]
    s_gu = ctx.sem(f"lf{u}_gu")
    s_si = ctx.sem(f"lf{u}_si")
    s_fe = ctx.sem(f"lf{u}_fe")
    s_dn = ctx.sem(f"lf{u}_dnm")
    s_xu = ctx.sem(f"lf{u}_xu")

    pb = ctx.pb

    def fload(sy):
        for t in range(NTILE):
            for oc in range(NOC):
                gidx = t * NOC + oc
                par = gidx % 2
                if gidx >= 2:
                    s_gu.wait(sy, gidx - 1)
                o0 = par * 2 * NCH * 128
                dst = wgu_sb[:, o0:o0 + NCH * 128].rearrange("p (c f) -> p c f", c=NCH)
                ins = sy.dma_start(out=dst, in_=wg.rearrange("(c p) f -> p c f", p=128)[:, :, oc * 128:(oc + 1) * 128])
                s_wg[par].inc(ins, 16)
                dst = wgu_sb[:, o0 + NCH * 128:o0 + 2 * NCH * 128].rearrange("p (c f) -> p c f", c=NCH)
                ins = sy.dma_start(out=dst, in_=wu.rearrange("(c p) f -> p c f", p=128)[:, :, oc * 128:(oc + 1) * 128])
                s_wg[par].inc(ins, 16)
            for sp in range(2):
                for oc in range(NOC):
                    didx = (t * 2 + sp) * NOC + oc
                    par = didx % 2
                    if didx >= 2:
                        s_dn.wait(sy, didx - 1)
                    o0 = par * NCH * 128
                    dst = dn_sb[:, o0:o0 + NCH * 128]
                    ins = sy.dma_start(out=dst, in_=wd[oc * 128:(oc + 1) * 128, :])
                    s_wd[par].inc(ins, 16)
    ctx.on("sync", fload)

    def ft(te):
        for t in range(NTILE):
            if t > 0:
                s_xu.wait(te, t * NCH)
            for oc in range(NOC):
                gidx = t * NOC + oc
                par = gidx % 2
                s_wg[par].wait(te, 32 * (gidx // 2 + 1))
                if gidx >= 2:
                    s_fe.wait(te, gidx - 1)
                gbank = pb[par][:, 0:TT]
                ubank = pb[2 + par][:, 0:TT]
                o0 = par * 2 * NCH * 128
                for ic in range(NCH):
                    te.matmul(gbank, wgu_sb[:, o0 + ic * 128:o0 + ic * 128 + 128],
                              xn_sb[:, ic * ntok + t * TT: ic * ntok + t * TT + TT],
                              start=(ic == 0), stop=(ic == NCH - 1))
                for ic in range(NCH):
                    ins = te.matmul(ubank, wgu_sb[:, o0 + NCH * 128 + ic * 128:o0 + NCH * 128 + ic * 128 + 128],
                                    xn_sb[:, ic * ntok + t * TT: ic * ntok + t * TT + TT],
                                    start=(ic == 0), stop=(ic == NCH - 1))
                s_gu.inc(ins)
            s_fe.wait(te, (t + 1) * NOC)
            for sp in range(2):
                s_xu.wait(te, (t * 2 + sp) * 8)
                for oc in range(NOC):
                    didx = (t * 2 + sp) * NOC + oc
                    par = didx % 2
                    s_wd[par].wait(te, 16 * (didx // 2 + 1))
                    o0 = par * NCH * 128
                    for i8 in range(8):
                        dc = sp * 8 + i8
                        ins = te.matmul(pb[i8][:, 0:TT],
                                        dn_sb[:, o0 + dc * 128:o0 + dc * 128 + 128],
                                        fuse_sb[:, oc * TT:(oc + 1) * TT],
                                        start=(oc == 0), stop=(oc == NOC - 1))
                    s_dn.inc(ins)
    ctx.on("tensor", ft)

    def fs(se):
        for t in range(NTILE):
            for oc in range(NOC):
                gidx = t * NOC + oc
                par = gidx % 2
                s_gu.wait(se, gidx + 1)
                gbank = pb[par][:, 0:TT]
                ins = se.activation(out=gs_sb[:, par * TT:par * TT + TT], in_=gbank,
                                    func=AF.Silu, scale=sg_sb[:, oc:oc + 1])
                s_si.inc(ins)
    ctx.on("scalar", fs)

    def fv(v):
        for t in range(NTILE):
            for oc in range(NOC):
                gidx = t * NOC + oc
                par = gidx % 2
                s_si.wait(v, gidx + 1)
                ubank = pb[2 + par][:, 0:TT]
                ins = v.scalar_tensor_tensor(out=fuse_sb[:, oc * TT:(oc + 1) * TT],
                                             in0=ubank, scalar=su_sb[:, oc:oc + 1],
                                             in1=gs_sb[:, par * TT:par * TT + TT],
                                             op0=ALU.mult, op1=ALU.mult)
                s_fe.inc(ins)
            for sp in range(2):
                s_dn.wait(v, (t * 2 + sp + 1) * NOC)
                for i8 in range(8):
                    dc = sp * 8 + i8
                    xs = x_sb[:, dc * ntok + t * TT: dc * ntok + t * TT + TT]
                    ins = v.scalar_tensor_tensor(out=xs, in0=pb[i8][:, 0:TT], scalar=sd_sb[:, dc:dc + 1],
                                                 in1=xs, op0=ALU.mult, op1=ALU.add)
                    s_xu.inc(ins)
    ctx.on("vector", fv)
    return s_xu


def emit_l0ffn(ctx, xn_sb, x_sb, ntok, cfg):
    nc = ctx.nc
    u = ctx.uid()
    DFF = 5632
    NJ = DFF // 128
    TT = 256
    NTILE = ntok // TT
    E = 8
    wg, wu, wd, wr = cfg["wg"], cfg["wu"], cfg["wd"], cfg["wr"]
    agu, bgu, adt, bdt = cfg["agu"], cfg["bgu"], cfg["adt"], cfg["bdt"]
    ident_b = cfg["ident_b"]
    oh8 = cfg["oh8"]

    G_sb = ctx.sb(f"mf{u}_G", [128, NJ * TT], BF16)
    U_sb = ctx.sb(f"mf{u}_U", [128, NJ * TT], BF16)
    wgu_sb = ctx.sb(f"mf{u}_wgu", [128, 2 * 2 * NCH * 128], BF16)
    wd_sb = ctx.sb(f"mf{u}_wd", [128, 2 * 1024], BF16)
    agu_sb = ctx.sb(f"mf{u}_agu", [128, NCH * 128], BF16)
    bgu_sb = ctx.sb(f"mf{u}_bgu", [16, 2048], BF16)
    adt_sb = ctx.sb(f"mf{u}_adt", [128, 2 * 128], BF16)
    bdt_sb = ctx.sb(f"mf{u}_bdt", [16, E * 1024], BF16)
    wr_sb = ctx.sb(f"mf{u}_wr", [128, NCH * E], BF16)
    r_sb = ctx.sb(f"mf{u}_r", [16, 2 * E * TT], BF16)
    glb = ctx.sbufs["sh_scr1"] if "sh_scr1" in ctx.sbufs else ctx.sb(f"mf{u}_gl", [128, 5 * TT], BF16)
    ulb = ctx.sb(f"mf{u}_ul", [128, 5 * TT], BF16)
    hbuf = ctx.sb(f"mf{u}_h", [128, 5 * TT], BF16)
    t_sb = ctx.sb(f"mf{u}_t", [16, E * TT], BF16)
    rt_sb = ctx.sb(f"mf{u}_rt", [128, 16], F32)
    we_tok = ctx.sb(f"mf{u}_wet", [128, 2 * 8], BF16)
    we_all = ctx.sb(f"mf{u}_wea", [8, TT], BF16)
    webc_sb = ctx.sb(f"mf{u}_web", [16, TT], F32)
    scr1 = ctx.sb(f"mf{u}_s1", [128, 1], BF16)

    s_wgu = [ctx.sem(f"mf{u}_wg0"), ctx.sem(f"mf{u}_wg1")]
    s_wd = [ctx.sem(f"mf{u}_wd0"), ctx.sem(f"mf{u}_wd1")]
    s_bgu = [ctx.sem(f"mf{u}_bg0"), ctx.sem(f"mf{u}_bg1")]
    s_adt = [ctx.sem(f"mf{u}_ad0"), ctx.sem(f"mf{u}_ad1")]
    s_ld1 = ctx.sem(f"mf{u}_ld1")
    s_bd = ctx.sem(f"mf{u}_bd")
    s_gu = ctx.sem(f"mf{u}_gu")
    s_gue = ctx.sem(f"mf{u}_gue")
    s_rt = ctx.sem(f"mf{u}_rtp")
    s_nm = ctx.sem(f"mf{u}_nm")
    s_rte = ctx.sem(f"mf{u}_rte")
    s_wet = ctx.sem(f"mf{u}_wet")
    s_wtr = ctx.sem(f"mf{u}_wtr")
    s_wea = ctx.sem(f"mf{u}_weaS")
    s_r = ctx.sem(f"mf{u}_rr")
    s_re = ctx.sem(f"mf{u}_re")
    s_x1 = ctx.sem(f"mf{u}_x1")
    s_x1e = ctx.sem(f"mf{u}_x1e")
    s_sil = ctx.sem(f"mf{u}_sil")
    s_h = ctx.sem(f"mf{u}_hh")
    s_hdone = ctx.sem(f"mf{u}_hd")
    s_adm = ctx.sem(f"mf{u}_adm")
    s_web = ctx.sem(f"mf{u}_webc")
    s_te = ctx.sem(f"mf{u}_tev")
    s_dn = ctx.sem(f"mf{u}_dns")
    s_xu = ctx.sem(f"mf{u}_xus")

    pb = ctx.pb
    NU = NJ + 1

    def xnsl(ic, t, n=TT, o=0):
        return xn_sb[:, ic * ntok + t * TT + o: ic * ntok + t * TT + o + n]

    def fload(sy):
        ins = sy.dma_start(out=wr_sb[:].rearrange("p (c f) -> p c f", c=NCH),
                           in_=wr.rearrange("(c p) f -> p c f", p=128))
        s_ld1.inc(ins, 16)
        for t in range(NTILE):
            for j in range(NJ):
                gidx = t * NJ + j
                par = gidx % 2
                if gidx >= 2:
                    s_gu.wait(sy, gidx - 1)
                o0 = par * 2 * NCH * 128
                dst = wgu_sb[:, o0:o0 + NCH * 128].rearrange("p (c f) -> p c f", c=NCH)
                ins = sy.dma_start(out=dst, in_=wg.rearrange("(c p) f -> p c f", p=128)[:, :, j * 128:(j + 1) * 128])
                s_wgu[par].inc(ins, 16)
                dst = wgu_sb[:, o0 + NCH * 128:o0 + 2 * NCH * 128].rearrange("p (c f) -> p c f", c=NCH)
                ins = sy.dma_start(out=dst, in_=wu.rearrange("(c p) f -> p c f", p=128)[:, :, j * 128:(j + 1) * 128])
                s_wgu[par].inc(ins, 16)
            for side in range(2):
                si = t * 2 + side
                if si >= 1:
                    s_r.wait(sy, si * E)
                ins = sy.dma_start(out=agu_sb[:].rearrange("p (c f) -> p c f", c=NCH),
                                   in_=agu.rearrange("(c p) f -> p c f", p=128)[:, :, side * 128:(side + 1) * 128])
                s_agu = ctx.sem(f"mf{u}_agu_s")
                s_agu.inc(ins, 16)
            for b2 in range(2):
                for j in range(NJ):
                    gidx = (t * 2 + b2) * NJ + j
                    par = gidx % 2
                    if gidx >= 1:
                        s_x1.wait(sy, 2 * gidx)
                    ins = sy.dma_start(out=bgu_sb[:, 0:2048],
                                       in_=bgu[:, (b2 * NJ + j) * 2048:(b2 * NJ + j + 1) * 2048])
                    s_bgu[0].inc(ins, 16)
                    if gidx >= 2:
                        s_adm.wait(sy, gidx - 1)
                    ins = sy.dma_start(out=adt_sb[:, par * 128:(par + 1) * 128],
                                       in_=adt[:, j * 128:(j + 1) * 128])
                    s_adt[par].inc(ins, 16)
            for sp in range(2):
                spi = t * 2 + sp
                if spi >= 1:
                    s_dn.wait(sy, spi * NU)
                ins = sy.dma_start(out=bdt_sb[:].rearrange("p (e f) -> p e f", e=E),
                                   in_=bdt.rearrange("p (e f) -> p e f", e=E)[:, :, sp * 1024:(sp + 1) * 1024])
                s_bd.inc(ins, 16)
                for j in range(NJ):
                    didx = spi * NJ + j
                    par = didx % 2
                    Lg = spi * NU + j
                    if didx >= 2:
                        s_dn.wait(sy, Lg - 1)
                    ins = sy.dma_start(out=wd_sb[:, par * 1024:(par + 1) * 1024],
                                       in_=wd[j * 128:(j + 1) * 128, sp * 1024:(sp + 1) * 1024])
                    s_wd[par].inc(ins, 16)
    ctx.on("sync", fload)

    def ft(te):
        s_ld1.wait(te, 16)
        for t in range(NTILE):
            for tk in range(2):
                if t > 0 and tk == 0:
                    s_xu.wait(te, t * 16)
                bank = pb[4 + tk]
                for ic in range(NCH):
                    ins = te.matmul(bank[:, 0:E], xnsl(ic, t, 128, tk * 128),
                                    wr_sb[:, ic * E:(ic + 1) * E], start=(ic == 0), stop=(ic == NCH - 1))
                s_rt.inc(ins)
            for tk in range(2):
                s_wet.wait(te, 2 * t + tk + 1)
                pv = pb[6].bitcast(BF16)
                ins = te.transpose(pv[0:8, 0:128], we_tok[:, tk * 8:(tk + 1) * 8], ident_b[:])
                s_wtr.inc(ins)
            for j in range(NJ):
                gidx = t * NJ + j
                par = gidx % 2
                s_wgu[par].wait(te, 32 * (gidx // 2 + 1))
                if gidx >= 2:
                    s_gue.wait(te, gidx - 1)
                elif t == 0 and j == 0:
                    pass
                if t > 0 and j < 2:
                    s_xu.wait(te, t * 16)
                gbank = pb[par][:, 0:TT]
                ubank = pb[2 + par][:, 0:TT]
                o0 = par * 2 * NCH * 128
                for ic in range(NCH):
                    te.matmul(gbank, wgu_sb[:, o0 + ic * 128:o0 + ic * 128 + 128], xnsl(ic, t),
                              start=(ic == 0), stop=(ic == NCH - 1))
                for ic in range(NCH):
                    ins = te.matmul(ubank, wgu_sb[:, o0 + NCH * 128 + ic * 128:o0 + NCH * 128 + ic * 128 + 128],
                                    xnsl(ic, t), start=(ic == 0), stop=(ic == NCH - 1))
                s_gu.inc(ins)
            s_rte.wait(te, 2 * t + 2)
            s_wea.wait(te, 2 * t + 2)
            for side in range(2):
                ctx.sem(f"mf{u}_agu_s").wait(te, 16 * (t * 2 + side + 1))
                for e in range(E):
                    ri = (t * 2 + side) * E + e
                    bank = pb[4 + ri % 4][0:16, 0:TT]
                    if ri >= 4:
                        s_re.wait(te, ri - 3)
                    for ic in range(NCH):
                        ins = te.matmul(bank, agu_sb[:, ic * 128 + e * 16: ic * 128 + e * 16 + 16],
                                        xnsl(ic, t), start=(ic == 0), stop=(ic == NCH - 1))
                    s_r.inc(ins)
            for b2 in range(2):
                for j in range(NJ):
                    gidx = (t * 2 + b2) * NJ + j
                    par = gidx % 2
                    s_bgu[0].wait(te, 16 * (gidx + 1))
                    s_adt[par].wait(te, 16 * (gidx // 2 + 1))
                    if b2 == 0 and j == 0:
                        s_gue.wait(te, (t + 1) * NJ)
                        s_re.wait(te, 2 * E * (t + 1))
                    for side in range(2):
                        if b2 == 1 and j == 0 and side == 0:
                            s_te.wait(te, 2 * t + 1)
                        if gidx * 2 + side >= 1 and not (b2 == 0 and j == 0 and side == 0):
                            s_x1e.wait(te, 2 * gidx + side)
                        for e4 in range(4):
                            e = b2 * 4 + e4
                            ins = te.matmul(pb[e4][:, 0:TT],
                                            bgu_sb[0:16, side * 1024 + e4 * 256: side * 1024 + e4 * 256 + 128],
                                            r_sb[0:16, (side * E + e) * TT:(side * E + e) * TT + TT],
                                            start=True, stop=True)
                        s_x1.inc(ins)
                    s_h.wait(te, gidx + 1)
                    if b2 == 1 and j == 0:
                        s_te.wait(te, 2 * t + 1)
                    for e4 in range(4):
                        e = b2 * 4 + e4
                        ins = te.matmul(pb[4 + e4][0:16, 0:TT],
                                        adt_sb[:, par * 128 + e * 16: par * 128 + e * 16 + 16],
                                        hbuf[:, e4 * TT:(e4 + 1) * TT],
                                        start=(j == 0), stop=(j == NJ - 1))
                    s_adm.inc(ins)
                s_x1e.wait(te, 2 * ((t * 2 + b2) * NJ + NJ))
                for e4 in range(4):
                    e = b2 * 4 + e4
                    ins = te.matmul(pb[e4][0:16, 0:TT], oh8[0:8, e * 16:(e + 1) * 16],
                                    we_all[0:8, 0:TT], start=True, stop=True)
                s_web.inc(ins)
            for sp in range(2):
                if sp == 0:
                    s_te.wait(te, 2 * (t + 1))
                s_xu.wait(te, (t * 2 + sp) * 8)
                for j in range(NJ):
                    didx = (t * 2 + sp) * NJ + j
                    par = didx % 2
                    s_wd[par].wait(te, 16 * (didx // 2 + 1))
                    for i8 in range(8):
                        ins = te.matmul(pb[i8][:, 0:TT],
                                        wd_sb[:, par * 1024 + i8 * 128: par * 1024 + i8 * 128 + 128],
                                        G_sb[:, j * TT:(j + 1) * TT],
                                        start=(j == 0), stop=False)
                    s_dn.inc(ins)
                s_bd.wait(te, 16 * (t * 2 + sp + 1))
                for e in range(E):
                    for i8 in range(8):
                        ins = te.matmul(pb[i8][:, 0:TT],
                                        bdt_sb[0:16, e * 1024 + i8 * 128: e * 1024 + i8 * 128 + 128],
                                        t_sb[0:16, e * TT:(e + 1) * TT],
                                        start=False, stop=(e == E - 1))
                s_dn.inc(ins)
    ctx.on("tensor", ft)

    def fs(se):
        for t in range(NTILE):
            for tk in range(2):
                s_rt.wait(se, 2 * t + tk + 1)
                s_nm.wait(se, 2 * t + tk + 1)
                ins = se.activation(out=rt_sb[:, 8:16], in_=pb[4 + tk][:, 0:E], func=AF.Exp,
                                    bias=rt_sb[:, 0:1])
                s_rte.inc(ins)
            for b2 in range(2):
                for j in range(NJ):
                    gidx = (t * 2 + b2) * NJ + j
                    s_x1e.wait(se, 2 * gidx + 1)
                    if gidx > 0:
                        s_hdone.wait(se, gidx)
                        s_adm.wait(se, gidx)
                    ins = se.activation(out=hbuf[:, 0:5 * TT], in_=glb[:, 0:5 * TT], func=AF.Silu)
                    s_sil.inc(ins)
    ctx.on("scalar", fs)

    def fv(v):
        for t in range(NTILE):
            for tk in range(2):
                s_rt.wait(v, 2 * t + tk + 1)
                ins = v.reduce_max(out=rt_sb[:, 0:1], in_=pb[4 + tk][:, 0:E], axis=AX.X, negate=True)
                s_nm.inc(ins)
                s_rte.wait(v, 2 * t + tk + 1)
                ins = v.reduce_sum(out=rt_sb[:, 1:2], in_=rt_sb[:, 8:16], axis=AX.X)
                ctx.edge("vector", v, ins)
                ins = v.reciprocal(out=rt_sb[:, 2:3], in_=rt_sb[:, 1:2])
                ctx.edge("vector", v, ins)
                ins = v.tensor_scalar_mul(out=rt_sb[:, 8:16], in0=rt_sb[:, 8:16], scalar1=rt_sb[:, 2:3])
                ctx.edge("vector", v, ins)
                ins = v.max(out=rt_sb[:, 0:8], in_=rt_sb[:, 8:16])
                ctx.edge("vector", v, ins)
                ins = v.tensor_scalar(out=we_tok[:, tk * 8:(tk + 1) * 8], in0=rt_sb[:, 8:16],
                                      scalar1=rt_sb[:, 1:2], scalar2=2.0,
                                      op0=ALU.is_ge, op1=ALU.mult)
                ctx.edge("vector", v, ins)
                ins = v.tensor_mul(out=we_tok[:, tk * 8:(tk + 1) * 8], in0=we_tok[:, tk * 8:(tk + 1) * 8],
                                   in1=rt_sb[:, 8:16])
                s_wet.inc(ins)
                s_wtr.wait(v, 2 * t + tk + 1)
                ins = v.tensor_copy(out=we_all[0:8, tk * 128:(tk + 1) * 128],
                                    in_=pb[6].bitcast(BF16)[0:8, 0:128])
                s_wea.inc(ins)
            for j in range(NJ):
                gidx = t * NJ + j
                par = gidx % 2
                s_gu.wait(v, gidx + 1)
                v.tensor_copy(out=G_sb[:, j * TT:(j + 1) * TT], in_=pb[par][:, 0:TT])
                ins = v.tensor_copy(out=U_sb[:, j * TT:(j + 1) * TT], in_=pb[2 + par][:, 0:TT])
                s_gue.inc(ins)
            for side in range(2):
                for e in range(E):
                    ri = (t * 2 + side) * E + e
                    bank = pb[4 + ri % 4][0:16, 0:TT]
                    s_r.wait(v, ri + 1)
                    ins = v.tensor_scalar_mul(out=r_sb[0:16, (side * E + e) * TT:(side * E + e) * TT + TT],
                                              in0=bank, scalar1=2.0)
                    s_re.inc(ins)
            for b2 in range(2):
                for j in range(NJ):
                    gidx = (t * 2 + b2) * NJ + j
                    s_x1.wait(v, 2 * gidx + 1)
                    if gidx > 0:
                        s_sil.wait(v, gidx)
                    for e4 in range(4):
                        ins = v.tensor_copy(out=glb[:, e4 * TT:(e4 + 1) * TT], in_=pb[e4][:, 0:TT])
                    ctx.edge("vector", v, ins)
                    gsl = G_sb[:, j * TT:(j + 1) * TT]
                    gbc = gsl.unsqueeze(1).broadcast_to([128, 4, TT])
                    ins = v.tensor_add(out=glb[:, 0:4 * TT].rearrange("p (s f) -> p s f", s=4),
                                       in0=glb[:, 0:4 * TT].rearrange("p (s f) -> p s f", s=4), in1=gbc)
                    ins = v.tensor_copy(out=glb[:, 4 * TT:5 * TT], in_=gsl)
                    s_x1e.inc(ins)
                    s_x1.wait(v, 2 * gidx + 2)
                    if gidx > 0:
                        s_h.wait(v, gidx)
                    for e4 in range(4):
                        ins = v.tensor_copy(out=ulb[:, e4 * TT:(e4 + 1) * TT], in_=pb[e4][:, 0:TT])
                    ctx.edge("vector", v, ins)
                    usl = U_sb[:, j * TT:(j + 1) * TT]
                    ubc = usl.unsqueeze(1).broadcast_to([128, 4, TT])
                    ins = v.tensor_add(out=ulb[:, 0:4 * TT].rearrange("p (s f) -> p s f", s=4),
                                       in0=ulb[:, 0:4 * TT].rearrange("p (s f) -> p s f", s=4), in1=ubc)
                    s_x1e.inc(ins)
                    ins2 = v.tensor_copy(out=ulb[:, 4 * TT:5 * TT], in_=usl)
                    s_sil.wait(v, gidx + 1)
                    s_x1e.wait(v, s_x1e.n)
                    ctx.edge("vector", v, ins2)
                    ins = v.tensor_mul(out=hbuf[:, 0:5 * TT], in0=hbuf[:, 0:5 * TT], in1=ulb[:, 0:5 * TT])
                    s_h.inc(ins)
                    s_h.wait(v, s_h.n)
                    if b2 == 1:
                        ins = v.tensor_copy(out=G_sb[:, j * TT:(j + 1) * TT], in_=hbuf[:, 4 * TT:5 * TT])
                    else:
                        ins = v.tensor_copy(out=scr1[:], in_=hbuf[:, 0:1])
                    s_hdone.inc(ins)
                s_web.wait(v, 2 * t + b2 + 1)
                s_adm.wait(v, (t * 2 + b2 + 1) * NJ)
                for e4 in range(4):
                    e = b2 * 4 + e4
                    ins = v.tensor_copy(out=webc_sb[0:16, 0:TT], in_=pb[e4][0:16, 0:TT])
                    ctx.edge("vector", v, ins)
                    ins = v.tensor_mul(out=t_sb[0:16, e * TT:(e + 1) * TT],
                                       in0=pb[4 + e4][0:16, 0:TT], in1=webc_sb[0:16, 0:TT])
                    if e4 == 3:
                        s_te.inc(ins)
                    else:
                        ctx.edge("vector", v, ins)
            for sp in range(2):
                s_dn.wait(v, (t * 2 + sp + 1) * NU)
                for i8 in range(8):
                    dc = sp * 8 + i8
                    xs = x_sb[:, dc * ntok + t * TT: dc * ntok + t * TT + TT]
                    ins = v.tensor_add(out=xs, in0=pb[i8][:, 0:TT], in1=xs)
                    s_xu.inc(ins)
    ctx.on("vector", fv)
    return s_xu


def emit_l1attnb(ctx, xn_sb, ntok, cfg):
    nc = ctx.nc
    u = ctx.uid()
    wzT, whT = cfg["wzT"], cfg["whT"]
    cols = cfg["cols_sb"]
    tril_f = cfg["tril_f"]
    onesc_f = cfg["onesc_f"]
    pair_groups = cfg["pair_groups"]
    all_groups = cfg["all_groups"]
    x_load = cfg["x_load"]
    x_store = cfg["x_store"]
    NSTAT = float(cfg.get("nstat", 4.0 * 2049.0 * 2048.0))
    DUM = -1.0e6

    f0 = cols[:, 80:81]
    f1 = cols[:, 81:82]
    bzc = lambda c: cols[:, c:c + 1]
    bhc = lambda c: cols[:, 16 + c:17 + c]

    UB = ctx.sb(f"gr{u}_U", [128, NCH * ntok], F32)
    AB = ctx.sb(f"gr{u}_A", [128, NCH * ntok], F32)
    wbuf = ctx.sb(f"gr{u}_w", [128, 2 * 2048], BF16)
    tB = ctx.sb(f"gr{u}_tB", [128, 1024], F32)
    tC = ctx.sb(f"gr{u}_tC", [128, 1024], F32)
    tD = ctx.sb(f"gr{u}_tD", [128, 1024], F32)
    tE = ctx.sb(f"gr{u}_tE", [128, 1024], F32)
    tF = ctx.sb(f"gr{u}_tF", [128, 1024], F32)
    tG = ctx.sb(f"gr{u}_tG", [128, 1024], F32)
    prow = ctx.sb(f"gr{u}_pr", [1, 1024], F32)
    pub = ctx.sb(f"gr{u}_pub", [128, 32], F32)
    ppre = ctx.sb(f"gr{u}_ppre", [128, 32], F32)
    scol = ctx.sb(f"gr{u}_scol", [128, 32], F32)
    rows = ctx.sb(f"gr{u}_rows", [1, 1024], F32)
    st2 = ctx.sb(f"gr{u}_st2", [1, 8], F32)
    xs_sb = ctx.sb(f"gr{u}_xs", [128, 1024], F32)
    u0 = ctx.sb(f"gr{u}_u0", [128, 16], F32)
    fneg = ctx.sb(f"gr{u}_fn", [128, 2], F32)
    msk0_sb = ctx.sb(f"gr{u}_msk0", [128, 16], mybir.dt.uint8)

    d_pub = ctx.dram(f"gr{u}_dpub", [128, 32], F32)
    d_pre = ctx.dram(f"gr{u}_dpre", [128, 32], F32)
    d_st = ctx.dram(f"gr{u}_dst", [1, 2], F32)
    d_st2 = ctx.dram(f"gr{u}_dst2", [1, 2], F32)

    s_w = [ctx.sem(f"gr{u}_w0"), ctx.sem(f"gr{u}_w1")]
    s_kp = ctx.sem(f"gr{u}_kp")
    s_hp = ctx.sem(f"gr{u}_hp")
    s_t3k = ctx.sem(f"gr{u}_t3k")
    s_lcD = ctx.sem(f"gr{u}_lcD")
    s_csum = ctx.sem(f"gr{u}_cs")
    s_prow = ctx.sem(f"gr{u}_prw")
    s_asp = ctx.sem(f"gr{u}_asp")
    s_t3h = ctx.sem(f"gr{u}_t3h")
    s_p2 = ctx.sem(f"gr{u}_p2")
    s_p1 = ctx.sem(f"gr{u}_p1")
    s_u = ctx.sem(f"gr{u}_su")
    s_am = ctx.sem(f"gr{u}_am")
    s_ax = ctx.sem(f"gr{u}_ax")
    s_scan = ctx.sem(f"gr{u}_sc")
    s_u0a = ctx.sem(f"gr{u}_u0a")
    s_u0b = ctx.sem(f"gr{u}_u0b")
    s_u0c = ctx.sem(f"gr{u}_u0c")
    s_cc = ctx.sem(f"gr{u}_cc")
    s_ccop = ctx.sem(f"gr{u}_ccop")
    s_am2 = ctx.sem(f"gr{u}_am2")
    s_ax2 = ctx.sem(f"gr{u}_ax2")
    s_v1 = ctx.sem(f"gr{u}_v1")
    s_t2 = ctx.sem(f"gr{u}_t2")
    s_e1 = ctx.sem(f"gr{u}_e1")
    s_e2 = ctx.sem(f"gr{u}_e2")
    s_vp2 = ctx.sem(f"gr{u}_vp2")
    s_lnv = ctx.sem(f"gr{u}_lnv")
    s_lh = ctx.sem(f"gr{u}_lh")
    s_sq = ctx.sem(f"gr{u}_sq")
    s_st = ctx.sem(f"gr{u}_st")
    s_sr = ctx.sem(f"gr{u}_sr")
    s_sred = ctx.sem(f"gr{u}_srd")
    s_gs = ctx.sem(f"gr{u}_gs")
    s_gr = ctx.sem(f"gr{u}_grr")
    s_gb = ctx.sem(f"gr{u}_gb")
    s_mr = ctx.sem(f"gr{u}_mr")
    s_hx = ctx.sem(f"gr{u}_hx")
    s_v = ctx.sem(f"gr{u}_v")
    s_vsq = ctx.sem(f"gr{u}_vsq")
    s_vs1 = ctx.sem(f"gr{u}_vs1")
    s_vs2 = ctx.sem(f"gr{u}_vs2")
    s_rowA = ctx.sem(f"gr{u}_rwA")
    s_rowB = ctx.sem(f"gr{u}_rwB")
    s_rowC = ctx.sem(f"gr{u}_rwC")
    s_bc2 = ctx.sem(f"gr{u}_bc2")
    s_xin = [ctx.sem(f"gr{u}_xi0"), ctx.sem(f"gr{u}_xi1")]
    s_xap = ctx.sem(f"gr{u}_xa")
    s_xo = [ctx.sem(f"gr{u}_xo0"), ctx.sem(f"gr{u}_xo1")]

    pb = ctx.pb

    NCHL = cfg.get("nch1", NCH)

    def usl(buf, c, lo=0, n=None):
        n = ntok - lo if n is None else n
        return buf[:, c * ntok + lo: c * ntok + lo + n]

    def xnsl(ic, lo, n):
        return xn_sb[:, ic * ntok + lo: ic * ntok + lo + n]

    def u0c(c):
        return u0[:, c:c + 1]

    smalldma = "smalldma" in cfg.get("skip", ())
    def fsy(sy):
        for c in range(NCHL):
            for w_i, wsrc in ((0, wzT), (1, whT)):
                gi = 2 * c + w_i
                par = gi % 2
                if gi >= 2:
                    (s_kp if w_i == 0 else s_hp).wait(sy, c)
                dst = wbuf[:, par * 2048:(par + 1) * 2048].rearrange("p (i f) -> p i f", i=NCH)
                src = wsrc.rearrange("(i p) f -> p i f", p=128)[:, :, c * 128:(c + 1) * 128]
                if smalldma:
                    ins = sy.dma_start(out=dst[0:1, 0:1, 0:16], in_=src[0:1, 0:1, 0:16])
                else:
                    ins = sy.dma_start(out=dst, in_=src)
                s_w[par].inc(ins, 16)
    ctx.on("sync", fsy)

    def ft(te):
        skip = cfg.get("skip", ())
        for c in range(NCHL):
            gi = 2 * c
            s_w[gi % 2].wait(te, 16 * (gi // 2 + 1))
            if c > 0:
                s_u.wait(te, c)
            if "nok" in skip:
                ins = te.nop()
            else:
                for half in range(2):
                    for ic in range(NCH):
                        ins = te.matmul(pb[half][:, :], wbuf[:, (gi % 2) * 2048 + ic * 128:(gi % 2) * 2048 + ic * 128 + 128],
                                        xnsl(ic, half * 512, 512), start=(ic == 0), stop=(ic == NCH - 1))
            s_kp.inc(ins)
            gi = 2 * c + 1
            s_w[gi % 2].wait(te, 16 * (gi // 2 + 1))
            if "nok" in skip:
                ins = te.nop()
            else:
                for half in range(2):
                    for ic in range(NCH):
                        ins = te.matmul(pb[2 + half][:, :], wbuf[:, (gi % 2) * 2048 + ic * 128:(gi % 2) * 2048 + ic * 128 + 128],
                                        xnsl(ic, half * 512, 512), start=(ic == 0), stop=(ic == NCH - 1))
            s_hp.inc(ins)
            s_lcD.wait(te, c + 1)
            if c > 0:
                s_prow.wait(te, c + 1)
            if "noas" in skip:
                ins = te.nop()
                s_csum.inc(ins)
                s_prow.wait(te, c + 1)
                ins = te.nop()
                s_asp.inc(ins)
                continue
            for half in range(2):
                ins = te.matmul(pb[6 + half][0:1, 0:512], onesc_f[:],
                                tD[:, half * 512:half * 512 + 512], start=True, stop=True)
            s_csum.inc(ins)
            s_prow.wait(te, c + 1)
            for half in range(2):
                te.matmul(pb[4 + half][:, :], tril_f[:], tD[:, half * 512:half * 512 + 512],
                          start=True, stop=False)
                ins = te.matmul(pb[4 + half][:, :], ctx.ones_row[:], prow[0:1, half * 512:half * 512 + 512],
                                start=False, stop=True)
            s_asp.inc(ins)
    ctx.on("tensor", ft)

    def fs(se):
        if cfg.get("skip_u0"):
            s_u0a.wait(se, 1)
            return _fs_chunks(se)
        s_u0a.wait(se, 1)
        ins = se.activation(out=tF[:, 32:48], in_=tF[:, 0:16], func=AF.Ln, bias=ctx.cconst(0.5)[0:128, 0:1])
        ctx.edge("scalar", se, ins)
        ins = se.activation(out=tF[:, 48:64], in_=tF[:, 16:32], func=AF.Exp, scale=-1.0)
        ctx.edge("scalar", se, ins)
        ins = se.activation(out=tF[:, 48:64], in_=tF[:, 48:64], func=AF.Ln, bias=ctx.cconst(1.0)[0:128, 0:1])
        s_u0b.inc(ins)
        return _fs_chunks(se)

    def _fs_chunks(se):
        sk = cfg.get("skip", ())
        SWk = 16 if ("smallscalar" in sk or "smt3k" in sk) else 512
        SWh = 16 if ("smallscalar" in sk or "smt3h" in sk) else 512
        SWp = 16 if ("smallscalar" in sk or "smp1" in sk) else 512
        for c in range(NCHL):
            s_kp.wait(se, c + 1)
            if c > 0:
                s_u.wait(se, c)
            for half in range(2):
                SW = SWk
                hs = slice(half * 512, half * 512 + SW)
                ins = se.activation(out=tB[:, hs], in_=pb[half][:, 0:SW], func=AF.Abs, bias=bzc(c))
                ctx.edge("scalar", se, ins)
                ins = se.activation(out=tB[:, hs], in_=tB[:, hs], func=AF.Exp, scale=-1.0)
                ctx.edge("scalar", se, ins)
                ins = se.activation(out=tB[:, hs], in_=tB[:, hs], func=AF.Ln, bias=ctx.cconst(1.0)[0:128, 0:1])
            s_t3k.inc(ins)
            s_hp.wait(se, c + 1)
            s_lcD.wait(se, c + 1)
            for half in range(2):
                SW = SWh
                hs = slice(half * 512, half * 512 + SW)
                ins = se.activation(out=tB[:, hs], in_=pb[2 + half][:, 0:SW], func=AF.Abs, bias=bhc(c))
                ctx.edge("scalar", se, ins)
                ins = se.activation(out=tB[:, hs], in_=tB[:, hs], func=AF.Exp, scale=-1.0)
                ctx.edge("scalar", se, ins)
                ins = se.activation(out=tB[:, hs], in_=tB[:, hs], func=AF.Ln, bias=ctx.cconst(1.0)[0:128, 0:1])
            s_t3h.inc(ins)
            s_p2.wait(se, c + 1)
            for half in range(2):
                SW = SWp
                hs = slice(half * 512, half * 512 + SW)
                ins = se.activation(out=tB[:, hs], in_=pb[2 + half][:, 0:SW], func=AF.Relu, bias=bhc(c))
                ctx.edge("scalar", se, ins)
                ins = se.activation(out=tB[:, hs], in_=tB[:, hs], func=AF.Ln, bias=ctx.cconst(0.5)[0:128, 0:1])
            s_p1.inc(ins)
            s_am.wait(se, 2 * c + 1)
            if "scanexp" in cfg.get("skip", ()):
                ins = se.copy(out=tG[:, 0:1], in_=tG[:, 0:1])
                s_ax.inc(ins)
                s_am.wait(se, 2 * c + 2)
                ins = se.copy(out=tE[:, 0:1], in_=tE[:, 0:1])
                s_ax.inc(ins)
            else:
                ins = se.activation(out=tG[:, 0:1024], in_=tG[:, 0:1024], func=AF.Exp)
                s_ax.inc(ins)
                s_am.wait(se, 2 * c + 2)
                ins = se.activation(out=tE[:, 0:1024], in_=tE[:, 0:1024], func=AF.Exp)
                s_ax.inc(ins)
    ctx.on("scalar", fs)

    def fv(v):
        ins = v.memset(prow[0:1, :], 0.0)
        s_prow.inc(ins)
        if cfg.get("skip_u0"):
            ins = v.memset(u0[:], 0.0)
            ctx.edge("vector", v, ins)
            ins = v.memset(fneg[:], 0.0)
            s_u0a.inc(ins)
            s_u0b.wait(v, 0)
            ins = v.memset(tF[:, 0:96], 0.0)
            s_u0c.inc(ins)
            return _fv_chunks(v)
        h0s = cols[:, 64:80]
        ins = v.tensor_scalar_max(out=tF[:, 0:16], in0=h0s, scalar1=0.0)
        ctx.edge("vector", v, ins)
        ins = v.tensor_scalar_mul(out=tF[:, 16:32], in0=h0s, scalar1=-1.0)
        ctx.edge("vector", v, ins)
        ins = v.tensor_max(out=tF[:, 16:32], in0=tF[:, 16:32], in1=h0s)
        s_u0a.inc(ins)
        s_u0b.wait(v, 1)
        ins = v.tensor_scalar_min(out=tF[:, 64:80], in0=h0s, scalar1=0.0)
        ctx.edge("vector", v, ins)
        ins = v.tensor_sub(out=tF[:, 64:80], in0=tF[:, 64:80], in1=tF[:, 48:64])
        ctx.edge("vector", v, ins)
        ins = v.tensor_scalar(out=tF[:, 80:96], in0=h0s, scalar1=0.0, scalar2=None, op0=ALU.is_ge)
        ctx.edge("vector", v, ins)
        ins = v.tensor_sub(out=tF[:, 32:48], in0=tF[:, 32:48], in1=tF[:, 64:80])
        ctx.edge("vector", v, ins)
        ins = v.tensor_mul(out=tF[:, 32:48], in0=tF[:, 32:48], in1=tF[:, 80:96])
        ctx.edge("vector", v, ins)
        ins = v.tensor_add(out=tF[:, 0:16], in0=tF[:, 64:80], in1=tF[:, 32:48])
        ctx.edge("vector", v, ins)
        ins = v.tensor_scalar(out=u0[:], in0=tF[:, 0:16], scalar1=f0, scalar2=None, op0=ALU.mult)
        ctx.edge("vector", v, ins)
        ins = v.tensor_scalar_mul(out=fneg[:, 1:2], in0=f1, scalar1=DUM)
        ctx.edge("vector", v, ins)
        ins = v.tensor_scalar(out=u0[:], in0=u0[:], scalar1=fneg[:, 1:2], scalar2=None, op0=ALU.add)
        ctx.edge("vector", v, ins)
        ins = v.tensor_scalar_mul(out=fneg[:, 0:1], in0=f0, scalar1=DUM)
        s_u0c.inc(ins)
        return _fv_chunks(v)

    def _fv_chunks(v):
        sk = cfg.get("skip", ())
        VWa = 16 if ("smallvector" in sk or "smv_pre" in sk) else 512
        VWb = 16 if ("smallvector" in sk or "smv_te" in sk) else 512
        VWc = 16 if ("smallvector" in sk or "smv_uab" in sk) else 512
        for c in range(NCHL):
            VW = VWa; FW = 2 * VW
            s_kp.wait(v, c + 1)
            s_t3k.wait(v, c + 1)
            for half in range(2):
                hs = slice(half * 512, half * 512 + VW)
                ins = v.tensor_scalar(out=tC[:, hs], in0=pb[half][:, 0:VW], scalar1=bzc(c),
                                      scalar2=0.0, op0=ALU.add, op1=ALU.min)
            ctx.edge("vector", v, ins)
            ins = v.tensor_sub(out=tC[:, 0:FW], in0=tC[:, 0:FW], in1=tB[:, 0:FW])
            ctx.edge("vector", v, ins)
            for half in range(2):
                hs = slice(half * 512, half * 512 + VW)
                ins = v.scalar_tensor_tensor(out=tD[:, hs], in0=tC[:, hs], scalar=bzc(c),
                                             in1=pb[half][:, 0:VW], op0=ALU.subtract, op1=ALU.subtract)
            s_lcD.inc(ins)
            VW = VWb; FW = 2 * VW
            s_hp.wait(v, c + 1)
            s_t3h.wait(v, c + 1)
            for half in range(2):
                hs = slice(half * 512, half * 512 + VW)
                ins = v.tensor_scalar(out=tE[:, hs], in0=pb[2 + half][:, 0:VW], scalar1=bhc(c),
                                      scalar2=0.0, op0=ALU.add, op1=ALU.min)
            ctx.edge("vector", v, ins)
            ins = v.tensor_sub(out=tE[:, 0:FW], in0=tE[:, 0:FW], in1=tB[:, 0:FW])
            s_p2.inc(ins)
            s_p1.wait(v, c + 1)
            for half in range(2):
                hs = slice(half * 512, half * 512 + VW)
                ins = v.tensor_scalar(out=tG[:, hs], in0=pb[2 + half][:, 0:VW], scalar1=bhc(c),
                                      scalar2=0.0, op0=ALU.add, op1=ALU.is_lt)
            ctx.edge("vector", v, ins)
            ins = v.tensor_sub(out=tE[:, 0:FW], in0=tE[:, 0:FW], in1=tB[:, 0:FW])
            ctx.edge("vector", v, ins)
            ins = v.tensor_mul(out=tE[:, 0:FW], in0=tE[:, 0:FW], in1=tG[:, 0:FW])
            ctx.edge("vector", v, ins)
            ins = v.tensor_add(out=tC[:, 0:FW], in0=tC[:, 0:FW], in1=tE[:, 0:FW])
            ctx.edge("vector", v, ins)
            ins = v.tensor_add(out=tC[:, 0:FW], in0=tC[:, 0:FW], in1=tB[:, 0:FW])
            ctx.edge("vector", v, ins)
            VW = VWc
            s_asp.wait(v, c + 1)
            for half in range(2):
                ins = v.tensor_sub(out=usl(UB, c, half * 512, VW), in0=tC[:, half * 512:half * 512 + VW],
                                   in1=pb[4 + half][:, 0:VW])
            ctx.edge("vector", v, ins)
            for half in range(2):
                ins = v.tensor_copy(out=usl(AB, c, half * 512, VW), in_=pb[4 + half][:, 0:VW])
            s_u.inc(ins)
            s_csum.wait(v, c + 1)
            s_asp.wait(v, c + 1)
            v.tensor_add(out=prow[0:1, 0:VW], in0=prow[0:1, 0:VW], in1=pb[6][0:1, 0:VW])
            ins = v.tensor_add(out=prow[0:1, 512:512 + VW], in0=prow[0:1, 512:512 + VW], in1=pb[7][0:1, 0:VW])
            s_prow.inc(ins)
            if cfg.get("phases", 9) <= 0:
                s_scan.inc(ins) if False else None
                continue
            s_u.wait(v, s_u.n)
            s_u0c.wait(v, 1)
            s_ax.wait(v, 2 * c)
            if "scan" in cfg.get("skip", ()):
                ins = v.tensor_copy(out=tF[:, 0:1024], in_=usl(UB, c))
            else:
                ins = v.tensor_tensor_scan(out=tF[:, 0:1024], data0=usl(UB, c), data1=usl(UB, c),
                                           initial=u0c(c), op0=ALU.max, op1=ALU.max)
            ctx.edge("vector", v, ins)
            ins = v.tensor_sub(out=tG[:, 1:1024], in0=tF[:, 0:1023], in1=tF[:, 1:1024])
            ctx.edge("vector", v, ins)
            ins = v.tensor_scalar(out=tG[:, 0:1], in0=tF[:, 0:1], scalar1=u0c(c), scalar2=-1.0,
                                  op0=ALU.subtract, op1=ALU.mult)
            s_am.inc(ins)
            ins = v.tensor_sub(out=tE[:, 0:1024], in0=usl(UB, c), in1=tF[:, 0:1024])
            s_am.inc(ins)
            ins = v.tensor_scalar(out=pub[:, c:c + 1], in0=tF[:, 1023:1024], scalar1=f0,
                                  scalar2=None, op0=ALU.mult)
            ctx.edge("vector", v, ins)
            s_ax.wait(v, 2 * (c + 1))
            if "scan" in cfg.get("skip", ()):
                ins = v.tensor_copy(out=tF[:, 0:1024], in_=tG[:, 0:1024])
            else:
                ins = v.tensor_tensor_scan(out=tF[:, 0:1024], data0=tG[:, 0:1024], data1=tE[:, 0:1024],
                                           initial=1.0, op0=ALU.mult, op1=ALU.add)
            ctx.edge("vector", v, ins)
            ins = v.tensor_scalar(out=pub[:, 16 + c:17 + c], in0=tF[:, 1023:1024], scalar1=f0,
                                  scalar2=None, op0=ALU.mult)
            s_scan.inc(ins)
    ctx.on("vector", fv)
    ctx.barrier()
    if cfg.get("phases", 9) <= 1:
        return

    if cfg.get("dbg_u") is not None:
        sdbg = ctx.sem(f"gr{u}_dbg")
        def fdbgu(gp):
            ins = gp.dma_start(out=cfg["dbg_u"].rearrange("(c p) t -> p c t", p=128),
                               in_=UB[:].rearrange("p (c t) -> p c t", c=NCH))
            sdbg.inc(ins, 16)
            ins = gp.dma_start(out=cfg["dbg_as"].rearrange("(c p) t -> p c t", p=128),
                               in_=AB[:].rearrange("p (c t) -> p c t", c=NCH))
            sdbg.inc(ins, 16)
            gp.wait_ge(sdbg.h, sdbg.n)
        ctx.on("gpsimd", fdbgu)
        ctx.barrier()

    def fg1(gp):
        ins = gp.dma_start(out=d_pub.ap(), in_=pub[:])
        s_cc.inc(ins, 16)
        gp.wait_ge(s_cc.h, s_cc.n)
        ins = gp.collective_compute("AllReduce", ALU.add, replica_groups=pair_groups,
                                    ins=[d_pub.ap().opt()], outs=[d_pre.ap().opt()])
        s_ccop.inc(ins, 1)
        gp.wait_ge(s_ccop.h, s_ccop.n)
        ins = gp.dma_start(out=ppre[:], in_=d_pre.ap())
        s_cc.inc(ins, 16)
        gp.wait_ge(s_cc.h, s_cc.n)
    ctx.on("gpsimd", fg1)
    ctx.barrier()
    if cfg.get("phases", 9) <= 2:
        return

    def fv2(v):
        for c in range(NCH):
            ins = v.tensor_scalar(out=fneg[:, 1:2], in0=ppre[:, c:c + 1], scalar1=f1, scalar2=None, op0=ALU.mult)
            ctx.edge("vector", v, ins)
            ins = v.tensor_scalar(out=fneg[:, 1:2], in0=fneg[:, 1:2], scalar1=fneg[:, 0:1], scalar2=None, op0=ALU.add)
            ctx.edge("vector", v, ins)
            if c > 0:
                s_sq.wait(v, c)
                s_ax2.wait(v, 2 * c)
            ins = v.tensor_tensor_scan(out=tF[:, 0:1024], data0=usl(UB, c), data1=usl(UB, c),
                                       initial=u0c(c), op0=ALU.max, op1=ALU.max)
            ctx.edge("vector", v, ins)
            ins = v.tensor_sub(out=tG[:, 1:1024], in0=tF[:, 0:1023], in1=tF[:, 1:1024])
            ctx.edge("vector", v, ins)
            ins = v.tensor_scalar(out=tG[:, 0:1], in0=tF[:, 0:1], scalar1=u0c(c), scalar2=-1.0,
                                  op0=ALU.subtract, op1=ALU.mult)
            s_am2.inc(ins)
            ins = v.tensor_sub(out=tE[:, 0:1024], in0=usl(UB, c), in1=tF[:, 0:1024])
            s_am2.inc(ins)
            s_ax2.wait(v, 2 * (c + 1))
            ins = v.tensor_tensor_scan(out=tD[:, 0:1024], data0=tG[:, 0:1024], data1=tE[:, 0:1024],
                                       initial=1.0, op0=ALU.mult, op1=ALU.add)
            ctx.edge("vector", v, ins)
            ins = v.tensor_scalar(out=tG[:, 0:1024], in0=tF[:, 0:1024], scalar1=fneg[:, 1:2],
                                  scalar2=0.0, op0=ALU.subtract, op1=ALU.min)
            ctx.edge("vector", v, ins)
            ins = v.tensor_sub(out=tE[:, 0:1024], in0=tF[:, 0:1024], in1=tG[:, 0:1024])
            s_t2.inc(ins)
            s_e1.wait(v, c + 1)
            ins = v.tensor_mul(out=tD[:, 0:1024], in0=tD[:, 0:1024], in1=tG[:, 0:1024])
            s_v1.inc(ins)
            s_e2.wait(v, c + 1)
            ins = v.tensor_scalar(out=fneg[:, 1:2], in0=ppre[:, 16 + c:17 + c], scalar1=f1, scalar2=None, op0=ALU.mult)
            ctx.edge("vector", v, ins)
            ins = v.scalar_tensor_tensor(out=tD[:, 0:1024], in0=tB[:, 0:1024], scalar=fneg[:, 1:2],
                                         in1=tD[:, 0:1024], op0=ALU.mult, op1=ALU.add)
            s_vp2.inc(ins)
            s_lnv.wait(v, c + 1)
            ins = v.tensor_add(out=tE[:, 0:1024], in0=tE[:, 0:1024], in1=tD[:, 0:1024])
            ctx.edge("vector", v, ins)
            ins = v.tensor_add(out=usl(UB, c), in0=usl(AB, c), in1=tE[:, 0:1024])
            s_lh.inc(ins)
            s_lh.wait(v, s_lh.n)
            ins = v.reduce_sum(out=scol[:, c:c + 1], in_=usl(UB, c), axis=AX.X)
            ctx.edge("vector", v, ins)
            s_sq.wait(v, c + 1)
            ins = v.reduce_sum(out=scol[:, 16 + c:17 + c], in_=tE[:, 0:1024], axis=AX.X)
            s_st.inc(ins)
        s_st.wait(v, s_st.n)
        ins = v.scalar_tensor_tensor(out=scol[:, 0:16], in0=u0[:], scalar=f0, in1=scol[:, 0:16],
                                     op0=ALU.mult, op1=ALU.add)
        ctx.edge("vector", v, ins)
        ins = v.tensor_mul(out=tB[:, 0:16], in0=u0[:], in1=u0[:])
        ctx.edge("vector", v, ins)
        ins = v.scalar_tensor_tensor(out=scol[:, 16:32], in0=tB[:, 0:16], scalar=f0, in1=scol[:, 16:32],
                                     op0=ALU.mult, op1=ALU.add)
        s_st.inc(ins)
    ctx.on("vector", fv2)

    def fs3(se):
        for c in range(NCH):
            s_am2.wait(se, 2 * c + 1)
            ins = se.activation(out=tG[:, 0:1024], in_=tG[:, 0:1024], func=AF.Exp)
            s_ax2.inc(ins)
            s_am2.wait(se, 2 * c + 2)
            ins = se.activation(out=tE[:, 0:1024], in_=tE[:, 0:1024], func=AF.Exp)
            s_ax2.inc(ins)
            s_t2.wait(se, c + 1)
            ins = se.activation(out=tG[:, 0:1024], in_=tG[:, 0:1024], func=AF.Exp)
            s_e1.inc(ins)
            s_v1.wait(se, c + 1)
            ins = se.activation(out=tB[:, 0:1024], in_=tE[:, 0:1024], func=AF.Exp, scale=-1.0,
                                bias=fneg[:, 1:2])
            s_e2.inc(ins)
            s_vp2.wait(se, c + 1)
            ins = se.activation(out=tD[:, 0:1024], in_=tD[:, 0:1024], func=AF.Ln)
            s_lnv.inc(ins)
            s_lh.wait(se, c + 1)
            ins = se.activation(out=tE[:, 0:1024], in_=usl(UB, c), func=AF.Square)
            s_sq.inc(ins)
    ctx.on("scalar", fs3)
    ctx.barrier()
    if cfg.get("phases", 9) <= 3:
        return

    def ft2(te):
        ins = te.matmul(pb[0][0:1, 0:32], onesc_f[:], scol[:], start=True, stop=True)
        s_sr.inc(ins)
    ctx.on("tensor", ft2)

    def fv3(v):
        s_sr.wait(v, 1)
        ins = v.reduce_sum(out=st2[0:1, 0:1], in_=pb[0][0:1, 0:16], axis=AX.X)
        ctx.edge("vector", v, ins)
        ins = v.reduce_sum(out=st2[0:1, 1:2], in_=pb[0][0:1, 16:32], axis=AX.X)
        s_sred.inc(ins)
    ctx.on("vector", fv3)

    def fg2(gp):
        gp.wait_ge(s_sred.h, 1)
        ins = gp.dma_start(out=d_st.ap(), in_=st2[0:1, 0:2])
        s_cc.inc(ins, 16)
        gp.wait_ge(s_cc.h, s_cc.n)
        ins = gp.collective_compute("AllReduce", ALU.add, replica_groups=all_groups,
                                    ins=[d_st.ap().opt()], outs=[d_st2.ap().opt()])
        s_ccop.inc(ins, 1)
        gp.wait_ge(s_ccop.h, s_ccop.n)
        ins = gp.dma_start(out=st2[0:1, 2:4], in_=d_st2.ap())
        s_cc.inc(ins, 16)
        gp.wait_ge(s_cc.h, s_cc.n)
    ctx.on("gpsimd", fg2)
    ctx.barrier()

    def fv4(v):
        ins = v.tensor_scalar_mul(out=st2[0:1, 4:5], in0=st2[0:1, 2:3], scalar1=1.0 / NSTAT)
        ctx.edge("vector", v, ins)
        ins = v.tensor_mul(out=st2[0:1, 5:6], in0=st2[0:1, 4:5], in1=st2[0:1, 2:3])
        ctx.edge("vector", v, ins)
        ins = v.tensor_sub(out=st2[0:1, 5:6], in0=st2[0:1, 3:4], in1=st2[0:1, 5:6])
        ctx.edge("vector", v, ins)
        ins = v.tensor_scalar_mul(out=st2[0:1, 5:6], in0=st2[0:1, 5:6], scalar1=1.0 / (NSTAT - 1.0))
        s_gs.inc(ins)
    ctx.on("vector", fv4)

    def fs4(se):
        s_gs.wait(se, 1)
        ins = se.activation(out=st2[0:1, 6:7], in_=st2[0:1, 5:6], func=AF.Ln)
        ctx.edge("scalar", se, ins)
        ins = se.activation(out=st2[0:1, 6:7], in_=st2[0:1, 6:7], func=AF.Exp, scale=-0.5)
        s_gr.inc(ins)
    ctx.on("scalar", fs4)

    def ft3(te):
        s_gr.wait(te, 1)
        te.matmul(pb[1][:, 0:1], ctx.ones_row[:], st2[0:1, 4:5], start=True, stop=True,
                  skip_group_check=True)
        ins = te.matmul(pb[1][:, 1:2], ctx.ones_row[:], st2[0:1, 6:7], start=True, stop=True,
                        skip_group_check=True)
        s_gb.inc(ins)
    ctx.on("tensor", ft3)

    def fv5(v):
        s_gb.wait(v, 1)
        ins = v.tensor_copy(out=tF[:, 0:2], in_=pb[1][:, 0:2])
        ctx.edge("vector", v, ins)
        ins = v.tensor_mul(out=tF[:, 2:3], in0=tF[:, 0:1], in1=tF[:, 1:2])
        ctx.edge("vector", v, ins)
        ins = v.tensor_scalar_mul(out=tF[:, 2:3], in0=tF[:, 2:3], scalar1=-1.0)
        s_mr.inc(ins)
    ctx.on("vector", fv5)
    ctx.barrier()
    if cfg.get("phases", 9) <= 4:
        return

    def fs5(se):
        for c in range(NCH):
            if c > 0:
                s_v.wait(se, c)
            ins = se.activation(out=tB[:, 0:1024], in_=usl(UB, c), func=AF.Exp,
                                scale=tF[:, 1:2], bias=tF[:, 2:3])
            s_hx.inc(ins)
            s_v.wait(se, c + 1)
            if c > 0:
                s_vs2.wait(se, c)
            ins = se.activation(out=tC[:, 0:1024], in_=usl(AB, c), func=AF.Square)
            s_vsq.inc(ins)
    ctx.on("scalar", fs5)

    def fv6(v):
        for c in range(NCH):
            s_hx.wait(v, c + 1)
            ins = v.tensor_add(out=usl(AB, c), in0=tB[:, 0:1024],
                               in1=xn_sb[:, c * ntok:(c + 1) * ntok])
            s_v.inc(ins)
    ctx.on("vector", fv6)

    def ft4(te):
        for c in range(NCH):
            s_v.wait(te, c + 1)
            for half in range(2):
                ins = te.matmul(pb[2 + half][0:1, 0:512], onesc_f[:],
                                usl(AB, c, half * 512, 512), start=(c == 0), stop=(c == NCH - 1))
            s_vs1.inc(ins)
            s_vsq.wait(te, c + 1)
            for half in range(2):
                ins = te.matmul(pb[6 + half][0:1, 0:512], onesc_f[:],
                                tC[:, half * 512:half * 512 + 512], start=(c == 0), stop=(c == NCH - 1))
            s_vs2.inc(ins)
        s_rowC.wait(te, 1)
        te.matmul(pb[4][:, :], ctx.ones_row[:], rows[0:1, 0:512], start=True, stop=True,
                  skip_group_check=True)
        ins = te.matmul(pb[5][:, :], ctx.ones_row[:], rows[0:1, 512:1024], start=True, stop=True,
                        skip_group_check=True)
        te.matmul(pb[0][:, :], ctx.ones_row[:], prow[0:1, 0:512], start=True, stop=True,
                  skip_group_check=True)
        ins = te.matmul(pb[1][:, :], ctx.ones_row[:], prow[0:1, 512:1024], start=True, stop=True,
                        skip_group_check=True)
        s_bc2.inc(ins)
    ctx.on("tensor", ft4)

    def fs6(se):
        s_vs1.wait(se, NCH)
        se.activation(out=rows[0:1, 0:512], in_=pb[2][0:1, 0:512], func=AF.Copy, scale=1.0 / D)
        ins = se.activation(out=rows[0:1, 512:1024], in_=pb[3][0:1, 0:512], func=AF.Copy, scale=1.0 / D)
        s_vs2.wait(se, NCH)
        se.activation(out=prow[0:1, 0:512], in_=pb[6][0:1, 0:512], func=AF.Copy, scale=1.0 / D)
        ins = se.activation(out=prow[0:1, 512:1024], in_=pb[7][0:1, 0:512], func=AF.Copy, scale=1.0 / D)
        s_rowA.inc(ins)
    ctx.on("scalar", fs6)

    def fv8(v):
        s_rowA.wait(v, 1)
        ins = v.tensor_mul(out=tB[0:1, 0:1024], in0=rows[0:1, 0:1024], in1=rows[0:1, 0:1024])
        ctx.edge("vector", v, ins)
        ins = v.tensor_sub(out=prow[0:1, 0:1024], in0=prow[0:1, 0:1024], in1=tB[0:1, 0:1024])
        s_rowB.inc(ins)
    ctx.on("vector", fv8)

    def fs7(se):
        s_rowB.wait(se, 1)
        ins = se.activation(out=prow[0:1, 0:1024], in_=prow[0:1, 0:1024], func=AF.Ln,
                            bias=ctx.cconst(LN_EPS)[0:1, 0:1])
        ctx.edge("scalar", se, ins)
        ins = se.activation(out=prow[0:1, 0:1024], in_=prow[0:1, 0:1024], func=AF.Exp, scale=-0.5)
        s_rowC.inc(ins)
    ctx.on("scalar", fs7)

    def fg3(gp):
        for c in range(NCH):
            if c >= 1:
                gp.wait_ge(s_xo[0].h, 16 * c)
            ins = x_load(gp, c, xs_sb[:, 0:1024])
            s_xin[0].inc(ins, 16)
            gp.wait_ge(s_xap.h, c + 1)
            ins = x_store(gp, c, xs_sb[:, 0:1024])
            s_xo[0].inc(ins, 16)
        gp.wait_ge(s_xo[0].h, s_xo[0].n)
    ctx.on("gpsimd", fg3)

    def fv9(v):
        s_bc2.wait(v, 1)
        for c in range(NCH):
            s_xin[0].wait(v, 16 * (c + 1))
            lnw_c = cols[:, 32 + c:33 + c]
            lnb_c = cols[:, 48 + c:49 + c]
            for half in range(2):
                hs = slice(half * 512, half * 512 + 512)
                ins = v.tensor_sub(out=tB[:, hs], in0=usl(AB, c, half * 512, 512),
                                   in1=pb[4 + half][:, :])
            ctx.edge("vector", v, ins)
            for half in range(2):
                hs = slice(half * 512, half * 512 + 512)
                ins = v.tensor_mul(out=tB[:, hs], in0=tB[:, hs], in1=pb[0 + half][:, :])
            ctx.edge("vector", v, ins)
            ins = v.tensor_scalar(out=tB[:, 0:1024], in0=tB[:, 0:1024], scalar1=lnw_c, scalar2=lnb_c,
                                  op0=ALU.mult, op1=ALU.add)
            ctx.edge("vector", v, ins)
            ins = v.tensor_add(out=xs_sb[:, 0:1024], in0=xs_sb[:, 0:1024], in1=tB[:, 0:1024])
            s_xap.inc(ins)
    ctx.on("vector", fv9)
import ml_dtypes

BF16NP = ml_dtypes.bfloat16

_PROGS = {}

_CASTS = [
    ("l0a_wqT", 2048, 2048, "f4"), ("l0a_wkT", 2048, 2048, "f4"),
    ("l0a_wvT", 2048, 2048, "f4"), ("l0a_woT", 2048, 2048, "f4"),
    ("l0b_pack", 2048, 4096, "f4"), ("l0b_woT", 2048, 2048, "f4"),
    ("f0_wg", 2048, 5632, "f4"), ("f0_wu", 2048, 5632, "f4"),
    ("f0_wd", 5632, 2048, "f4"), ("f0_wr", 2048, 8, "f4"),
    ("f0_agu", 2048, 256, "f4"), ("f0_bgu", 16, 2 * 44 * 2048, "f4"),
    ("f0_adt", 128, 44 * 128, "f4"), ("f0_bdt", 16, 8 * 2048, "f4"),
    ("l1a_wqT", 2048, 2048, "f4"), ("l1a_wkT", 2048, 512, "f4"),
    ("l1a_wvT", 2048, 512, "f4"),
    ("f1_wg", 2048, 8192, "i1"), ("f1_wu", 2048, 8192, "i1"),
    ("f1_wd", 8192, 2048, "i1"),
    ("b_wzT", 2048, 2048, "f4"), ("b_whT", 2048, 2048, "f4"),
]
CTILE = 4096


def build_prep():
    ctx = Ctx()
    d_in = {}
    d_out = {}
    tiles = []
    for nm, R, C, dt in _CASTS:
        m = R * C // NCORES // 128
        d_in[nm] = ctx.inp(nm + "_in", [128, m], I8 if dt == "i1" else F32)
        d_out[nm] = ctx.outp(nm + "_bf", [128, m], BF16)
        off = 0
        while off < m:
            w = min(CTILE, m - off)
            tiles.append((nm, dt, off, w))
            off += w
    d_x0 = ctx.inp("x0", [D, TC], F32)
    d_xn0 = ctx.outp("xn0", [D, TC], BF16)
    x_sb = ctx.sb("x_sb", [128, NCH * TC], F32)
    xn_sb = ctx.sb("xn_sb", [128, NCH * TC], BF16)
    ibufs = [ctx.sb(f"cif{p}", [128, CTILE], F32) for p in range(2)]
    i8bufs = [ctx.sb(f"ci8{p}", [128, CTILE], I8) for p in range(2)]
    obufs = [ctx.sb(f"cob{p}", [128, CTILE], BF16) for p in range(2)]
    ld = ctx.sem("ld")
    st = ctx.sem("st")
    s_ci = ctx.sem("cast_i")
    s_cc = ctx.sem("cast_c")
    s_co = ctx.sem("cast_o")
    with ctx.nc.Block() as block:
        ctx.block = block

        def fload(sy):
            ins = sy.dma_start(out=x_sb[:].rearrange("p (c n) -> p c n", c=NCH),
                               in_=d_x0.rearrange("(c p) n -> p c n", p=128))
            ld.inc(ins, 16)
        ctx.on("sync", fload)
        ctx.cconst(LN_EPS)
        ctx.setup_common()

        def fw(v):
            ld.wait_all(v)
        ctx.on("vector", fw)
        ctx.barrier()
        emit_ln(ctx, x_sb, xn_sb, TC)
        ctx.barrier()

        def fxn(g):
            ins = g.dma_start(out=d_xn0.rearrange("(c p) n -> p c n", p=128),
                              in_=xn_sb[:].rearrange("p (c n) -> p c n", c=NCH))
            st.inc(ins, 16)
        ctx.on("gpsimd", fxn)

        def fci(sy):
            for k, (nm, dt, off, w) in enumerate(tiles):
                par = k % 2
                if k >= 2:
                    sy.wait_ge(s_cc.h, k - 1)
                buf = (i8bufs if dt == "i1" else ibufs)[par]
                ins = sy.dma_start(out=buf[:, 0:w], in_=d_in[nm][:, off:off + w])
                s_ci.inc(ins, 16)
        ctx.on("sync", fci)

        def fcc(v):
            for k, (nm, dt, off, w) in enumerate(tiles):
                par = k % 2
                s_ci.wait(v, 16 * (k + 1))
                if k >= 2:
                    s_co.wait(v, 16 * (k - 1))
                buf = (i8bufs if dt == "i1" else ibufs)[par]
                ins = v.tensor_copy(out=obufs[par][:, 0:w], in_=buf[:, 0:w])
                s_cc.inc(ins)
        ctx.on("vector", fcc)

        def fco(g):
            for k, (nm, dt, off, w) in enumerate(tiles):
                g.wait_ge(s_cc.h, k + 1)
                ins = g.dma_start(out=d_out[nm][:, off:off + w], in_=obufs[k % 2][:, 0:w])
                s_co.inc(ins, 16)
            g.wait_ge(s_co.h, s_co.n)
            g.wait_ge(st.h, 16)
        ctx.on("gpsimd", fco)
    return ctx.nc


def build_attn(kind):
    if kind == "l0a":
        NHc, NKVc, grp, pad, window, oproj, bias, rope = 8, 8, 1, 0, None, True, True, False
    elif kind == "l0b":
        NHc, NKVc, grp, pad, window, oproj, bias, rope = 8, 4, 2, 1, 1024, True, False, True
    else:
        NHc, NKVc, grp, pad, window, oproj, bias, rope = 8, 2, 4, 0, None, False, True, False
    DVC = NKVc * 128
    ctx = Ctx()
    Tp = T + pad
    d_xn0 = ctx.inp("xn_h0", [D, TC], BF16)
    d_xn1 = ctx.inp("xn_h1", [D, TC], BF16)
    d_wqT = ctx.inp("wqT", [D, NHc * 128], BF16)
    d_wkT = ctx.inp("wkT", [D, DVC], BF16)
    d_wvT = ctx.inp("wvT", [D, DVC], BF16)
    d_woT = ctx.inp("woT", [NHc * 128, D], BF16) if oproj else None
    if bias:
        d_bq = ctx.inp("bqc", [128, NHc])
        d_bk = ctx.inp("bkc", [128, NKVc])
        d_bv = ctx.inp("bvbc", [128, DVC])
    d_msk = ctx.inp("masks", [128, 4 * 512], BF16)
    if window:
        d_mskl = ctx.inp("masks_lo", [128, 4 * 512], BF16)
        d_c2 = ctx.inp("C2", [128, T], BF16)
        d_s2 = ctx.inp("S2", [128, T], BF16)
        d_ck = ctx.inp("convk", [128, 2 * NKVc])
        d_w0v = ctx.inp("w0v", [128, DVC])
        d_w1v = ctx.inp("w1v", [128, DVC])
    if oproj:
        d_y = ctx.outp("y_part", [D, T], BF16)
    else:
        d_o = ctx.outp("o_part", [NHc * 128, T], BF16)

    xn_sb = ctx.sb("xn_sb", [128, NCH * Tp], BF16)
    mk_sb = ctx.sb("mk_sb", [128, 4 * 512], BF16)
    if bias:
        bq_sb = ctx.sb("bq_sb", [128, NHc], F32)
        bk_sb = ctx.sb("bk_sb", [128, NKVc], F32)
        bv_sb = ctx.sb("bv_sb", [128, DVC], F32)
    if window:
        mkl_sb = ctx.sb("mkl_sb", [128, 4 * 512], BF16)
        c2_sb = ctx.sb("c2_sb", [128, T], BF16)
        s2_sb = ctx.sb("s2_sb", [128, T], BF16)
        ck_sb = ctx.sb("ck_sb", [128, 2 * NKVc], F32)
        w0v_sb = ctx.sb("w0v_sb", [128, DVC], F32)
        w1v_sb = ctx.sb("w1v_sb", [128, DVC], F32)
    ld = ctx.sem("ld0")
    ms = ctx.sem("ms0")
    st = ctx.sem("st0")
    with ctx.nc.Block() as block:
        ctx.block = block
        if pad:
            def fz(v):
                ins = v.memset(xn_sb[:], 0.0)
                ms.inc(ins)
            ctx.on("vector", fz)

        def fload(sy):
            if pad:
                ms.wait(sy, 1)
            xv = xn_sb[:].rearrange("p (c t) -> p c t", c=NCH)
            pairs = [
                (xv[:, :, pad:TC + pad], d_xn0.rearrange("(c p) t -> p c t", p=128)),
                (xv[:, :, TC + pad:Tp], d_xn1.rearrange("(c p) t -> p c t", p=128)),
                (mk_sb[:], d_msk)]
            if bias:
                pairs += [(bq_sb[:], d_bq), (bk_sb[:], d_bk), (bv_sb[:], d_bv)]
            if window:
                pairs += [(mkl_sb[:], d_mskl), (c2_sb[:], d_c2), (s2_sb[:], d_s2),
                          (ck_sb[:], d_ck), (w0v_sb[:], d_w0v), (w1v_sb[:], d_w1v)]
            for dst, src in pairs:
                ins = sy.dma_start(out=dst, in_=src)
                ld.inc(ins, 16)
        ctx.on("sync", fload)
        ctx.setup_common()

        def fw(v):
            ld.wait_all(v)
        ctx.on("vector", fw)
        ctx.barrier()
        cfg = dict(T=T, NHc=NHc, NKVc=NKVc, HD=128, group=grp, pad=pad,
                   xn_sb=xn_sb, wqT=d_wqT, wkT=d_wkT, wvT=d_wvT,
                   woT=d_woT,
                   bq_cols=bq_sb if bias else None,
                   bk_cols=bk_sb if bias else None,
                   bv_bc=bv_sb if bias else None,
                   masks=mk_sb)
        if window:
            cfg.update(window=window, masks_lo=mkl_sb, rope=(c2_sb, s2_sb),
                       convk_cols=ck_sb, convv=(w0v_sb, w1v_sb))
        if oproj:
            cfg["y_out"] = d_y
        o_sb = emit_attention(ctx, cfg)
        ctx.barrier()
        if not oproj:
            def fstore(g):
                ins = g.dma_start(out=d_o.rearrange("(h p) t -> p h t", p=128),
                                  in_=o_sb.rearrange("p (h t) -> p h t", h=NHc))
                st.inc(ins, 16)
                st.wait_all(g)
            ctx.on("gpsimd", fstore)
    return ctx.nc


def build_ffn(which):
    ctx = Ctx()
    d_x = ctx.inp("x", [D, TC])
    d_pa = ctx.inp("pLO", [1024, TC], BF16)
    d_pb = ctx.inp("pHI", [1024, TC], BF16)
    d_wg = ctx.inp("wg", [D, 8192], BF16)
    d_wu = ctx.inp("wu", [D, 8192], BF16)
    d_wd = ctx.inp("wd", [8192, D], BF16)
    d_sg = ctx.inp("sg", [128, 64])
    d_su = ctx.inp("su", [128, 64])
    d_sd = ctx.inp("sd", [128, NCH])
    d_xo = ctx.outp("xout", [D, TC])
    d_xn = ctx.outp("xnout", [D, TC], BF16)

    x_sb = ctx.sb("x_sb", [128, NCH * TC], F32)
    xn_sb = ctx.sb("xn_sb", [128, NCH * TC], BF16)
    sg_sb = ctx.sb("sg_sb", [128, 64], F32)
    su_sb = ctx.sb("su_sb", [128, 64], F32)
    sd_sb = ctx.sb("sd_sb", [128, NCH], F32)
    ld = ctx.sem("ld0")
    sa = ctx.sem("sa0")
    st = ctx.sem("st0")
    with ctx.nc.Block() as block:
        ctx.block = block

        def fload(sy):
            pairs = [(x_sb[:].rearrange("p (c t) -> p c t", c=NCH),
                      d_x.rearrange("(c p) t -> p c t", p=128)),
                     (xn_sb[:, 0:8 * TC].rearrange("p (c t) -> p c t", c=8),
                      d_pa.rearrange("(c p) t -> p c t", p=128)),
                     (xn_sb[:, 8 * TC:16 * TC].rearrange("p (c t) -> p c t", c=8),
                      d_pb.rearrange("(c p) t -> p c t", p=128)),
                     (sg_sb[:], d_sg), (su_sb[:], d_su), (sd_sb[:], d_sd)]
            for dst, src in pairs:
                ins = sy.dma_start(out=dst, in_=src)
                ld.inc(ins, 16)
        ctx.on("sync", fload)
        ctx.cconst(LN_EPS)
        ctx.setup_common()

        def fadd1(v):
            ld.wait_all(v)
            ins = v.tensor_add(out=x_sb[:], in0=x_sb[:], in1=xn_sb[:])
            sa.inc(ins)
        ctx.on("vector", fadd1)
        ctx.barrier()
        emit_ln(ctx, x_sb, xn_sb, TC)
        ctx.barrier()
        emit_l1ffn(ctx, xn_sb[:], x_sb[:], TC, dict(
            wg=d_wg, wu=d_wu, wd=d_wd, sg_sb=sg_sb, su_sb=su_sb, sd_sb=sd_sb))
        ctx.barrier()
        emit_ln(ctx, x_sb, xn_sb, TC)
        ctx.barrier()

        def fstore(g):
            for dst, src in [
                    (d_xo.rearrange("(c p) t -> p c t", p=128),
                     x_sb[:].rearrange("p (c t) -> p c t", c=NCH)),
                    (d_xn.rearrange("(c p) t -> p c t", p=128),
                     xn_sb[:].rearrange("p (c t) -> p c t", c=NCH))]:
                ins = g.dma_start(out=dst, in_=src)
                st.inc(ins, 16)
            st.wait_all(g)
        ctx.on("gpsimd", fstore)
    return ctx.nc


def build_lnp():
    ctx = Ctx()
    d_x = ctx.inp("x", [D, TC])
    d_pa = ctx.inp("yA", [D, TC], BF16)
    d_pb = ctx.inp("yB", [D, TC], BF16)
    d_bo = ctx.inp("bo_cols", [128, NCH])
    d_xo = ctx.outp("xout", [D, TC])
    d_xn = ctx.outp("xnout", [D, TC], BF16)
    x_sb = ctx.sb("x_sb", [128, NCH * TC], F32)
    xn_sb = ctx.sb("xn_sb", [128, NCH * TC], BF16)
    bo_sb = ctx.sb("bo_sb", [128, NCH], F32)
    ld = ctx.sem("ld0")
    ld2 = ctx.sem("ld2")
    sa = ctx.sem("sa0")
    st = ctx.sem("st0")
    with ctx.nc.Block() as block:
        ctx.block = block

        def fload(sy):
            for dst, src in [(x_sb[:].rearrange("p (c t) -> p c t", c=NCH),
                              d_x.rearrange("(c p) t -> p c t", p=128)),
                             (xn_sb[:].rearrange("p (c t) -> p c t", c=NCH),
                              d_pa.rearrange("(c p) t -> p c t", p=128)),
                             (bo_sb[:], d_bo)]:
                ins = sy.dma_start(out=dst, in_=src)
                ld.inc(ins, 16)
        ctx.on("sync", fload)
        ctx.cconst(LN_EPS)
        ctx.setup_common()

        def fadd1(v):
            ld.wait_all(v)
            ins = v.tensor_add(out=x_sb[:], in0=x_sb[:], in1=xn_sb[:])
            sa.inc(ins)
        ctx.on("vector", fadd1)

        def fload2(sy):
            sa.wait(sy, 1)
            ins = sy.dma_start(out=xn_sb[:].rearrange("p (c t) -> p c t", c=NCH),
                               in_=d_pb.rearrange("(c p) t -> p c t", p=128))
            ld2.inc(ins, 16)
        ctx.on("sync", fload2)

        def fadd2(v):
            ld2.wait_all(v)
            ins = v.tensor_add(out=x_sb[:], in0=x_sb[:], in1=xn_sb[:])
            ctx.edge("vector", v, ins)
            for c in range(NCH):
                ins = v.tensor_scalar(out=x_sb[:, c * TC:(c + 1) * TC],
                                      in0=x_sb[:, c * TC:(c + 1) * TC],
                                      scalar1=bo_sb[:, c:c + 1], scalar2=None, op0=ALU.add)
                ctx.edge("vector", v, ins)
        ctx.on("vector", fadd2)
        ctx.barrier()
        emit_ln(ctx, x_sb, xn_sb, TC)
        ctx.barrier()

        def fstore(g):
            for dst, src in [
                    (d_xo.rearrange("(c p) t -> p c t", p=128),
                     x_sb[:].rearrange("p (c t) -> p c t", c=NCH)),
                    (d_xn.rearrange("(c p) t -> p c t", p=128),
                     xn_sb[:].rearrange("p (c t) -> p c t", c=NCH))]:
                ins = g.dma_start(out=dst, in_=src)
                st.inc(ins, 16)
            st.wait_all(g)
        ctx.on("gpsimd", fstore)
    return ctx.nc


def build_moe():
    ctx = Ctx()
    d_x = ctx.inp("x", [D, TC])
    d_xnin = ctx.inp("xn", [D, TC], BF16)
    d_wg = ctx.inp("wg", [D, 5632], BF16)
    d_wu = ctx.inp("wu", [D, 5632], BF16)
    d_wd = ctx.inp("wd", [5632, D], BF16)
    d_wr = ctx.inp("wr", [D, 8], BF16)
    d_agu = ctx.inp("agu", [D, 256], BF16)
    d_bgu = ctx.inp("bgu", [16, 2 * 44 * 2048], BF16)
    d_adt = ctx.inp("adt", [128, 44 * 128], BF16)
    d_bdt = ctx.inp("bdt", [16, 8 * 2048], BF16)
    d_id = ctx.inp("identb", [128, 128], BF16)
    d_oh8 = ctx.inp("oh8", [8, 128], BF16)
    d_xo = ctx.outp("xout", [D, TC])
    x_sb = ctx.sb("x_sb", [128, NCH * TC], F32)
    xn_sb = ctx.sb("xn_sb", [128, NCH * TC], BF16)
    id_sb = ctx.sb("id_sb", [128, 128], BF16)
    oh8_sb = ctx.sb("oh8_sb", [8, 128], BF16)
    ld = ctx.sem("ld0")
    st = ctx.sem("st0")
    with ctx.nc.Block() as block:
        ctx.block = block

        def fload(sy):
            for dst, src in [
                    (x_sb[:].rearrange("p (c t) -> p c t", c=NCH),
                     d_x.rearrange("(c p) t -> p c t", p=128)),
                    (xn_sb[:].rearrange("p (c t) -> p c t", c=NCH),
                     d_xnin.rearrange("(c p) t -> p c t", p=128)),
                    (id_sb[:], d_id), (oh8_sb[:], d_oh8)]:
                ins = sy.dma_start(out=dst, in_=src)
                ld.inc(ins, 16)
        ctx.on("sync", fload)
        ctx.setup_common()

        def fw(v):
            ld.wait_all(v)
        ctx.on("vector", fw)
        ctx.barrier()
        emit_l0ffn(ctx, xn_sb[:], x_sb[:], TC, dict(
            wg=d_wg, wu=d_wu, wd=d_wd, wr=d_wr, agu=d_agu, bgu=d_bgu,
            adt=d_adt, bdt=d_bdt, ident_b=id_sb[:], oh8=oh8_sb[:]))
        ctx.barrier()

        def fstore(g):
            ins = g.dma_start(out=d_xo.rearrange("(c p) t -> p c t", p=128),
                              in_=x_sb[:].rearrange("p (c t) -> p c t", c=NCH))
            st.inc(ins, 16)
            st.wait_all(g)
        ctx.on("gpsimd", fstore)
    return ctx.nc


def build_lmp():
    ctx = Ctx()
    d_x = ctx.inp("x", [D, TC])
    d_pa = ctx.inp("yA", [D, TC], BF16)
    d_pb = ctx.inp("yB", [D, TC], BF16)
    d_bo = ctx.inp("bo_cols", [128, NCH])
    d_wg = ctx.inp("wg", [D, 5632], BF16)
    d_wu = ctx.inp("wu", [D, 5632], BF16)
    d_wd = ctx.inp("wd", [5632, D], BF16)
    d_wr = ctx.inp("wr", [D, 8], BF16)
    d_agu = ctx.inp("agu", [D, 256], BF16)
    d_bgu = ctx.inp("bgu", [16, 2 * 44 * 2048], BF16)
    d_adt = ctx.inp("adt", [128, 44 * 128], BF16)
    d_bdt = ctx.inp("bdt", [16, 8 * 2048], BF16)
    d_id = ctx.inp("identb", [128, 128], BF16)
    d_oh8 = ctx.inp("oh8", [8, 128], BF16)
    d_xo = ctx.outp("xout", [D, TC])
    d_xn = ctx.outp("xnout", [D, TC], BF16)
    x_sb = ctx.sb("x_sb", [128, NCH * TC], F32)
    xn_sb = ctx.sb("xn_sb", [128, NCH * TC], BF16)
    bo_sb = ctx.sb("bo_sb", [128, NCH], F32)
    id_sb = ctx.sb("id_sb", [128, 128], BF16)
    oh8_sb = ctx.sb("oh8_sb", [8, 128], BF16)
    ld = ctx.sem("ld0")
    ld2 = ctx.sem("ld2")
    sa = ctx.sem("sa0")
    st = ctx.sem("st0")
    with ctx.nc.Block() as block:
        ctx.block = block

        def fload(sy):
            for dst, src in [(x_sb[:].rearrange("p (c t) -> p c t", c=NCH),
                              d_x.rearrange("(c p) t -> p c t", p=128)),
                             (xn_sb[:].rearrange("p (c t) -> p c t", c=NCH),
                              d_pa.rearrange("(c p) t -> p c t", p=128)),
                             (bo_sb[:], d_bo), (id_sb[:], d_id), (oh8_sb[:], d_oh8)]:
                ins = sy.dma_start(out=dst, in_=src)
                ld.inc(ins, 16)
        ctx.on("sync", fload)
        ctx.cconst(LN_EPS)
        ctx.setup_common()

        def fadd1(v):
            ld.wait_all(v)
            ins = v.tensor_add(out=x_sb[:], in0=x_sb[:], in1=xn_sb[:])
            sa.inc(ins)
        ctx.on("vector", fadd1)

        def fload2(sy):
            sa.wait(sy, 1)
            ins = sy.dma_start(out=xn_sb[:].rearrange("p (c t) -> p c t", c=NCH),
                               in_=d_pb.rearrange("(c p) t -> p c t", p=128))
            ld2.inc(ins, 16)
        ctx.on("sync", fload2)

        def fadd2(v):
            ld2.wait_all(v)
            ins = v.tensor_add(out=x_sb[:], in0=x_sb[:], in1=xn_sb[:])
            ctx.edge("vector", v, ins)
            for c in range(NCH):
                ins = v.tensor_scalar(out=x_sb[:, c * TC:(c + 1) * TC],
                                      in0=x_sb[:, c * TC:(c + 1) * TC],
                                      scalar1=bo_sb[:, c:c + 1], scalar2=None, op0=ALU.add)
                ctx.edge("vector", v, ins)
        ctx.on("vector", fadd2)
        ctx.barrier()
        ctx.push_scope()
        emit_ln(ctx, x_sb, xn_sb, TC)
        ctx.barrier()
        ctx.pop_scope()
        ctx.push_scope()
        emit_l0ffn(ctx, xn_sb[:], x_sb[:], TC, dict(
            wg=d_wg, wu=d_wu, wd=d_wd, wr=d_wr, agu=d_agu, bgu=d_bgu,
            adt=d_adt, bdt=d_bdt, ident_b=id_sb[:], oh8=oh8_sb[:]))
        ctx.barrier()
        ctx.pop_scope()
        ctx.push_scope()
        emit_ln(ctx, x_sb, xn_sb, TC)
        ctx.barrier()
        ctx.pop_scope()

        def fstore(g):
            for dst, src in [
                    (d_xo.rearrange("(c p) t -> p c t", p=128),
                     x_sb[:].rearrange("p (c t) -> p c t", c=NCH)),
                    (d_xn.rearrange("(c p) t -> p c t", p=128),
                     xn_sb[:].rearrange("p (c t) -> p c t", c=NCH))]:
                ins = g.dma_start(out=dst, in_=src)
                st.inc(ins, 16)
            st.wait_all(g)
        ctx.on("gpsimd", fstore)
    return ctx.nc


def build_fll():
    ctx = Ctx()
    d_x = ctx.inp("x", [D, TC])
    d_pa = ctx.inp("pLO", [1024, TC], BF16)
    d_pb = ctx.inp("pHI", [1024, TC], BF16)
    d_wg = ctx.inp("wg", [D, 8192], BF16)
    d_wu = ctx.inp("wu", [D, 8192], BF16)
    d_wd = ctx.inp("wd", [8192, D], BF16)
    d_sg = ctx.inp("sg", [128, 64])
    d_su = ctx.inp("su", [128, 64])
    d_sd = ctx.inp("sd", [128, NCH])
    d_wz = ctx.inp("wzT", [D, D], BF16)
    d_wh = ctx.inp("whT", [D, D], BF16)
    d_cols = ctx.inp("colsin", [128, 96])
    d_tril = ctx.inp("tril", [128, 128])
    d_xo = ctx.outp("xout", [D, TC])
    d_xm1 = ctx.dram("xmid1", [D, TC])
    d_xm2 = ctx.dram("xmid2", [D, TC])
    xn_sb = ctx.sb("xn_sb", [128, NCH * TC], BF16)
    cols_sb = ctx.sb("cols_sb", [128, 96], F32)
    tril_sb = ctx.sb("tril_sb", [128, 128], F32)
    onescf = ctx.sb("onescf", [128, 1], F32)
    sg_sb = ctx.sb("sg_sb", [128, 64], F32)
    su_sb = ctx.sb("su_sb", [128, 64], F32)
    sd_sb = ctx.sb("sd_sb", [128, NCH], F32)
    ld = ctx.sem("ld0")
    ldc = ctx.sem("ldc")
    sa = ctx.sem("sa0")
    sxm = ctx.sem("sxm")
    st = ctx.sem("st0")
    with ctx.nc.Block() as block:
        ctx.block = block

        def fones(v):
            v.memset(onescf[:], 1.0)
        ctx.on("vector", fones)
        ctx.cconst(LN_EPS)
        ctx.setup_common()

        ctx.push_scope()
        x_sb = ctx.sb("x_sbA", [128, NCH * TC], F32)

        def floadA(sy):
            for dst, src in [(x_sb[:].rearrange("p (c t) -> p c t", c=NCH),
                              d_x.rearrange("(c p) t -> p c t", p=128)),
                             (xn_sb[:, 0:8 * TC].rearrange("p (c t) -> p c t", c=8),
                              d_pa.rearrange("(c p) t -> p c t", p=128)),
                             (xn_sb[:, 8 * TC:16 * TC].rearrange("p (c t) -> p c t", c=8),
                              d_pb.rearrange("(c p) t -> p c t", p=128)),
                             (sg_sb[:], d_sg), (su_sb[:], d_su), (sd_sb[:], d_sd),
                             (cols_sb[:], d_cols), (tril_sb[:], d_tril)]:
                ins = sy.dma_start(out=dst, in_=src)
                ld.inc(ins, 16)
        ctx.on("sync", floadA)

        def faddA(v):
            ld.wait_all(v)
            ins = v.tensor_add(out=x_sb[:], in0=x_sb[:], in1=xn_sb[:])
            sa.inc(ins)
        ctx.on("vector", faddA)
        ctx.barrier()
        emit_ln(ctx, x_sb, xn_sb, TC)
        ctx.barrier()
        emit_l1ffn(ctx, xn_sb[:], x_sb[:], TC, dict(
            wg=d_wg, wu=d_wu, wd=d_wd, sg_sb=sg_sb, su_sb=su_sb, sd_sb=sd_sb))
        ctx.barrier()
        emit_ln(ctx, x_sb, xn_sb, TC)
        ctx.barrier()

        def fstoreA(g):
            ins = g.dma_start(out=d_xm1.ap().rearrange("(c p) t -> p c t", p=128),
                              in_=x_sb[:].rearrange("p (c t) -> p c t", c=NCH))
            sxm.inc(ins, 16)
            g.wait_ge(sxm.h, 16)
        ctx.on("gpsimd", fstoreA)
        ctx.barrier()
        ctx.pop_scope()

        ctx.push_scope()

        def x_load(gp, c, buf):
            return gp.dma_start(out=buf, in_=d_xm1.ap()[c * 128:(c + 1) * 128, :])

        def x_store(gp, c, buf):
            return gp.dma_start(out=d_xm2.ap()[c * 128:(c + 1) * 128, :], in_=buf)
        emit_l1attnb(ctx, xn_sb[:], TC, dict(
            wzT=d_wz, whT=d_wh, cols_sb=cols_sb[:], tril_f=tril_sb[:], onesc_f=onescf[:],
            pair_groups=[[2 * i, 2 * i + 1] for i in range(4)],
            all_groups=[list(range(NCORES))],
            x_load=x_load, x_store=x_store,
            nstat=4.0 * 2049.0 * 2048.0))
        ctx.barrier()
        ctx.pop_scope()

        ctx.push_scope()
        x_sb2 = ctx.sb("x_sbC", [128, NCH * TC], F32)

        def floadC(sy):
            ins = sy.dma_start(out=x_sb2[:].rearrange("p (c t) -> p c t", c=NCH),
                               in_=d_xm2.ap().rearrange("(c p) t -> p c t", p=128))
            ldc.inc(ins, 16)
        ctx.on("sync", floadC)

        def fwC(v):
            ldc.wait_all(v)
        ctx.on("vector", fwC)
        ctx.barrier()
        emit_ln(ctx, x_sb2, xn_sb, TC)
        ctx.barrier()
        emit_l1ffn(ctx, xn_sb[:], x_sb2[:], TC, dict(
            wg=d_wg, wu=d_wu, wd=d_wd, sg_sb=sg_sb, su_sb=su_sb, sd_sb=sd_sb))
        ctx.barrier()

        def fstoreC(g):
            ins = g.dma_start(out=d_xo.rearrange("(c p) t -> p c t", p=128),
                              in_=x_sb2[:].rearrange("p (c t) -> p c t", c=NCH))
            st.inc(ins, 16)
            st.wait_all(g)
        ctx.on("gpsimd", fstoreC)
        ctx.pop_scope()
    return ctx.nc


def build_l1b():
    ctx = Ctx()
    d_x = ctx.inp("x", [D, TC])
    d_xn = ctx.inp("xn", [D, TC], BF16)
    d_wz = ctx.inp("wzT", [D, D], BF16)
    d_wh = ctx.inp("whT", [D, D], BF16)
    d_cols = ctx.inp("colsin", [128, 96])
    d_tril = ctx.inp("tril", [128, 128])
    d_out = ctx.outp("xout", [D, TC])
    xn_sb = ctx.sb("xn_sb", [128, NCH * TC], BF16)
    cols_sb = ctx.sb("cols_sb", [128, 96], F32)
    tril_sb = ctx.sb("tril_sb", [128, 128], F32)
    onescf = ctx.sb("onescf", [128, 1], F32)
    ld = ctx.sem("ld0")
    with ctx.nc.Block() as block:
        ctx.block = block

        def fload(sy):
            for dst, src in [
                    (xn_sb[:].rearrange("p (c t) -> p c t", c=NCH),
                     d_xn.rearrange("(c p) t -> p c t", p=128)),
                    (cols_sb[:], d_cols), (tril_sb[:], d_tril)]:
                ins = sy.dma_start(out=dst, in_=src)
                ld.inc(ins, 16)
        ctx.on("sync", fload)

        def fones(v):
            v.memset(onescf[:], 1.0)
        ctx.on("vector", fones)
        ctx.setup_common()

        def fw(v):
            ld.wait_all(v)
        ctx.on("vector", fw)
        ctx.barrier()

        def x_load(gp, c, buf):
            return gp.dma_start(out=buf, in_=d_x[c * 128:(c + 1) * 128, :])

        def x_store(gp, c, buf):
            return gp.dma_start(out=d_out[c * 128:(c + 1) * 128, :], in_=buf)
        emit_l1attnb(ctx, xn_sb[:], TC, dict(
            wzT=d_wz, whT=d_wh, cols_sb=cols_sb[:], tril_f=tril_sb[:], onesc_f=onescf[:],
            pair_groups=[[2 * i, 2 * i + 1] for i in range(4)],
            all_groups=[list(range(NCORES))],
            x_load=x_load, x_store=x_store,
            nstat=4.0 * 2049.0 * 2048.0))
        ctx.barrier()
    return ctx.nc


def _A(a):
    return np.ascontiguousarray(a)


def _consts():
    kp = np.arange(128)[:, None]
    qf = np.arange(512)[None, :]
    masks = np.stack([((128 * r + kp) <= qf) for r in range(4)], 1).astype(np.float32)
    masks_lo = np.stack([((128 * r + kp) >= qf + 1) for r in range(4)], 1).astype(np.float32)
    inv_freq = 1.0 / (100000.0 ** (np.arange(0, 128, 2, dtype=np.float32) / 128.0))
    fr = np.outer(np.arange(T, dtype=np.float32), inv_freq)
    cos, sin = np.cos(fr), np.sin(fr)
    return dict(
        masks=masks.reshape(128, -1).astype(BF16NP),
        masks_lo=masks_lo.reshape(128, -1).astype(BF16NP),
        C2=np.concatenate([cos.T, cos.T], 0).astype(BF16NP),
        S2=np.concatenate([-sin.T, sin.T], 0).astype(BF16NP),
        identb=np.eye(128, dtype=np.float32).astype(BF16NP),
        oh8=np.kron(np.eye(8, dtype=np.float32), np.ones((1, 16), np.float32)).astype(BF16NP),
        tril=np.triu(np.ones((128, 128), np.float32)),
        zero_bo=np.zeros((128, NCH), np.float32),
        zero_p=np.zeros((1024, TC), BF16NP),
    )


def kernel(**inputs):
    inp = {k: np.asarray(v) for k, v in inputs.items()}
    X = inp["L0_attn_a_x"].astype(np.float32, copy=False)

    E, R_ = 8, 16
    Ag, Bg = inp["L0_ffn_Ag"], inp["L0_ffn_Bg"]
    Au, Bu = inp["L0_ffn_Au"], inp["L0_ffn_Bu"]
    Ad_, Bd = inp["L0_ffn_Ad"], inp["L0_ffn_Bd"]
    agu_h = np.zeros((D, 256), np.float32)
    for e in range(E):
        agu_h[:, e * 16:(e + 1) * 16] = Ag[e].T
        agu_h[:, 128 + e * 16:128 + (e + 1) * 16] = Au[e].T
    bgu_h = np.zeros((16, 2 * 44 * 2048), np.float32)
    for b2 in range(2):
        for j in range(44):
            blk = np.zeros((16, 2048), np.float32)
            for e4 in range(4):
                e = b2 * 4 + e4
                blk[:, e4 * 256:e4 * 256 + 128] = Bg[e][j * 128:(j + 1) * 128, :].T
                blk[:, 1024 + e4 * 256:1024 + e4 * 256 + 128] = Bu[e][j * 128:(j + 1) * 128, :].T
            bgu_h[:, (b2 * 44 + j) * 2048:(b2 * 44 + j + 1) * 2048] = blk
    adt_h = np.zeros((128, 44 * 128), np.float32)
    for j in range(44):
        for e in range(E):
            adt_h[:, j * 128 + e * 16: j * 128 + (e + 1) * 16] = Ad_[e][:, j * 128:(j + 1) * 128].T
    bdt_h = np.zeros((16, E * 2048), np.float32)
    for e in range(E):
        bdt_h[:, e * 2048:(e + 1) * 2048] = Bd[e].T

    arranged = {
        "l0a_wqT": _A(inp["L0_attn_a_wq"].T), "l0a_wkT": _A(inp["L0_attn_a_wk"].T),
        "l0a_wvT": _A(inp["L0_attn_a_wv"].T), "l0a_woT": _A(inp["L0_attn_a_wo"].T),
        "l0b_pack": _A(inp["L0_attn_b_W_pack"]), "l0b_woT": _A(inp["L0_attn_b_W_o"]),
        "f0_wg": _A(inp["L0_ffn_Wg"].T), "f0_wu": _A(inp["L0_ffn_Wu"].T),
        "f0_wd": _A(inp["L0_ffn_Wd"].T), "f0_wr": _A(inp["L0_ffn_Wr"].T),
        "f0_agu": agu_h, "f0_bgu": bgu_h, "f0_adt": adt_h, "f0_bdt": bdt_h,
        "l1a_wqT": _A(np.concatenate([inp["L1_attn_a_Wq"][h] for h in range(16)], 1)),
        "l1a_wkT": _A(np.concatenate([inp["L1_attn_a_Wk"][g] for g in range(4)], 1)),
        "l1a_wvT": _A(np.concatenate([inp["L1_attn_a_Wv"][g] for g in range(4)], 1)),
        "f1_wg": _A(inp["L1_ffn_w_gate"]), "f1_wu": _A(inp["L1_ffn_w_up"]),
        "f1_wd": _A(inp["L1_ffn_w_down"]),
        "b_wzT": _A(inp["L1_attn_b_W_z"].T), "b_whT": _A(inp["L1_attn_b_W_h"].T),
    }
    C = _consts()

    for key, builder in [("prep", build_prep),
                         ("a0a", lambda: build_attn("l0a")),
                         ("a0b", lambda: build_attn("l0b")),
                         ("a1a", lambda: build_attn("l1a")),
                         ("lmp", build_lmp),
                         ("fll", build_fll)]:
        if key not in _PROGS:
            _PROGS[key] = builder()
    cores = list(range(NCORES))
    zero_y = np.zeros((D, TC), BF16NP)

    x0c = []
    im = []
    for c in cores:
        q, s = c // 2, c % 2
        x0c.append(_A(X[q, s * TC:(s + 1) * TC, :].T))
        d = {"x0": x0c[c]}
        for nm, R, Cc, dt in _CASTS:
            n = R * Cc // NCORES
            d[nm + "_in"] = _A(arranged[nm].reshape(-1)[c * n:(c + 1) * n].reshape(128, n // 128))
        im.append(d)
    r0 = run_bass_kernel_spmd(_PROGS["prep"], im, core_ids=cores)
    xn_cur = [r0.results[c]["xn0"] for c in cores]
    W = {}
    for nm, R, Cc, dt in _CASTS:
        W[nm] = np.concatenate([np.asarray(r0.results[c][nm + "_bf"]).reshape(-1)
                                for c in cores]).reshape(R, Cc)
    x_cur = x0c

    def lmp_launch(x_cur, yp, bo):
        im = []
        for c in cores:
            q, s = c // 2, c % 2
            ssl = slice(s * TC, (s + 1) * TC)
            d = {"x": x_cur[c], "yA": _A(yp[2 * q][:, ssl]), "yB": _A(yp[2 * q + 1][:, ssl]),
                 "bo_cols": bo, "identb": C["identb"], "oh8": C["oh8"],
                 "wg": W["f0_wg"], "wu": W["f0_wu"], "wd": W["f0_wd"], "wr": W["f0_wr"],
                 "agu": W["f0_agu"], "bgu": W["f0_bgu"], "adt": W["f0_adt"], "bdt": W["f0_bdt"]}
            im.append(d)
        r = run_bass_kernel_spmd(_PROGS["lmp"], im, core_ids=cores)
        return ([np.asarray(r.results[c]["xout"]) for c in cores],
                [np.asarray(r.results[c]["xnout"]) for c in cores])

    bq, bk, bv = inp["L0_attn_a_bq"], inp["L0_attn_a_bk"], inp["L0_attn_a_bv"]
    im = []
    for c in cores:
        q, r = c // 2, c % 2
        rsl = slice(r * 1024, (r + 1) * 1024)
        im.append({
            "xn_h0": xn_cur[2 * q], "xn_h1": xn_cur[2 * q + 1],
            "wqT": _A(W["l0a_wqT"][:, rsl]), "wkT": _A(W["l0a_wkT"][:, rsl]),
            "wvT": _A(W["l0a_wvT"][:, rsl]), "woT": _A(W["l0a_woT"][rsl, :]),
            "bqc": _A(bq[rsl].reshape(8, 128).T), "bkc": _A(bk[rsl].reshape(8, 128).T),
            "bvbc": _A(np.broadcast_to(bv[rsl], (128, 1024)).astype(np.float32)),
            "masks": C["masks"]})
    r1 = run_bass_kernel_spmd(_PROGS["a0a"], im, core_ids=cores)
    yp = [np.asarray(r1.results[c]["y_part"]) for c in cores]

    bo_cols = _A(inp["L0_attn_a_bo"].reshape(16, 128).T)
    x_cur, xn_cur = lmp_launch(x_cur, yp, bo_cols)

    conv_k, conv_v = inp["L0_attn_b_conv_k"], inp["L0_attn_b_conv_v"]
    im = []
    for c in cores:
        q, r = c // 2, c % 2
        qsl = slice(r * 1024, (r + 1) * 1024)
        ksl = slice(2048 + r * 512, 2048 + (r + 1) * 512)
        vsl = slice(3072 + r * 512, 3072 + (r + 1) * 512)
        ckc = np.zeros((128, 8), np.float32)
        for g in range(4):
            ckc[:, g] = conv_k[r * 4 + g, 0]
            ckc[:, 4 + g] = conv_k[r * 4 + g, 1]
        w0v = np.repeat(conv_v[r * 4:(r + 1) * 4, 0], 128)
        w1v = np.repeat(conv_v[r * 4:(r + 1) * 4, 1], 128)
        im.append({
            "xn_h0": xn_cur[2 * q], "xn_h1": xn_cur[2 * q + 1],
            "wqT": _A(W["l0b_pack"][:, qsl]), "wkT": _A(W["l0b_pack"][:, ksl]),
            "wvT": _A(W["l0b_pack"][:, vsl]), "woT": _A(W["l0b_woT"][qsl, :]),
            "masks": C["masks"], "masks_lo": C["masks_lo"], "C2": C["C2"], "S2": C["S2"],
            "convk": ckc,
            "w0v": _A(np.broadcast_to(w0v, (128, 512)).astype(np.float32)),
            "w1v": _A(np.broadcast_to(w1v, (128, 512)).astype(np.float32))})
    r3 = run_bass_kernel_spmd(_PROGS["a0b"], im, core_ids=cores)
    yp = [np.asarray(r3.results[c]["y_part"]) for c in cores]

    x_cur, xn_cur = lmp_launch(x_cur, yp, C["zero_bo"])

    bq1, bk1, bv1 = inp["L1_attn_a_bq"], inp["L1_attn_a_bk"], inp["L1_attn_a_bv"]
    im = []
    for c in cores:
        q, r = c // 2, c % 2
        im.append({
            "xn_h0": xn_cur[2 * q], "xn_h1": xn_cur[2 * q + 1],
            "wqT": _A(W["l1a_wqT"][:, r * 1024:(r + 1) * 1024]),
            "wkT": _A(W["l1a_wkT"][:, r * 256:(r + 1) * 256]),
            "wvT": _A(W["l1a_wvT"][:, r * 256:(r + 1) * 256]),
            "bqc": _A(bq1[r * 8:(r + 1) * 8].T), "bkc": _A(bk1[r * 2:(r + 1) * 2].T),
            "bvbc": _A(np.broadcast_to(bv1[r * 2:(r + 1) * 2].reshape(-1), (128, 256)).astype(np.float32)),
            "masks": C["masks"]})
    r5 = run_bass_kernel_spmd(_PROGS["a1a"], im, core_ids=cores)
    op = [np.asarray(r5.results[c]["o_part"]) for c in cores]

    cols = np.zeros((128, 96), np.float32)
    cols[:, 0:16] = inp["L1_attn_b_b_z"].reshape(16, 128).T
    cols[:, 16:32] = inp["L1_attn_b_b_h"].reshape(16, 128).T
    cols[:, 32:48] = inp["L1_attn_b_ln_w"].reshape(16, 128).T
    cols[:, 48:64] = inp["L1_attn_b_ln_b"].reshape(16, 128).T
    cols[:, 64:80] = inp["L1_attn_b_h0"].reshape(16, 128).T
    f1_shared = {"wg": W["f1_wg"], "wu": W["f1_wu"], "wd": W["f1_wd"],
                 "sg": _A(inp["L1_ffn_s_gate"].reshape(64, 128).T),
                 "su": _A(inp["L1_ffn_s_up"].reshape(64, 128).T),
                 "sd": _A(inp["L1_ffn_s_down"].reshape(16, 128).T)}
    im = []
    for c in cores:
        q, s = c // 2, c % 2
        ssl = slice(s * TC, (s + 1) * TC)
        colsin = cols.copy()
        colsin[:, 80] = 1.0 - s
        colsin[:, 81] = s
        d = {"x": x_cur[c], "pLO": _A(op[2 * q][:, ssl]), "pHI": _A(op[2 * q + 1][:, ssl]),
             "wzT": W["b_wzT"], "whT": W["b_whT"], "colsin": colsin, "tril": C["tril"]}
        d.update(f1_shared)
        im.append(d)
    r8 = run_bass_kernel_spmd(_PROGS["fll"], im, core_ids=cores)

    OUT = np.empty((B, T, D), np.float32)
    for c in cores:
        q, s = c // 2, c % 2
        OUT[q, s * TC:(s + 1) * TC, :] = np.asarray(r8.results[c]["xout"]).T
    return OUT
